# version 33
# baseline (speedup 1.0000x reference)
import math
import numpy as np
import ml_dtypes
import concourse.bass as bass
import concourse.mybir as mybir
from concourse.bass_utils import run_bass_kernel_spmd

F32 = mybir.dt.float32
BF16 = mybir.dt.bfloat16
I32 = mybir.dt.int32
U8 = mybir.dt.uint8
F16 = mybir.dt.float16
AF = mybir.ActivationFunctionType
ALU = mybir.AluOpType

NCORES = 8
T = 4096
D = 2048
TB = 512
NTB = T // TB
DC = D // 128

SEM_NAMES = (
    "wa", "wb",
    "xa", "xb",
    "ga", "gb",
    "st",
    "pe",
    "act",
    "dve",
    "pool",
    "dqa", "dqs",
)


class Ctx:

    def __init__(self, nc, which, eng, sems):
        self.nc = nc
        self.which = which
        self.eng = eng
        self.sems = sems
        self.cnt = {k: 0 for k in sems}
        self.last_wait = {k: 0 for k in sems}
        self.acked = {k: 0 for k in sems}

    def op(self, engine, meth, *a, incs=(), **kw):
        if self.which == engine:
            inst = getattr(self.eng, meth)(*a, **kw)
            for s, n in incs:
                inst.then_inc(self.sems[s], n)
        for s, n in incs:
            self.cnt[s] += n

    def wait(self, engine, *sem_names):
        for s in sem_names:
            if self.cnt[s] > 0:
                if self.which == engine:
                    self.eng.wait_ge(self.sems[s], self.cnt[s])
                self.last_wait[s] = max(self.last_wait[s], self.cnt[s])

    def val(self, s):
        return self.cnt[s]

    def wait_val(self, engine, s, v):
        if v > 0:
            if self.which == engine:
                self.eng.wait_ge(self.sems[s], v)
            self.last_wait[s] = max(self.last_wait[s], v)

    def dma(self, engine, sem, out, in_):
        if self.last_wait[sem] > self.acked[sem]:
            v = self.last_wait[sem]
            if self.which == engine:
                self.eng.wait_ge(self.sems[sem], v)
            self.acked[sem] = v
        self.op(engine, "dma_start", out, in_, incs=((sem, 16),))


class Hist:

    def __init__(self, depth):
        self.depth = depth
        self.vals = []

    def push(self, v):
        self.vals.append(v)

    def guard(self):
        return self.vals[-self.depth] if len(self.vals) >= self.depth else 0


class SbufAlloc:

    def __init__(self, nc):
        self.nc = nc
        self.cur = (nc.sbuf_base + 31) & ~31
        self.limit = nc.sbuf_top - 1024
        self.uid = 0
        self.hwm = 0

    def tile(self, name, shape, dtype):
        self.uid += 1
        h = self.nc.alloc_sbuf_tensor_at(f"{name}_{self.uid}", list(shape), dtype,
                                         offset=self.cur)
        per = int(np.prod(shape[1:])) * mybir.dt.size(dtype)
        self.cur = (self.cur + per + 31) & ~31
        self.hwm = max(self.hwm, self.cur)
        assert self.cur <= self.limit, f"SBUF overflow at {name}: {self.cur}"
        return h

    def mark(self):
        return self.cur

    def reset(self, mark):
        self.cur = mark



def emit_postln_prep(ctx, b, dc):
    sl = slice(TB * dc, TB * (dc + 1))
    ctx.wait("vector", "act")
    if dc == 0:
        ctx.wait("vector", "xb")
        ctx.wait("vector", "st")
    ctx.op("vector", "tensor_add", b["XN"][:, sl], b["XA"][:, sl], b["Y"][:, sl])
    ctx.op("vector", "tensor_mul", b["Y"][:, sl], b["XN"][:, sl], b["XN"][:, sl],
           incs=(("dve", 1),))


def emit_postln(ctx, b, pb, ones_b, ones_c, ones_f, x_out_view, banks=(6, 7)):
    inv_d = 1.0 / D
    s1, s2 = pb[banks[0]], pb[banks[1]]
    ctx.wait("tensor", "act")
    ctx.wait("tensor", "dve")
    for dc in range(DC):
        ctx.op("tensor", "matmul", s1[0:1, 0:TB], ones_c[:, 0:1],
               b["XN"][:, TB * dc:TB * (dc + 1)], start=(dc == 0), stop=(dc == DC - 1))
    for dc in range(DC):
        ctx.op("tensor", "matmul", s2[0:1, 0:TB], ones_b[:, 0:1],
               b["Y"][:, TB * dc:TB * (dc + 1)], start=(dc == 0), stop=(dc == DC - 1),
               incs=((("pe", 1),) if dc == DC - 1 else ()))
    ST = b["ST"]
    mrow = ST[0:1, 0:TB]
    qrow = ST[0:1, TB:2 * TB]
    msq = ST[0:1, 2 * TB:3 * TB]
    var = ST[0:1, 3 * TB:4 * TB]
    rstd = ST[0:1, 4 * TB:5 * TB]
    ctx.wait("scalar", "pe")
    ctx.op("scalar", "activation", mrow, s1[0:1, 0:TB], AF.Copy, scale=inv_d,
           incs=(("act", 1),))
    ctx.op("scalar", "activation", qrow, s2[0:1, 0:TB], AF.Copy, scale=inv_d,
           incs=(("act", 1),))
    ctx.wait("vector", "act")
    ctx.op("vector", "tensor_mul", msq, mrow, mrow, incs=(("dve", 1),))
    ctx.wait("vector", "dve")
    ctx.op("vector", "tensor_sub", var, qrow, msq, incs=(("dve", 1),))
    ctx.wait("vector", "dve")
    ctx.op("vector", "tensor_scalar", var, var, 1e-5, None, op0=ALU.add,
           incs=(("dve", 1),))
    ctx.wait("scalar", "dve")
    ctx.op("scalar", "activation", rstd, var, AF.Sqrt, incs=(("act", 1),))
    ctx.wait("vector", "act")
    ctx.op("vector", "reciprocal", rstd, rstd, incs=(("dve", 1),))
    bm, br = pb[banks[0]], pb[banks[1]]
    ctx.wait("tensor", "dve")
    ctx.op("tensor", "matmul", bm[:, 0:TB], ones_f[0:1, :], mrow,
           start=True, stop=True)
    ctx.op("tensor", "matmul", br[:, 0:TB], ones_f[0:1, :], rstd,
           start=True, stop=True, incs=(("pe", 1),))
    ctx.wait("vector", "pe")
    for dc in range(DC):
        ctx.op("vector", "scalar_tensor_tensor",
               b["XN"][:, TB * dc:TB * (dc + 1)],
               b["XN"][:, TB * dc:TB * (dc + 1)], 1.0, bm[:, 0:TB],
               op0=ALU.mult, op1=ALU.subtract,
               incs=((("dve", 1),) if dc == DC - 1 else ()))
    ctx.wait("vector", "dve")
    for dc in range(DC):
        ctx.op("vector", "tensor_mul",
               b["XN"][:, TB * dc:TB * (dc + 1)],
               b["XN"][:, TB * dc:TB * (dc + 1)], br[:, 0:TB],
               incs=((("dve", 1),) if dc == DC - 1 else ()))
    out_dve = ctx.val("dve")
    ctx.wait_val("scalar", "dve", out_dve)
    ctx.dma("scalar", "st", x_out_view,
           b["XN"][:, :].rearrange("p (c t) -> p c t", c=DC))
    return out_dve



def plan_ffn0(sb, io):
    b = dict(io)
    b["GSC"] = sb.tile("p2_gsc", [128, 44], F32)
    b["USC"] = sb.tile("p2_usc", [128, 44], F32)
    b["DSC"] = sb.tile("p2_dsc", [128, 16], F32)
    m = sb.mark()
    b["XB"] = sb.tile("p2_xb", [128, DC * T], BF16)
    b["GW"] = [sb.tile(f"p2_gw{i}", [128, DC * 128], BF16) for i in range(2)]
    b["UW"] = [sb.tile(f"p2_uw{i}", [128, DC * 128], BF16) for i in range(2)]
    b["HG"] = [sb.tile(f"p2_hg{i}", [128, TB], BF16) for i in range(2)]
    b["HU"] = [sb.tile(f"p2_hu{i}", [128, TB], BF16) for i in range(2)]
    b["HH"] = [sb.tile(f"p2_hh{i}", [128, TB], BF16) for i in range(2)]
    sb.reset(m)
    b["HT"] = [sb.tile(f"p2_ht{i}", [128, 44 * TB], BF16) for i in range(2)]
    b["DW"] = [sb.tile(f"p2_dw{i}", [128, 44 * 128], BF16) for i in range(2)]
    b["XA"] = sb.tile("p2_xa", [128, DC * TB], F32)
    b["XN"] = sb.tile("p2_xn", [128, DC * TB], F32)
    b["Y"] = sb.tile("p2_y", [128, DC * TB], BF16)
    b["ST"] = sb.tile("p2_st", [1, 5 * TB], F32)
    sb.reset(m)
    return b


def emit_ffn0(ctx, b, pb, ones_b, ones_c, ones_f,
              x_in="x1T", x_out="x2T", ln_banks=(6, 7)):
    x1T, x2T = b[x_in], b[x_out]
    gwT, uwT, dwT, hscr = b["gwT"], b["uwT"], b["dwT"], b["h_scr"]
    NI = 44

    ctx.dma("sync", "xa", b["GSC"][:, :], b["gs_t"])
    ctx.dma("sync", "xa", b["USC"][:, :], b["us_t"])
    ctx.dma("sync", "xa", b["DSC"][:, :], b["ds_t"])
    XB = b["XB"]
    xv = x1T.rearrange("(c p) t -> p c t", p=128)
    for i in range(4):
        ctx.dma("gpsimd", "ga",
               XB[:, :].rearrange("p (c t) -> p c t", c=DC)[:, 4 * i:4 * i + 4, :],
               xv[:, 4 * i:4 * i + 4, :])

    gv = gwT.rearrange("(c p) o -> p c o", p=128)
    uv = uwT.rearrange("(c p) o -> p c o", p=128)
    pe_at_chunk = {}
    ebank_h = Hist(2)
    ehh_h = Hist(2)
    est_h = Hist(2)
    for ic in range(NI):
        slot = ic % 2
        wsem = "wa" if slot == 0 else "wb"
        if ic >= 2:
            ctx.wait_val("gpsimd", "pe", pe_at_chunk[ic - 2])
        ctx.dma("gpsimd", wsem,
               b["GW"][slot][:, :].rearrange("p (c o) -> p c o", c=DC),
               gv[:, :, 128 * ic:128 * (ic + 1)])
        ctx.dma("gpsimd", wsem,
               b["UW"][slot][:, :].rearrange("p (c o) -> p c o", c=DC),
               uv[:, :, 128 * ic:128 * (ic + 1)])
        if ic < 16:
            dwtv = dwT.rearrange("(c p) (dc j) -> p dc c j", p=128, j=128)
            ctx.dma("gpsimd", "gb",
                    b["dw_bf"].rearrange("p (dc c j) -> p dc c j", dc=16, c=44)[:, ic, :, :],
                    dwtv[:, ic, :, :])
        ctx.wait("tensor", wsem)
        if ic == 0:
            ctx.wait("tensor", "ga")
        for tb in range(NTB):
            bg = pb[2 * (tb % 2)]
            bu = pb[2 * (tb % 2) + 1]
            ctx.wait_val("tensor", "act", ebank_h.guard())
            for kc in range(DC):
                ctx.op("tensor", "matmul", bg[:, :],
                       b["GW"][slot][:, 128 * kc:128 * (kc + 1)],
                       XB[:, T * kc + TB * tb: T * kc + TB * (tb + 1)],
                       start=(kc == 0), stop=(kc == DC - 1))
            for kc in range(DC):
                ctx.op("tensor", "matmul", bu[:, :],
                       b["UW"][slot][:, 128 * kc:128 * (kc + 1)],
                       XB[:, T * kc + TB * tb: T * kc + TB * (tb + 1)],
                       start=(kc == 0), stop=(kc == DC - 1),
                       incs=((("pe", 1),) if kc == DC - 1 else ()))
            hg = b["HG"][tb % 2]
            hu = b["HU"][tb % 2]
            ctx.wait("scalar", "pe")
            ctx.wait_val("scalar", "dve", ehh_h.guard())
            if ic == 0 and tb == 0:
                ctx.wait("scalar", "xa")
            ctx.op("scalar", "activation", hg[:, :], bg[:, :], AF.Silu,
                   scale=b["GSC"][:, ic:ic + 1], incs=(("act", 1),))
            ctx.op("scalar", "activation", hu[:, :], bu[:, :], AF.Copy,
                   scale=b["USC"][:, ic:ic + 1], incs=(("act", 1),))
            ebank_h.push(ctx.val("act"))
            hh = b["HH"][tb % 2]
            ctx.wait("vector", "act")
            ctx.wait_val("vector", "st", est_h.guard())
            ctx.op("vector", "tensor_mul", hh[:, :], hg[:, :], hu[:, :],
                   incs=(("dve", 1),))
            ehh_h.push(ctx.val("dve"))
            ctx.wait("sync", "dve")
            ctx.dma("sync", "st",
                   hscr.rearrange("(c p) t -> p c t", p=128)[:, ic, TB * tb:TB * (tb + 1)],
                   hh[:, :])
            est_h.push(ctx.val("st"))
        pe_at_chunk[ic] = ctx.val("pe")

    dv = b["dw_bf"].rearrange("p (dc c j) -> p dc c j", dc=16, c=44)
    hv = hscr.rearrange("(c p) t -> p c t", p=128)
    x1v = x1T.rearrange("(c p) t -> p c t", p=128)
    x2v = x2T.rearrange("(c p) t -> p c t", p=128)
    inv_d = 1.0 / D
    ctx.wait("sync", "st")
    ctx.wait("gpsimd", "gb")
    ctx.wait("scalar", "gb")
    pe_at_tb = {}
    pe_at_dw = {}
    dve_at_tb = {}
    xa_free = {}
    dbank_h = Hist(4)
    for tb in range(NTB):
        hts = b["HT"][tb % 2]
        if tb >= 2:
            ctx.wait_val("sync", "pe", pe_at_tb[tb - 2])
        if tb >= 1:
            ctx.wait_val("sync", "dve", xa_free[tb - 1])
        ctx.dma("sync", "xa",
               hts[:, :].rearrange("p (c t) -> p c t", c=44),
               hv[:, :, TB * tb:TB * (tb + 1)])
        ctx.dma("sync", "xb",
               b["XA"][:, :].rearrange("p (c t) -> p c t", c=DC),
               x1v[:, :, TB * tb:TB * (tb + 1)])
        for dc in range(DC):
            slot = dc % 2
            wsem = "wa" if slot == 0 else "wb"
            key = (tb, dc)
            prev = (tb, dc - 2) if dc >= 2 else ((tb - 1, DC - 2 + dc) if tb >= 1 else None)
            if prev is not None:
                ctx.wait_val("gpsimd", "pe", pe_at_dw[prev])
            else:
                ctx.wait_val("gpsimd", "pe", pe_at_chunk[NI - 1])
            ctx.dma("gpsimd", wsem,
                    b["DW"][slot][:, :].rearrange("p (c o) -> p c o", c=44),
                    dv[:, dc, :, :])
            bankd = pb[(4, 5, 0, 1)[dc % 4]]
            ctx.wait("tensor", wsem)
            if dc == 0:
                ctx.wait("tensor", "xa")
            ctx.wait_val("tensor", "act", dbank_h.guard())
            for kc in range(44):
                ctx.op("tensor", "matmul", bankd[:, :],
                       b["DW"][slot][:, 128 * kc:128 * (kc + 1)],
                       hts[:, TB * kc:TB * (kc + 1)],
                       start=(kc == 0), stop=(kc == 43),
                       incs=((("pe", 1),) if kc == 43 else ()))
            pe_at_dw[key] = ctx.val("pe")
            ctx.wait("scalar", "pe")
            ctx.wait("scalar", "dve")
            ctx.op("scalar", "activation", b["Y"][:, TB * dc:TB * (dc + 1)], bankd[:, :],
                   AF.Copy, scale=b["DSC"][:, dc:dc + 1], incs=(("act", 1),))
            dbank_h.push(ctx.val("act"))
            emit_postln_prep(ctx, b, dc)
        pe_at_tb[tb] = ctx.val("pe")
        xa_free[tb] = ctx.val("dve")
        dve_at_tb[tb] = emit_postln(ctx, b, pb, ones_b, ones_c, ones_f,
                                    x2v[:, :, TB * tb:TB * (tb + 1)],
                                    banks=ln_banks)



def _run_block(nc, emit, sems):
    with nc.Block() as block:
        @block.sync
        def _(eng):
            emit(Ctx(nc, "sync", eng, sems))

        @block.tensor
        def _(eng):
            emit(Ctx(nc, "tensor", eng, sems))

        @block.scalar
        def _(eng):
            emit(Ctx(nc, "scalar", eng, sems))

        @block.vector
        def _(eng):
            emit(Ctx(nc, "vector", eng, sems))

        @block.gpsimd
        def _(eng):
            emit(Ctx(nc, "gpsimd", eng, sems))


def build_ffn0_program():
    nc = bass.Bass("TRN2", target_bir_lowering=False)
    io = {}
    io["x1T"] = nc.dram_tensor("x1T", [D, T], F32, kind="ExternalInput").ap()
    io["gwT"] = nc.dram_tensor("gwT", [D, 5632], F32, kind="ExternalInput").ap()
    io["uwT"] = nc.dram_tensor("uwT", [D, 5632], F32, kind="ExternalInput").ap()
    io["dwT"] = nc.dram_tensor("dwT", [5632, D], F32, kind="ExternalInput").ap()
    io["gs_t"] = nc.dram_tensor("gs_t", [128, 44], F32, kind="ExternalInput").ap()
    io["us_t"] = nc.dram_tensor("us_t", [128, 44], F32, kind="ExternalInput").ap()
    io["ds_t"] = nc.dram_tensor("ds_t", [128, 16], F32, kind="ExternalInput").ap()
    io["x2T"] = nc.dram_tensor("x2T", [D, T], F32, kind="ExternalOutput").ap()
    io["h_scr"] = nc.dram_tensor("h_scr", [5632, T], BF16, kind="Internal").ap()

    sb = SbufAlloc(nc)
    ones_b = sb.tile("ones_b", [128, 1], BF16)
    ones_c = sb.tile("ones_c", [128, 1], F32)
    ones_f = sb.tile("ones_f", [1, 128], F32)
    bufs = plan_ffn0(sb, io)
    pb = [nc.alloc_psum_tensor(f"pb{i}", [128, 512], F32) for i in range(8)]
    sems = {k: nc.alloc_semaphore(name=f"s_{k}") for k in SEM_NAMES}

    def emit(ctx):
        ctx.op("vector", "memset", ones_b[:, :], 1.0)
        ctx.op("vector", "memset", ones_c[:, :], 1.0)
        ctx.op("vector", "memset", ones_f[:, :], 1.0, incs=(("dve", 1),))
        emit_ffn0(ctx, bufs, pb, ones_b, ones_c, ones_f)
        ctx.wait("sync", "st")

    _run_block(nc, emit, sems)
    return nc


def host_prep_ffn0(inputs, x1_full):
    gwT = np.ascontiguousarray(inputs["L0_ffn_gate_w"].T)
    uwT = np.ascontiguousarray(inputs["L0_ffn_up_w"].T)
    dwT = np.ascontiguousarray(inputs["L0_ffn_down_w"].T)
    gs_t = np.ascontiguousarray(inputs["L0_ffn_gate_s"].reshape(44, 128).T)
    us_t = np.ascontiguousarray(inputs["L0_ffn_up_s"].reshape(44, 128).T)
    ds_t = np.ascontiguousarray(inputs["L0_ffn_down_s"].reshape(16, 128).T)
    maps = []
    for c in range(NCORES):
        maps.append({
            "x1T": np.ascontiguousarray(x1_full[c].T),
            "gwT": gwT, "uwT": uwT, "dwT": dwT,
            "gs_t": gs_t, "us_t": us_t, "ds_t": ds_t,
        })
    return maps


ENGINES = ("sync", "tensor", "scalar", "vector", "gpsimd")


def phase_barrier(ctx):
    for e in ENGINES:
        ctx.wait(e, *SEM_NAMES)


def build_fused_program():
    nc = bass.Bass("TRN2", target_bir_lowering=False)
    io = {}
    io["x0T"] = nc.dram_tensor("x0T", [D, T], F32, kind="ExternalInput").ap()
    io["yT"] = nc.dram_tensor("yT", [D, T], F32, kind="ExternalOutput").ap()
    io["l0_wqT"] = nc.dram_tensor("l0_wqT", [D, 1024], F32, kind="ExternalInput").ap()
    io["l0_wkT"] = nc.dram_tensor("l0_wkT", [D, 1024], F32, kind="ExternalInput").ap()
    io["l0_wvT"] = nc.dram_tensor("l0_wvT", [D, D], F32, kind="ExternalInput").ap()
    io["l0_wgT"] = nc.dram_tensor("l0_wgT", [D, D], F32, kind="ExternalInput").ap()
    io["l0_woT"] = nc.dram_tensor("l0_woT", [D, D], F32, kind="ExternalInput").ap()
    io["gn_t"] = nc.dram_tensor("gn_t", [128, 4], F32, kind="ExternalInput").ap()
    io["gwT"] = nc.dram_tensor("gwT", [D, 5632], F32, kind="ExternalInput").ap()
    io["uwT"] = nc.dram_tensor("uwT", [D, 5632], F32, kind="ExternalInput").ap()
    io["dwT"] = nc.dram_tensor("dwT", [5632, D], F32, kind="ExternalInput").ap()
    io["gs_t"] = nc.dram_tensor("gs_t", [128, 44], F32, kind="ExternalInput").ap()
    io["us_t"] = nc.dram_tensor("us_t", [128, 44], F32, kind="ExternalInput").ap()
    io["ds_t"] = nc.dram_tensor("ds_t", [128, 16], F32, kind="ExternalInput").ap()
    io["wqT"] = nc.dram_tensor("wqT", [D, D], F32, kind="ExternalInput").ap()
    io["wkT"] = nc.dram_tensor("wkT", [D, D], F32, kind="ExternalInput").ap()
    io["wvT"] = nc.dram_tensor("wvT", [D, D], F32, kind="ExternalInput").ap()
    io["woT"] = nc.dram_tensor("woT", [D, D], F32, kind="ExternalInput").ap()
    io["gupT"] = nc.dram_tensor("gupT", [1024, 11264], U8, kind="ExternalInput").ap()
    io["gup_sr"] = nc.dram_tensor("gup_sr", [1024, 11264], F16, kind="ExternalInput").ap()
    io["dnT"] = nc.dram_tensor("dnT", [2816, 2048], U8, kind="ExternalInput").ap()
    io["dn_sr"] = nc.dram_tensor("dn_sr", [2816, 2048], F16, kind="ExternalInput").ap()
    io["x1T"] = nc.dram_tensor("x1T", [D, T], F32, kind="Internal").ap()
    io["x2T"] = nc.dram_tensor("x2T", [D, T], F32, kind="Internal").ap()
    io["x3T"] = nc.dram_tensor("x3T", [D, T], F32, kind="Internal").ap()
    io["a0_q"] = nc.dram_tensor("a0_q", [1024, T], BF16, kind="Internal").ap()
    io["a0_k"] = nc.dram_tensor("a0_k", [1024, T], BF16, kind="Internal").ap()
    io["a0_v"] = nc.dram_tensor("a0_v", [T, D], BF16, kind="Internal").ap()
    io["a0_g"] = nc.dram_tensor("a0_g", [D, T], BF16, kind="Internal").ap()
    io["a0_og"] = nc.dram_tensor("a0_og", [D, T], BF16, kind="Internal").ap()
    io["h0_scr"] = nc.dram_tensor("h0_scr", [5632, T], BF16, kind="Internal").ap()
    io["a1_q"] = nc.dram_tensor("a1_q", [D, T], BF16, kind="Internal").ap()
    io["a1_k"] = nc.dram_tensor("a1_k", [D, T], BF16, kind="Internal").ap()
    io["a1_v"] = nc.dram_tensor("a1_v", [T, D], BF16, kind="Internal").ap()
    io["a1_z"] = nc.dram_tensor("a1_z", [D, T], BF16, kind="Internal").ap()
    io["wgu_scr"] = nc.dram_tensor("wgu_scr", [D, 11264], BF16, kind="Internal").ap()
    io["wdn_scr"] = nc.dram_tensor("wdn_scr", [NI1, D], BF16, kind="Internal").ap()
    io["h1_scr"] = nc.dram_tensor("h1_scr", [NI1, T], BF16, kind="Internal").ap()
    io["dw_bf"] = nc.dram_tensor("dw_bf", [128, 16 * 44 * 128], BF16, kind="Internal").ap()
    io["wdn_t"] = nc.dram_tensor("wdn_t", [128, 16 * 44 * 128], BF16, kind="Internal").ap()

    sb = SbufAlloc(nc)
    ones_b = sb.tile("ones_b", [128, 1], BF16)
    ones_c = sb.tile("ones_c", [128, 1], F32)
    ones_f = sb.tile("ones_f", [1, 128], F32)
    m0 = sb.mark()
    io1 = {k: io[k] for k in ("x0T", "x1T", "l0_wqT", "l0_wkT", "l0_wvT",
                              "l0_wgT", "l0_woT", "gn_t")}
    io1.update(q_scr=io["a0_q"], k_scr=io["a0_k"], v_scr=io["a0_v"],
               g_scr=io["a0_g"], og_scr=io["a0_og"])
    b1 = plan_attn0(sb, io1)
    sb.reset(m0)
    io2 = {k: io[k] for k in ("x1T", "x2T", "gwT", "uwT", "dwT",
                              "gs_t", "us_t", "ds_t", "dw_bf")}
    io2.update(h_scr=io["h0_scr"])
    b2 = plan_ffn0(sb, io2)
    sb.reset(m0)
    io3 = {k: io[k] for k in ("x2T", "x3T", "wqT", "wkT", "wvT", "woT")}
    io3.update(q_scr=io["a1_q"], k_scr=io["a1_k"], v_scr=io["a1_v"],
               z_scr=io["a1_z"])
    hwm_save = sb.hwm
    sb.hwm = sb.cur
    b3 = plan_attn1(sb, io3)
    a1_hwm = sb.hwm
    sb.hwm = max(hwm_save, a1_hwm)
    sb.reset(a1_hwm)
    OS2 = 512
    bD = {}
    bD["UI"] = [sb.tile(f"d1_ui{i}", [128, OS2], I32) for i in range(2)]
    bD["SCT"] = [sb.tile(f"d1_sct{i}", [128, OS2], F32) for i in range(2)]
    bD["NL"] = sb.tile("d1_nl", [128, OS2], I32)
    bD["NH"] = sb.tile("d1_nh", [128, OS2], I32)
    bD["T2"] = sb.tile("d1_t2", [128, OS2], I32)
    bD["SG"] = sb.tile("d1_sg", [128, OS2], I32)
    bD["VP"] = sb.tile("d1_vp", [128, OS2], F32)
    bD["OUTQ"] = [[sb.tile(f"d1_outq{p}_{i}", [128, OS2], BF16) for i in range(2)]
                  for p in range(2)]
    sb.reset(m0)
    io4 = {k: io[k] for k in ("x3T", "yT", "gupT", "gup_sr", "dnT", "dn_sr",
                              "wgu_scr", "wdn_scr", "wdn_t")}
    io4.update(h_scr=io["h1_scr"])
    b4 = plan_ffn1(sb, io4)
    sb.reset(m0)

    pb = [nc.alloc_psum_tensor(f"pb{i}", [128, 512], F32) for i in range(7)]
    ptr = nc.alloc_psum_tensor("ptr", [128, 1024], BF16)
    sems = {k: nc.alloc_semaphore(name=f"s_{k}") for k in SEM_NAMES}

    gupv = io["gupT"].rearrange("(s p) o -> p s o", p=128)
    gscv = io["gup_sr"].rearrange("(s p) o -> p s o", p=128)
    wguv = io["wgu_scr"].rearrange("(c p) o -> p c o", p=128)
    d1_jobs = [(s, g) for s in range(8) for g in range(11264 // 512)]
    N_FILLER_SLOTS = 3 * NTB + NH * NQB

    total_cap = 3 * NTB * 16 + sum(4 * (qb + 1) for _h in range(NH)
                                   for qb in range(NQB))

    def emit(ctx):
        st8 = {"i": 0, "dve": {}, "st": {}, "cap": 0, "pool_ok": False}

        def filler(w=16):
            st8["cap"] += w
            target = min(len(d1_jobs), len(d1_jobs) * st8["cap"] // total_cap)
            while st8["i"] < target:
                emit_d1_seg(ctx, bD, gupv, gscv, wguv, d1_jobs[st8["i"]], st8,
                            use_pool=st8["pool_ok"])

        ctx.op("vector", "memset", ones_b[:, :], 1.0)
        ctx.op("vector", "memset", ones_c[:, :], 1.0)
        ctx.op("vector", "memset", ones_f[:, :], 1.0, incs=(("dve", 1),))
        emit_attn0(ctx, b1, pb, ptr, ones_b, ones_c, ones_f)
        phase_barrier(ctx)
        emit_ffn0(ctx, b2, pb, ones_b, ones_c, ones_f, ln_banks=(2, 3))
        phase_barrier(ctx)
        emit_attn1(ctx, b3, pb, ones_b, ones_c, ones_f, ln_banks=(0, 1),
                   filler=filler)
        while st8["i"] < len(d1_jobs):
            emit_d1_seg(ctx, bD, gupv, gscv, wguv, d1_jobs[st8["i"]], st8)
        phase_barrier(ctx)
        emit_ffn1(ctx, b4, pb, ones_b, ones_c, ones_f, gate_dequant=False)
        ctx.wait("sync", "st")

    _run_block(nc, emit, sems)
    return nc


def host_prep_fused(inputs):
    w = {
        "l0_wqT": np.ascontiguousarray(inputs["L0_attn_Wq"].T),
        "l0_wkT": np.ascontiguousarray(inputs["L0_attn_Wk"].T),
        "l0_wvT": np.ascontiguousarray(inputs["L0_attn_Wv"].T),
        "l0_wgT": np.ascontiguousarray(inputs["L0_attn_Wg"].T),
        "l0_woT": np.ascontiguousarray(inputs["L0_attn_Wo"].T),
        "gn_t": np.ascontiguousarray(inputs["L0_attn_gnorm_w"].reshape(4, 128).T),
        "gwT": np.ascontiguousarray(inputs["L0_ffn_gate_w"].T),
        "uwT": np.ascontiguousarray(inputs["L0_ffn_up_w"].T),
        "dwT": np.ascontiguousarray(inputs["L0_ffn_down_w"].T),
        "gs_t": np.ascontiguousarray(inputs["L0_ffn_gate_s"].reshape(44, 128).T),
        "us_t": np.ascontiguousarray(inputs["L0_ffn_up_s"].reshape(44, 128).T),
        "ds_t": np.ascontiguousarray(inputs["L0_ffn_down_s"].reshape(16, 128).T),
        "wqT": np.ascontiguousarray(inputs["L1_attn_W_Q"].transpose(2, 0, 1).reshape(D, D)),
        "wkT": np.ascontiguousarray(inputs["L1_attn_W_K"].transpose(2, 0, 1).reshape(D, D)),
        "wvT": np.ascontiguousarray(inputs["L1_attn_W_V"].transpose(2, 0, 1).reshape(D, D)),
        "woT": np.ascontiguousarray(inputs["L1_attn_W_O"].transpose(0, 2, 1).reshape(D, D)),
        "gupT": np.ascontiguousarray(
            inputs["L1_ffn_gate_up_packed"].view(np.uint8).reshape(11264, 1024).T),
        "gup_sr": np.ascontiguousarray(
            np.repeat(inputs["L1_ffn_gate_up_scales"], 64, axis=0)),
        "dnT": np.ascontiguousarray(
            inputs["L1_ffn_down_packed"].view(np.uint8).reshape(2048, 2816).T),
        "dn_sr": np.ascontiguousarray(
            np.repeat(inputs["L1_ffn_down_scales"], 64, axis=0)),
    }
    x = inputs["L0_attn_x"]
    return [dict(w, x0T=np.ascontiguousarray(x[c].T)) for c in range(NCORES)]


def kernel(**inputs):
    nc = build_fused_program()
    maps = host_prep_fused(inputs)
    res = run_bass_kernel_spmd(nc, maps, core_ids=list(range(NCORES)))
    out = np.stack([np.ascontiguousarray(r["yT"].T) for r in res.results])
    return out.astype(np.float32)


def host_prep_ffn0_t(inputs, x1_list):
    gwT = np.ascontiguousarray(inputs["L0_ffn_gate_w"].T)
    uwT = np.ascontiguousarray(inputs["L0_ffn_up_w"].T)
    dwT = np.ascontiguousarray(inputs["L0_ffn_down_w"].T)
    gs_t = np.ascontiguousarray(np.asarray(inputs["L0_ffn_gate_s"]).reshape(44, 128).T)
    us_t = np.ascontiguousarray(np.asarray(inputs["L0_ffn_up_s"]).reshape(44, 128).T)
    ds_t = np.ascontiguousarray(np.asarray(inputs["L0_ffn_down_s"]).reshape(16, 128).T)
    return [{"x1T": x1_list[c], "gwT": gwT, "uwT": uwT, "dwT": dwT,
             "gs_t": gs_t, "us_t": us_t, "ds_t": ds_t} for c in range(NCORES)]


def host_prep_attn1_t(inputs, x2_list):
    wqT = np.ascontiguousarray(inputs["L1_attn_W_Q"].transpose(2, 0, 1).reshape(D, D))
    wkT = np.ascontiguousarray(inputs["L1_attn_W_K"].transpose(2, 0, 1).reshape(D, D))
    wvT = np.ascontiguousarray(inputs["L1_attn_W_V"].transpose(2, 0, 1).reshape(D, D))
    woT = np.ascontiguousarray(inputs["L1_attn_W_O"].transpose(0, 2, 1).reshape(D, D))
    return [{"x2T": x2_list[c], "wqT": wqT, "wkT": wkT, "wvT": wvT, "woT": woT}
            for c in range(NCORES)]


def host_prep_ffn1_t(inputs, x3_list):
    gupT = np.ascontiguousarray(
        np.asarray(inputs["L1_ffn_gate_up_packed"]).view(np.uint8).reshape(11264, 1024).T)
    gup_sr = np.ascontiguousarray(
        np.repeat(np.asarray(inputs["L1_ffn_gate_up_scales"]), 64, axis=0))
    dnT = np.ascontiguousarray(
        np.asarray(inputs["L1_ffn_down_packed"]).view(np.uint8).reshape(2048, 2816).T)
    dn_sr = np.ascontiguousarray(
        np.repeat(np.asarray(inputs["L1_ffn_down_scales"]), 64, axis=0))
    return [{"x3T": x3_list[c], "gupT": gupT, "gup_sr": gup_sr,
             "dnT": dnT, "dn_sr": dn_sr} for c in range(NCORES)]



NH = 16
DH = 128
QB = 512
NQB = T // QB
KC = 128

def plan_attn1(sb, io):
    b = dict(io)
    b["BM"] = sb.tile("p3_bm", [128, 4 * QB], BF16)
    b["EB"] = sb.tile("p3_eb", [128, 1], F32)
    m = sb.mark()
    b["WT"] = sb.tile("p3_wt", [128, DC * D], BF16)
    b["XT"] = [sb.tile(f"p3_xt{i}", [128, DC * TB], BF16) for i in range(2)]
    b["EV"] = [sb.tile(f"p3_ev{i}", [128, TB], BF16) for i in range(2)]
    sb.reset(m)
    b["KH"] = [sb.tile(f"p3_kh{i}", [128, T], BF16) for i in range(2)]
    b["VH"] = [sb.tile(f"p3_vh{i}", [128, 32 * DH], BF16) for i in range(2)]
    b["QH"] = [sb.tile(f"p3_qh{i}", [128, T], BF16) for i in range(2)]
    b["PT"] = sb.tile("p3_pt", [128, 32 * QB], BF16)
    b["RLT"] = sb.tile("p3_rlt", [128, QB], F32)
    b["ZB"] = sb.tile("p3_zb", [128, QB], BF16)
    b["RL"] = sb.tile("p3_rl", [1, QB], F32)
    sb.reset(m)
    b["WO"] = sb.tile("p3_wo", [128, DC * D], BF16)
    b["ZV"] = sb.tile("p3_zv", [128, DC * TB], BF16)
    b["XA"] = sb.tile("p3_xa", [128, DC * TB], F32)
    b["XN"] = sb.tile("p3_xn", [128, DC * TB], F32)
    b["Y"] = sb.tile("p3_y", [128, DC * TB], BF16)
    b["ST"] = sb.tile("p3_st", [1, 5 * TB], F32)
    sb.reset(m)
    return b


def emit_attn1(ctx, b, pb, ones_b, ones_c, ones_f, x_in="x2T", x_out="x3T",
               ln_banks=(6, 7), filler=None, flash_hook=None):
    x2T, x3T = b[x_in], b[x_out]
    qs, ks, vs, zs = b["q_scr"], b["k_scr"], b["v_scr"], b["z_scr"]
    x2v = x2T.rearrange("(c p) t -> p c t", p=128)
    x3v = x3T.rearrange("(c p) t -> p c t", p=128)

    for a in range(4):
        ctx.op("gpsimd", "memset", b["BM"][:, QB * a:QB * (a + 1)], 1.0,
               incs=(("pool", 1),))
        ctx.wait("gpsimd", "pool")
        ctx.op("gpsimd", "affine_select",
               b["BM"][:, QB * a:QB * (a + 1)], b["BM"][:, QB * a:QB * (a + 1)],
               pattern=[[1, QB]], compare_op=ALU.is_ge, fill=0.0,
               channel_multiplier=-1, base=-128 * a, incs=(("pool", 1),))

    pe_at_w = {}
    pe_at_mm = {}
    bank_h = Hist(4)
    ev_h = Hist(2)
    for wi, (wT, outscr, tokmajor) in enumerate((
            (b["wqT"], qs, False), (b["wkT"], ks, False), (b["wvT"], vs, True))):
        wv = wT.rearrange("(c p) e -> p c e", p=128)
        if wi > 0:
            ctx.wait_val("gpsimd", "pe", pe_at_w[wi - 1])
        for i in range(2):
            ctx.dma("gpsimd", "wa",
                   b["WT"][:, :].rearrange("p (c e) -> p c e", c=DC)[:, 8 * i:8 * i + 8, :],
                   wv[:, 8 * i:8 * i + 8, :])
        for tb in range(NTB):
            if filler is not None:
                filler()
            xt = b["XT"][tb % 2]
            key = (wi, tb)
            prev = (wi, tb - 2) if tb >= 2 else ((wi - 1, NTB - 2 + tb) if wi >= 1 else None)
            if prev is not None:
                ctx.wait_val("gpsimd", "pe", pe_at_mm[prev])
            ctx.dma("gpsimd", "ga",
                   xt[:, :].rearrange("p (c t) -> p c t", c=DC),
                   x2v[:, :, TB * tb:TB * (tb + 1)])
            ctx.wait("tensor", "wa")
            ctx.wait("tensor", "ga")
            if not tokmajor:
                for oc in range(DC):
                    bank = pb[oc % 4]
                    ctx.wait_val("tensor", "act", bank_h.guard())
                    for kc in range(DC):
                        ctx.op("tensor", "matmul", bank[:, :],
                               b["WT"][:, D * kc + 128 * oc: D * kc + 128 * (oc + 1)],
                               xt[:, TB * kc:TB * (kc + 1)],
                               start=(kc == 0), stop=(kc == DC - 1),
                               incs=((("pe", 1),) if kc == DC - 1 else ()))
                    ev = b["EV"][oc % 2]
                    ctx.wait("scalar", "pe")
                    ctx.wait_val("scalar", "st", ev_h.guard())
                    ctx.op("scalar", "activation", ev[:, :], bank[:, :], AF.Copy,
                           incs=(("act", 1),))
                    bank_h.push(ctx.val("act"))
                    ctx.wait("sync", "act")
                    ctx.dma("sync", "st",
                           outscr.rearrange("(c p) t -> p c t", p=128)[:, oc, TB * tb:TB * (tb + 1)],
                           ev[:, :])
                    ev_h.push(ctx.val("st"))
            else:
                for ts in range(4):
                    for eb in range(4):
                        bank = pb[(4 * ts + eb) % 4]
                        ctx.wait_val("tensor", "act", bank_h.guard())
                        for kc in range(DC):
                            ctx.op("tensor", "matmul", bank[:, :],
                                   xt[:, TB * kc + 128 * ts: TB * kc + 128 * (ts + 1)],
                                   b["WT"][:, D * kc + TB * eb: D * kc + TB * (eb + 1)],
                                   start=(kc == 0), stop=(kc == DC - 1),
                                   incs=((("pe", 1),) if kc == DC - 1 else ()))
                        ev = b["EV"][eb % 2]
                        ctx.wait("scalar", "pe")
                        ctx.wait_val("scalar", "st", ev_h.guard())
                        ctx.op("scalar", "activation", ev[:, :], bank[:, :], AF.Copy,
                               incs=(("act", 1),))
                        bank_h.push(ctx.val("act"))
                        ctx.wait("sync", "act")
                        ctx.dma("sync", "st",
                               vs.rearrange("(tt p) e -> p tt e", p=128)[:, 4 * tb + ts, TB * eb:TB * (eb + 1)],
                               ev[:, :])
                        ev_h.push(ctx.val("st"))
            pe_at_mm[key] = ctx.val("pe")
        pe_at_w[wi] = ctx.val("pe")

    if flash_hook is not None:
        flash_hook()
    ctx.wait("sync", "st")
    inv_sqrt_dh = 1.0 / math.sqrt(DH)
    ctx.op("vector", "memset", b["EB"][:, :], -8.0, incs=(("dve", 1),))
    dve_eb = ctx.val("dve")
    pe_at_hq = {}
    xa_at_head = {}
    lo_guard = {}
    sc_h = Hist(2)
    order = [(h, qb) for h in range(NH) for qb in range(NQB)]

    def load_head(hh):
        slot = hh % 2
        ctx.dma("sync", "xa", b["KH"][slot][:, :], ks[128 * hh:128 * (hh + 1), :])
        ctx.dma("sync", "xa",
               b["VH"][slot][:, :].rearrange("p (kc e) -> p kc e", kc=32),
               vs.rearrange("(kc p) e -> p kc e", p=128)[:, :, DH * hh:DH * (hh + 1)])
        ctx.dma("sync", "xa", b["QH"][slot][:, :], qs[128 * hh:128 * (hh + 1), :])
        xa_at_head[hh] = ctx.val("xa")

    for idx, (h, qb) in enumerate(order):
        if idx == 0:
            load_head(0)
        kh, vh, qh = b["KH"][h % 2], b["VH"][h % 2], b["QH"][h % 2]
        if qb == NQB - 2 and h + 1 < NH:
            if h >= 1:
                ctx.wait_val("sync", "pe", pe_at_hq[(h - 1, NQB - 1)])
            load_head(h + 1)
        nkc = 4 * (qb + 1)
        ctx.wait_val("tensor", "xa", xa_at_head[h])
        for kc in range(nkc):
            bank = pb[kc % 2]
            ctx.wait_val("tensor", "act", sc_h.guard())
            ctx.op("tensor", "matmul", bank[:, 0:QB],
                   kh[:, 128 * kc:128 * (kc + 1)],
                   qh[:, QB * qb:QB * (qb + 1)],
                   start=True, stop=True, incs=(("pe", 1),))
            ctx.wait("scalar", "pe")
            diag = kc - 4 * qb
            ctx.wait_val("scalar", "dve", dve_eb)
            ctx.op("scalar", "activation", b["PT"][:, QB * kc:QB * (kc + 1)],
                   bank[:, 0:QB], AF.Exp, scale=inv_sqrt_dh, bias=b["EB"][:, 0:1],
                   incs=(("act", 1),))
            sc_h.push(ctx.val("act"))
            if diag >= 0:
                ctx.wait("vector", "act")
                ctx.op("vector", "tensor_mul",
                       b["PT"][:, QB * kc:QB * (kc + 1)],
                       b["PT"][:, QB * kc:QB * (kc + 1)],
                       b["BM"][:, QB * diag:QB * (diag + 1)],
                       incs=(("dve", 1),))
        mask_dve = ctx.val("dve")
        bl, bo = (pb[2], pb[3]) if idx % 2 == 0 else (pb[4], pb[5])
        ctx.wait("tensor", "act")
        ctx.wait_val("tensor", "dve", mask_dve)
        ctx.wait_val("tensor", "act", lo_guard.get(("rl", idx - 2), 0))
        ctx.wait_val("tensor", "dve", lo_guard.get(("zb", idx - 2), 0))
        ctx.wait("tensor", "pool")
        for kc in range(nkc):
            ctx.op("tensor", "matmul", bl[0:1, 0:QB], ones_b[:, 0:1],
                   b["PT"][:, QB * kc:QB * (kc + 1)],
                   start=(kc == 0), stop=(kc == nkc - 1))
        for kc in range(nkc):
            ctx.op("tensor", "matmul", bo[:, 0:QB],
                   vh[:, DH * kc:DH * (kc + 1)],
                   b["PT"][:, QB * kc:QB * (kc + 1)],
                   start=(kc == 0), stop=(kc == nkc - 1),
                   incs=((("pe", 1),) if kc == nkc - 1 else ()))
        pe_at_hq[(h, qb)] = ctx.val("pe")
        ctx.wait("scalar", "pe")
        ctx.op("scalar", "activation", b["RL"][0:1, :], bl[0:1, 0:QB], AF.Copy,
               incs=(("act", 1),))
        lo_guard[("rl", idx)] = ctx.val("act")
        bb = pb[6]
        ctx.wait("tensor", "act")
        ctx.wait("tensor", "dve")
        ctx.op("tensor", "matmul", bb[:, 0:QB], ones_f[0:1, :], b["RL"][0:1, :],
               start=True, stop=True, incs=(("pe", 1),))
        ctx.wait("vector", "pe")
        ctx.op("vector", "reciprocal", b["RLT"][:, :], bb[:, 0:QB],
               incs=(("dve", 1),))
        ctx.wait("vector", "dve")
        ctx.wait("vector", "st")
        ctx.op("vector", "tensor_mul", b["ZB"][:, :], bo[:, 0:QB], b["RLT"][:, :],
               incs=(("dve", 1),))
        lo_guard[("zb", idx)] = ctx.val("dve")
        ctx.wait("sync", "dve")
        ctx.dma("sync", "st",
               zs.rearrange("(c p) t -> p c t", p=128)[:, h, QB * qb:QB * (qb + 1)],
               b["ZB"][:, :])
        if filler is not None:
            filler(nkc)

    ctx.wait("sync", "st")
    ctx.wait("gpsimd", "pe")
    ctx.wait("gpsimd", "dve")
    ctx.wait("gpsimd", "act")
    ctx.wait("gpsimd", "st")
    wov = b["woT"].rearrange("(c p) e -> p c e", p=128)
    for i in range(2):
        ctx.dma("gpsimd", "wb",
               b["WO"][:, :].rearrange("p (c e) -> p c e", c=DC)[:, 8 * i:8 * i + 8, :],
               wov[:, 8 * i:8 * i + 8, :])
    zv = zs.rearrange("(c p) t -> p c t", p=128)
    pe_at_tb = {}
    dve_at_tb = {}
    xa_free = {}
    cbank_h = Hist(4)
    for tb in range(NTB):
        if tb >= 2:
            ctx.wait_val("sync", "pe", pe_at_tb[tb - 2])
        if tb >= 1:
            ctx.wait_val("sync", "dve", xa_free[tb - 1])
        ctx.dma("sync", "xa",
               b["ZV"][:, :].rearrange("p (c t) -> p c t", c=DC),
               zv[:, :, TB * tb:TB * (tb + 1)])
        ctx.dma("sync", "xb",
               b["XA"][:, :].rearrange("p (c t) -> p c t", c=DC),
               x2v[:, :, TB * tb:TB * (tb + 1)])
        ctx.wait("tensor", "wb")
        ctx.wait("tensor", "xa")
        for dc in range(DC):
            bank = pb[(4, 5, 2, 3)[dc % 4]]
            ctx.wait_val("tensor", "act", cbank_h.guard())
            for kc in range(DC):
                ctx.op("tensor", "matmul", bank[:, :],
                       b["WO"][:, D * kc + 128 * dc: D * kc + 128 * (dc + 1)],
                       b["ZV"][:, TB * kc:TB * (kc + 1)],
                       start=(kc == 0), stop=(kc == DC - 1),
                       incs=((("pe", 1),) if kc == DC - 1 else ()))
            ctx.wait("scalar", "pe")
            ctx.wait("scalar", "dve")
            ctx.op("scalar", "activation", b["Y"][:, TB * dc:TB * (dc + 1)], bank[:, :],
                   AF.Copy, incs=(("act", 1),))
            cbank_h.push(ctx.val("act"))
            emit_postln_prep(ctx, b, dc)
        pe_at_tb[tb] = ctx.val("pe")
        xa_free[tb] = ctx.val("dve")
        dve_at_tb[tb] = emit_postln(ctx, b, pb, ones_b, ones_c, ones_f,
                                    x3v[:, :, TB * tb:TB * (tb + 1)],
                                    banks=ln_banks)


def build_attn1_program():
    nc = bass.Bass("TRN2", target_bir_lowering=False)
    io = {}
    io["x2T"] = nc.dram_tensor("x2T", [D, T], F32, kind="ExternalInput").ap()
    io["wqT"] = nc.dram_tensor("wqT", [D, D], F32, kind="ExternalInput").ap()
    io["wkT"] = nc.dram_tensor("wkT", [D, D], F32, kind="ExternalInput").ap()
    io["wvT"] = nc.dram_tensor("wvT", [D, D], F32, kind="ExternalInput").ap()
    io["woT"] = nc.dram_tensor("woT", [D, D], F32, kind="ExternalInput").ap()
    io["x3T"] = nc.dram_tensor("x3T", [D, T], F32, kind="ExternalOutput").ap()
    io["q_scr"] = nc.dram_tensor("q_scr", [D, T], BF16, kind="Internal").ap()
    io["k_scr"] = nc.dram_tensor("k_scr", [D, T], BF16, kind="Internal").ap()
    io["v_scr"] = nc.dram_tensor("v_scr", [T, D], BF16, kind="Internal").ap()
    io["z_scr"] = nc.dram_tensor("z_scr", [D, T], BF16, kind="Internal").ap()

    sb = SbufAlloc(nc)
    ones_b = sb.tile("ones_b", [128, 1], BF16)
    ones_c = sb.tile("ones_c", [128, 1], F32)
    ones_f = sb.tile("ones_f", [1, 128], F32)
    bufs = plan_attn1(sb, io)
    pb = [nc.alloc_psum_tensor(f"pb{i}", [128, 512], F32) for i in range(8)]
    sems = {k: nc.alloc_semaphore(name=f"s_{k}") for k in SEM_NAMES}

    def emit(ctx):
        ctx.op("vector", "memset", ones_b[:, :], 1.0)
        ctx.op("vector", "memset", ones_c[:, :], 1.0)
        ctx.op("vector", "memset", ones_f[:, :], 1.0, incs=(("dve", 1),))
        emit_attn1(ctx, bufs, pb, ones_b, ones_c, ones_f)
        ctx.wait("sync", "st")

    _run_block(nc, emit, sems)
    return nc


def host_prep_attn1(inputs, x2_full):
    wqT = np.ascontiguousarray(inputs["L1_attn_W_Q"].transpose(2, 0, 1).reshape(D, D))
    wkT = np.ascontiguousarray(inputs["L1_attn_W_K"].transpose(2, 0, 1).reshape(D, D))
    wvT = np.ascontiguousarray(inputs["L1_attn_W_V"].transpose(2, 0, 1).reshape(D, D))
    woT = np.ascontiguousarray(inputs["L1_attn_W_O"].transpose(0, 2, 1).reshape(D, D))
    maps = []
    for c in range(NCORES):
        maps.append({
            "x2T": np.ascontiguousarray(x2_full[c].T),
            "wqT": wqT, "wkT": wkT, "wvT": wvT, "woT": woT,
        })
    return maps



RH = 4
RDK = 256
RDV = 512
RC = 128
NRC = T // RC

def plan_attn0(sb, io):
    b = dict(io)
    b["EPS"] = sb.tile("p1_eps", [128, 1], F32)
    b["GBC"] = sb.tile("p1_gbc", [128, 8], F32)
    b["QSC"] = sb.tile("p1_qsc", [128, RH * TB], BF16)
    b["KSC"] = sb.tile("p1_ksc", [128, RH * TB], BF16)
    b["CM4"] = sb.tile("p1_cm4", [128, 4 * RC], BF16)
    b["ID"] = sb.tile("p1_id", [128, 128], BF16)
    b["SF"] = sb.tile("p1_sf", [128, RH * 2 * RDV], F32)
    b["SB16"] = sb.tile("p1_sb16", [128, RH * 2 * RDV], BF16)
    m = sb.mark()
    b["WT"] = sb.tile("p1_wt", [128, DC * D], BF16)
    b["XT"] = [sb.tile(f"p1_xt{i}", [128, DC * TB], BF16) for i in range(2)]
    b["COS"] = sb.tile("p1_cos", [128, T], F32)
    b["SIN"] = sb.tile("p1_sin", [128, T], F32)
    b["TM1"] = sb.tile("p1_tm1", [128, TB], F32)
    b["TM2"] = sb.tile("p1_tm2", [128, TB], F32)
    b["TM3"] = sb.tile("p1_tm3", [128, TB], F32)
    b["TM4"] = sb.tile("p1_tm4", [128, TB], F32)
    b["EV"] = [sb.tile(f"p1_ev{i}", [128, TB], BF16) for i in range(4)]
    b["TROW"] = sb.tile("p1_trow", [1, TB], F32)
    b["IVF"] = sb.tile("p1_ivf", [1, 128], F32)
    sb.reset(m)
    b["QT4"] = [sb.tile(f"p1_qt4{i}", [128, RH * 2 * 128], BF16) for i in range(4)]
    b["KT4"] = [sb.tile(f"p1_kt4{i}", [128, RH * 2 * 128], BF16) for i in range(4)]
    b["VC4"] = [sb.tile(f"p1_vc4{i}", [128, RH * RDV], BF16) for i in range(4)]
    b["GT4"] = [sb.tile(f"p1_gt4{i}", [128, RH * RDV], BF16) for i in range(4)]
    b["PA4"] = sb.tile("p1_pa4", [128, RH * 128], BF16)
    b["KTM4"] = sb.tile("p1_ktm4", [128, RH * 2 * 128], BF16)
    b["OO4"] = [sb.tile(f"p1_oo4{i}", [128, RH * RDV], F32) for i in range(2)]
    b["SQ4"] = [sb.tile(f"p1_sq4{i}", [128, RH * RDV], BF16) for i in range(2)]
    b["SRW"] = sb.tile("p1_srw", [1, RH * 128], F32)
    b["RSTD"] = [sb.tile(f"p1_rstd{i}", [128, RH * 128], F32) for i in range(2)]
    b["OG4"] = [sb.tile(f"p1_og4{i}", [128, RH * RDV], BF16) for i in range(2)]
    sb.reset(m)
    b["WO"] = sb.tile("p1_wo", [128, DC * D], BF16)
    b["GNC"] = sb.tile("p1_gnc", [128, 4], F32)
    b["ZV"] = sb.tile("p1_zv", [128, DC * TB], BF16)
    b["XA"] = sb.tile("p1_xa", [128, DC * TB], F32)
    b["XN"] = sb.tile("p1_xn", [128, DC * TB], F32)
    b["Y"] = sb.tile("p1_y", [128, DC * TB], BF16)
    b["ST"] = sb.tile("p1_st", [1, 5 * TB], F32)
    sb.reset(m)
    return b


def emit_attn0(ctx, b, pb, ptr, ones_b, ones_c, ones_f, x_in="x0T", x_out="x1T"):
    x0T, x1T = b[x_in], b[x_out]
    x0v = x0T.rearrange("(c p) t -> p c t", p=128)
    x1v = x1T.rearrange("(c p) t -> p c t", p=128)
    qs, ks, vs, gs, ogs = b["q_scr"], b["k_scr"], b["v_scr"], b["g_scr"], b["og_scr"]
    gammas = [1.0 - 2.0 ** (-5.0 - h) for h in range(RH)]
    LN2_10K = math.log(10000.0)
    TWO_PI = 2.0 * math.pi

    ctx.op("vector", "memset", b["EPS"][:, :], 1e-5, incs=(("dve", 1),))
    for h in range(RH):
        lg = math.log(gammas[h])
        ctx.op("vector", "memset", b["GBC"][:, h:h + 1], lg - math.log(16.0))
        ctx.op("vector", "memset", b["GBC"][:, 4 + h:5 + h], -lg,
               incs=((("dve", 1),) if h == RH - 1 else ()))
    ctx.op("gpsimd", "memset", b["ID"][:, :], 1.0, incs=(("pool", 1),))
    ctx.wait("gpsimd", "pool")
    ctx.op("gpsimd", "affine_select", b["ID"][:, :], b["ID"][:, :],
           pattern=[[-1, 128]], compare_op=ALU.is_equal, fill=0.0,
           channel_multiplier=1, incs=(("pool", 1),))
    ctx.op("gpsimd", "iota", b["IVF"][:, :], pattern=[[1, 128]], base=0,
           channel_multiplier=0, allow_small_or_imprecise_dtypes=True,
           incs=(("pool", 1),))
    for h in range(RH):
        for a in range(TB // RC):
            ctx.op("gpsimd", "iota", b["QSC"][:, TB * h + RC * a:TB * h + RC * (a + 1)],
                   pattern=[[1, RC]], base=0, channel_multiplier=0,
                   allow_small_or_imprecise_dtypes=True, incs=(("pool", 1),))
            ctx.op("gpsimd", "iota", b["KSC"][:, TB * h + RC * a:TB * h + RC * (a + 1)],
                   pattern=[[1, RC]], base=0, channel_multiplier=0,
                   allow_small_or_imprecise_dtypes=True, incs=(("pool", 1),))
    ctx.op("gpsimd", "memset", b["CM4"][:, :], 1.0, incs=(("pool", 1),))
    ctx.wait("gpsimd", "pool")
    for a in range(4):
        ctx.op("gpsimd", "affine_select",
               b["CM4"][:, RC * a:RC * (a + 1)], b["CM4"][:, RC * a:RC * (a + 1)],
               pattern=[[1, RC]], compare_op=ALU.is_ge, fill=0.0,
               channel_multiplier=-1, incs=(("pool", 1),))
    ctx.wait("scalar", "pool")
    ctx.wait("scalar", "dve")
    ctx.op("scalar", "activation", b["IVF"][:, :], b["IVF"][:, :], AF.Exp,
           scale=-LN2_10K / 128.0, incs=(("act", 1),))
    for h in range(RH):
        lg = math.log(gammas[h])
        ctx.op("scalar", "activation", b["QSC"][:, TB * h:TB * (h + 1)],
               b["QSC"][:, TB * h:TB * (h + 1)], AF.Exp, scale=lg,
               bias=b["GBC"][:, h:h + 1], incs=(("act", 1),))
        ctx.op("scalar", "activation", b["KSC"][:, TB * h:TB * (h + 1)],
               b["KSC"][:, TB * h:TB * (h + 1)], AF.Exp, scale=-lg,
               bias=b["GBC"][:, 4 + h:5 + h], incs=(("act", 1),))
    for tb in range(NTB):
        ctx.wait("gpsimd", "act")
        ctx.wait_val("gpsimd", "pe", ctx.val("pe"))
        ctx.op("gpsimd", "iota", b["TROW"][:, :], pattern=[[1, TB]], base=TB * tb,
               channel_multiplier=0, allow_small_or_imprecise_dtypes=True,
               incs=(("pool", 1),))
        bank = pb[2]
        ctx.wait("tensor", "pool")
        ctx.wait("tensor", "act")
        ctx.wait("tensor", "dve")
        ctx.op("tensor", "matmul", bank[:, 0:TB], b["IVF"][:, :], b["TROW"][:, :],
               start=True, stop=True, incs=(("pe", 1),))
        INV2PI = 1.0 / TWO_PI
        ctx.wait("vector", "pe")
        ctx.op("vector", "tensor_scalar", b["TM3"][:, :], bank[:, 0:TB],
               INV2PI, None, op0=ALU.mult, incs=(("dve", 1),))
        ctx.wait("vector", "dve")
        ctx.op("vector", "tensor_copy", b["TM4"][:, :].bitcast(I32), b["TM3"][:, :],
               incs=(("dve", 1),))
        ctx.wait("vector", "dve")
        ctx.op("vector", "tensor_copy", b["TM2"][:, :], b["TM4"][:, :].bitcast(I32),
               incs=(("dve", 1),))
        ctx.wait("vector", "dve")
        ctx.op("vector", "scalar_tensor_tensor", b["TM1"][:, :], b["TM2"][:, :],
               -TWO_PI, bank[:, 0:TB], op0=ALU.mult, op1=ALU.add, incs=(("dve", 1),))
        ctx.wait("vector", "dve")
        ctx.op("vector", "tensor_scalar", b["TM2"][:, :], b["TM1"][:, :],
               math.pi, None, op0=ALU.is_gt, incs=(("dve", 1),))
        ctx.wait("vector", "dve")
        ctx.op("vector", "scalar_tensor_tensor", b["TM1"][:, :], b["TM2"][:, :],
               -TWO_PI, b["TM1"][:, :], op0=ALU.mult, op1=ALU.add, incs=(("dve", 1),))
        ctx.op("vector", "tensor_scalar", b["TM3"][:, :], bank[:, 0:TB],
               1.0, 0.5 * math.pi, op0=ALU.mult, op1=ALU.add, incs=(("dve", 1),))
        ctx.wait("vector", "dve")
        ctx.op("vector", "tensor_scalar", b["TM2"][:, :], b["TM3"][:, :],
               INV2PI, None, op0=ALU.mult, incs=(("dve", 1),))
        ctx.wait("vector", "dve")
        ctx.op("vector", "tensor_copy", b["TM4"][:, :].bitcast(I32), b["TM2"][:, :],
               incs=(("dve", 1),))
        ctx.wait("vector", "dve")
        ctx.op("vector", "tensor_copy", b["TM2"][:, :], b["TM4"][:, :].bitcast(I32),
               incs=(("dve", 1),))
        ctx.wait("vector", "dve")
        ctx.op("vector", "scalar_tensor_tensor", b["TM4"][:, :], b["TM2"][:, :],
               -TWO_PI, b["TM3"][:, :], op0=ALU.mult, op1=ALU.add, incs=(("dve", 1),))
        ctx.wait("vector", "dve")
        ctx.op("vector", "tensor_scalar", b["TM2"][:, :], b["TM4"][:, :],
               math.pi, None, op0=ALU.is_gt, incs=(("dve", 1),))
        ctx.wait("vector", "dve")
        ctx.op("vector", "scalar_tensor_tensor", b["TM4"][:, :], b["TM2"][:, :],
               -TWO_PI, b["TM4"][:, :], op0=ALU.mult, op1=ALU.add, incs=(("dve", 1),))
        ctx.wait("scalar", "dve")
        ctx.op("scalar", "activation", b["SIN"][:, TB * tb:TB * (tb + 1)], b["TM1"][:, :],
               AF.Sin, incs=(("act", 1),))
        ctx.op("scalar", "activation", b["COS"][:, TB * tb:TB * (tb + 1)], b["TM4"][:, :],
               AF.Sin, incs=(("act", 1),))

    pe_at_wt = {}
    pe_at_xt = {}
    rope_h = Hist(2)
    for wi, (wT, outscr) in enumerate(((b["l0_wqT"], qs), (b["l0_wkT"], ks))):
        wv = wT.rearrange("(c p) e -> p c e", p=128)
        if wi >= 1:
            ctx.wait_val("gpsimd", "pe", pe_at_wt[wi - 1])
        ctx.dma("gpsimd", "wa",
                b["WT"][:, 0:DC * 1024].rearrange("p (c e) -> p c e", c=DC),
                wv[:, :, :])
        ovw = outscr.rearrange("(h c2 p) t -> p h c2 t", h=RH, c2=2)
        for tb in range(NTB):
            xt = b["XT"][tb % 2]
            key = (wi, tb)
            prev = (wi, tb - 2) if tb >= 2 else ((wi - 1, NTB - 2 + tb) if wi >= 1 else None)
            if prev is not None:
                ctx.wait_val("gpsimd", "pe", pe_at_xt[prev])
            ctx.dma("gpsimd", "ga",
                    xt[:, :].rearrange("p (c t) -> p c t", c=DC),
                    x0v[:, :, TB * tb:TB * (tb + 1)])
            ctx.wait("tensor", "wa")
            ctx.wait("tensor", "ga")
            for h in range(RH):
                b0 = pb[2 * (h % 2)]
                b1 = pb[2 * (h % 2) + 1]
                ctx.wait_val("tensor", "dve", rope_h.guard())
                for c2 in range(2):
                    bank = b0 if c2 == 0 else b1
                    oc = 2 * h + c2
                    for kc in range(DC):
                        ctx.op("tensor", "matmul", bank[:, :],
                               b["WT"][:, 1024 * kc + 128 * oc: 1024 * kc + 128 * (oc + 1)],
                               xt[:, TB * kc:TB * (kc + 1)],
                               start=(kc == 0), stop=(kc == DC - 1),
                               incs=((("pe", 1),) if kc == DC - 1 else ()))
                cosv = b["COS"][:, TB * tb:TB * (tb + 1)]
                sinv = b["SIN"][:, TB * tb:TB * (tb + 1)]
                ctx.wait("vector", "pe")
                ctx.wait("vector", "st")
                ctx.wait("vector", "act")
                ctx.op("vector", "tensor_mul", b["TM1"][:, :], b0[:, :], cosv)
                ctx.op("vector", "tensor_mul", b["TM2"][:, :], b1[:, :], sinv,
                       incs=(("dve", 1),))
                scv = (b["QSC"] if wi == 0 else b["KSC"])[:, TB * h:TB * (h + 1)]
                ctx.wait("vector", "dve")
                ctx.op("vector", "tensor_sub", b["TM1"][:, :], b["TM1"][:, :], b["TM2"][:, :])
                ctx.op("vector", "tensor_mul", b["TM3"][:, :], b1[:, :], cosv)
                ctx.op("vector", "tensor_mul", b["TM4"][:, :], b0[:, :], sinv,
                       incs=(("dve", 1),))
                rope_h.push(ctx.val("dve"))
                ctx.wait("vector", "dve")
                ctx.op("vector", "tensor_mul", b["EV"][0][:, :], b["TM1"][:, :], scv)
                ctx.op("vector", "tensor_add", b["TM3"][:, :], b["TM3"][:, :], b["TM4"][:, :],
                       incs=(("dve", 1),))
                ctx.wait("vector", "dve")
                ctx.op("vector", "tensor_mul", b["EV"][1][:, :], b["TM3"][:, :], scv,
                       incs=(("dve", 1),))
                ctx.wait("sync", "dve")
                ctx.dma("sync", "st", ovw[:, h, 0, TB * tb:TB * (tb + 1)], b["EV"][0][:, :])
                ctx.dma("sync", "st", ovw[:, h, 1, TB * tb:TB * (tb + 1)], b["EV"][1][:, :])
            pe_at_xt[key] = ctx.val("pe")
        pe_at_wt[wi] = ctx.val("pe")

    ctx.wait("tensor", "dve")
    a0bank_h = Hist(4)
    a0ev_h = Hist(2)
    a0ev_h.push(ctx.val("st"))
    a0ev_h.push(ctx.val("st"))
    for wi, (wT, outscr, tokmajor, width, silu) in enumerate((
            (b["l0_wvT"], vs, True, D, False), (b["l0_wgT"], gs, False, D, True))):
        wv = wT.rearrange("(c p) e -> p c e", p=128)
        ctx.wait_val("gpsimd", "pe", pe_at_wt[1] if wi == 0 else pe_at_wt2[wi - 1])
        for i in range(2):
            ctx.dma("gpsimd", "wa",
                    b["WT"][:, :].rearrange("p (c e) -> p c e", c=DC)[:, 8 * i:8 * i + 8, :],
                    wv[:, 8 * i:8 * i + 8, :])
        for tb in range(NTB):
            xt = b["XT"][tb % 2]
            key = (2 + wi, tb)
            prev = (2 + wi, tb - 2) if tb >= 2 else ((1 + wi, NTB - 2 + tb))
            ctx.wait_val("gpsimd", "pe", pe_at_xt[prev])
            ctx.dma("gpsimd", "ga",
                    xt[:, :].rearrange("p (c t) -> p c t", c=DC),
                    x0v[:, :, TB * tb:TB * (tb + 1)])
            ctx.wait("tensor", "wa")
            ctx.wait("tensor", "ga")
            if tokmajor:
                for ts in range(4):
                    for eb in range(4):
                        bank = pb[(4 * ts + eb) % 4]
                        ctx.wait_val("tensor", "act", a0bank_h.guard())
                        for kc in range(DC):
                            ctx.op("tensor", "matmul", bank[:, :],
                                   xt[:, TB * kc + 128 * ts: TB * kc + 128 * (ts + 1)],
                                   b["WT"][:, D * kc + TB * eb: D * kc + TB * (eb + 1)],
                                   start=(kc == 0), stop=(kc == DC - 1),
                                   incs=((("pe", 1),) if kc == DC - 1 else ()))
                        ev = b["EV"][eb % 2]
                        ctx.wait("scalar", "pe")
                        ctx.wait_val("scalar", "st", a0ev_h.guard())
                        ctx.op("scalar", "activation", ev[:, :], bank[:, :], AF.Copy,
                               incs=(("act", 1),))
                        a0bank_h.push(ctx.val("act"))
                        ctx.wait("sync", "act")
                        ctx.dma("sync", "st",
                                outscr.rearrange("(tt p) e -> p tt e", p=128)[:, 4 * tb + ts, TB * eb:TB * (eb + 1)],
                                ev[:, :])
                        a0ev_h.push(ctx.val("st"))
            else:
                for oc in range(DC):
                    bank = pb[oc % 4]
                    ctx.wait_val("tensor", "act", a0bank_h.guard())
                    for kc in range(DC):
                        ctx.op("tensor", "matmul", bank[:, :],
                               b["WT"][:, D * kc + 128 * oc: D * kc + 128 * (oc + 1)],
                               xt[:, TB * kc:TB * (kc + 1)],
                               start=(kc == 0), stop=(kc == DC - 1),
                               incs=((("pe", 1),) if kc == DC - 1 else ()))
                    ev = b["EV"][oc % 2]
                    ctx.wait("scalar", "pe")
                    ctx.wait_val("scalar", "st", a0ev_h.guard())
                    ctx.op("scalar", "activation", ev[:, :], bank[:, :],
                           AF.Silu if silu else AF.Copy, incs=(("act", 1),))
                    a0bank_h.push(ctx.val("act"))
                    ctx.wait("sync", "act")
                    ctx.dma("sync", "st",
                            outscr.rearrange("(c p) t -> p c t", p=128)[:, oc, TB * tb:TB * (tb + 1)],
                            ev[:, :])
                    a0ev_h.push(ctx.val("st"))
            pe_at_xt[key] = ctx.val("pe")
        pe_at_wt2 = {wi: ctx.val("pe")}

    ctx.wait("sync", "st")
    ctx.wait("gpsimd", "st")
    ctx.wait("vector", "pe")
    ctx.wait("vector", "act")
    ctx.wait("vector", "st")
    ctx.wait("vector", "pool")
    ctx.op("vector", "memset", b["SF"][:, :], 0.0)
    ctx.op("vector", "memset", b["SB16"][:, :], 0.0, incs=(("dve", 1),))
    sb16_ready = [("dve", ctx.val("dve"))] * RH
    qv = qs.rearrange("(h c2 p) t -> p h c2 t", h=RH, c2=2)
    kv = ks.rearrange("(h c2 p) t -> p h c2 t", h=RH, c2=2)
    vv = vs.rearrange("(c p) e -> p c e", p=128)
    gv = gs.rearrange("(hc p) t -> p hc t", p=128)
    ogv = ogs.rearrange("(hc p) t -> p hc t", p=128)
    gC = [gammas[h] ** RC for h in range(RH)]
    r = {}

    def issue_loads(cc):
        if cc >= NRC:
            return
        sl = cc % 4
        if cc >= 4:
            ctx.wait_val("sync", "pe", r[("pe_end", cc - 4)])
            ctx.wait_val("gpsimd", "pe", r[("pe_end", cc - 4)])
            ctx.wait_val("gpsimd", "pool", r[("og_pool", cc - 4)])
        ctx.dma("sync", "xa",
                b["QT4"][sl][:, :].rearrange("p (h c2 t) -> p h c2 t", h=RH, c2=2),
                qv[:, :, :, RC * cc:RC * (cc + 1)])
        ctx.dma("sync", "xa",
                b["KT4"][sl][:, :].rearrange("p (h c2 t) -> p h c2 t", h=RH, c2=2),
                kv[:, :, :, RC * cc:RC * (cc + 1)])
        ctx.dma("gpsimd", "gb", b["VC4"][sl][:, :], vv[:, cc, :])
        ctx.dma("gpsimd", "gb",
                b["GT4"][sl][:, :].rearrange("p (hc t) -> p hc t", hc=16),
                gv[:, :, RC * cc:RC * (cc + 1)])
        r[("ld_xa", cc)] = ctx.val("xa")
        r[("ld_gb", cc)] = ctx.val("gb")

    issue_loads(0)
    issue_loads(1)
    for c in range(NRC):
        slot = c % 4
        QT, KT, VC, GT = (b["QT4"][slot], b["KT4"][slot],
                          b["VC4"][slot], b["GT4"][slot])
        issue_loads(c + 2)
        ctx.wait_val("tensor", "xa", r[("ld_xa", c)])
        ctx.wait_val("tensor", "gb", r[("ld_gb", c)])
        ctx.wait_val("tensor", "dve", r.get(("supd", c - 1, RH - 1), 0))
        for h in range(RH):
            for c2 in range(2):
                ctx.op("tensor", "matmul", pb[0][:, 128 * h:128 * (h + 1)],
                       KT[:, 128 * (2 * h + c2):128 * (2 * h + c2 + 1)],
                       QT[:, 128 * (2 * h + c2):128 * (2 * h + c2 + 1)],
                       start=(c2 == 0), stop=(c2 == 1),
                       incs=((("pe", 1),) if h == RH - 1 and c2 == 1 else ()))
        r[("pe_attn", c)] = ctx.val("pe")
        ctx.wait_val("tensor", "dve", r.get(("ktm_dve", c - 1), 0))
        for h in range(RH):
            for c2 in range(2):
                j = 2 * h + c2
                ctx.op("tensor", "transpose", ptr[:, 128 * j:128 * (j + 1)],
                       KT[:, 128 * j:128 * (j + 1)], b["ID"][:, :],
                       incs=((("pe", 1),) if j == 7 else ()))
        r[("pe_ktr", c)] = ctx.val("pe")
        ctx.wait_val("vector", "pe", r[("pe_attn", c)])
        ctx.op("vector", "tensor_mul", b["PA4"][:, :], pb[0][:, :], b["CM4"][:, :],
               incs=(("dve", 1),))
        r[("pa_dve", c)] = ctx.val("dve")
        ctx.wait_val("vector", "pe", r[("pe_ktr", c)])
        for h in range(RH):
            ctx.op("vector", "tensor_scalar", b["KTM4"][:, 256 * h:256 * (h + 1)],
                   ptr[:, 256 * h:256 * (h + 1)], gC[h], None, op0=ALU.mult,
                   incs=((("dve", 1),) if h == RH - 1 else ()))
        r[("ktm_dve", c)] = ctx.val("dve")
        ctx.wait_val("tensor", "act", r.get(("oo_act", c - 1), 0))
        ctx.wait_val("tensor", "dve", r[("pa_dve", c)])
        for h in range(RH):
            ctx.wait_val("tensor", *sb16_ready[h])
            for dvc in range(4):
                dsl = slice(RDV * h + 128 * dvc, RDV * h + 128 * (dvc + 1))
                for c2 in range(2):
                    ctx.op("tensor", "matmul",
                           pb[1 + h][:, 128 * dvc:128 * (dvc + 1)],
                           b["SB16"][:, RDV * (2 * h + c2) + 128 * dvc:
                                     RDV * (2 * h + c2) + 128 * (dvc + 1)],
                           QT[:, 128 * (2 * h + c2):128 * (2 * h + c2 + 1)],
                           start=(c2 == 0), stop=False)
                ctx.op("tensor", "matmul",
                       pb[1 + h][:, 128 * dvc:128 * (dvc + 1)],
                       VC[:, dsl], b["PA4"][:, 128 * h:128 * (h + 1)],
                       start=False, stop=True,
                       incs=((("pe", 1),) if h == RH - 1 and dvc == 3 else ()))
        r[("pe_o", c)] = ctx.val("pe")
        ctx.wait_val("tensor", "dve", r[("ktm_dve", c)])
        for h in range(RH):
            gd = r.get(("supd", c, h - 1), r.get(("supd", c - 1, RH - 1), 0))
            ctx.wait_val("tensor", "dve", gd)
            for c2 in range(2):
                ctx.op("tensor", "matmul", pb[6 if c2 == 0 else 0][:, :],
                       b["KTM4"][:, 128 * (2 * h + c2):128 * (2 * h + c2 + 1)],
                       VC[:, RDV * h:RDV * (h + 1)],
                       start=True, stop=True,
                       incs=((("pe", 1),) if c2 == 1 else ()))
            r[("pe_delta", c, h)] = ctx.val("pe")
            ctx.wait_val("vector", "pe", r[("pe_delta", c, h)])
            for c2 in range(2):
                ctx.op("vector", "scalar_tensor_tensor",
                       b["SF"][:, RDV * (2 * h + c2):RDV * (2 * h + c2 + 1)],
                       b["SF"][:, RDV * (2 * h + c2):RDV * (2 * h + c2 + 1)],
                       gC[h], pb[6 if c2 == 0 else 0][:, :], op0=ALU.mult, op1=ALU.add,
                       incs=((("dve", 1),) if c2 == 1 else ()))
            r[("supd", c, h)] = ctx.val("dve")
        r[("pe_end", c)] = ctx.val("pe")
        sb16_next = []
        ctx.wait_val("scalar", "pe", r[("pe_o", c)])
        for h in range(RH):
            ctx.wait_val("scalar", "dve", r[("supd", c, h)])
            ctx.op("scalar", "activation",
                   b["SB16"][:, RDV * 2 * h:RDV * 2 * (h + 1)],
                   b["SF"][:, RDV * 2 * h:RDV * 2 * (h + 1)], AF.Copy,
                   incs=(("act", 1),))
            sb16_next.append(("act", ctx.val("act")))
        oo = b["OO4"][c % 2]
        sq = b["SQ4"][c % 2]
        ctx.wait_val("scalar", "dve", r.get(("sq_dve", c - 2), 0))
        ctx.wait_val("scalar", "pool", r.get(("og_pool", c - 2), 0))
        for h in range(RH):
            ctx.op("scalar", "activation", oo[:, RDV * h:RDV * (h + 1)],
                   pb[1 + h][:, :], AF.Copy, incs=(("act", 1),))
        r[("oo_act", c)] = ctx.val("act")
        ctx.wait_val("vector", "act", r[("oo_act", c)])
        ctx.wait_val("vector", "pe", r.get(("stats_pe", c - 2), 0))
        ctx.op("vector", "tensor_mul", sq[:, :], oo[:, :], oo[:, :],
               incs=(("dve", 1),))
        r[("sq_dve", c)] = ctx.val("dve")
        ctx.wait_val("tensor", "dve", r[("sq_dve", c)])
        ctx.wait_val("tensor", "act", r.get(("rstd_act", c - 1), 0))
        for h in range(RH):
            for dvc in range(4):
                ctx.op("tensor", "matmul", pb[5][0:1, 128 * h:128 * (h + 1)],
                       ones_b[:, 0:1],
                       sq[:, 128 * (4 * h + dvc):128 * (4 * h + dvc + 1)],
                       start=(dvc == 0), stop=(dvc == 3),
                       incs=((("pe", 1),) if h == RH - 1 and dvc == 3 else ()))
        r[("stats_pe", c)] = ctx.val("pe")
        ctx.wait_val("scalar", "pe", r[("stats_pe", c)])
        ctx.op("scalar", "activation", b["SRW"][0:1, :], pb[5][0:1, 0:512], AF.Ln,
               scale=1.0 / RDV, bias=b["EPS"][0:1, 0:1], incs=(("act", 1),))
        ctx.wait("scalar", "act")
        ctx.op("scalar", "activation", b["SRW"][0:1, :], b["SRW"][0:1, :], AF.Exp,
               scale=-0.5, incs=(("act", 1),))
        r[("exp_act", c)] = ctx.val("act")
        ctx.wait_val("tensor", "act", r[("exp_act", c)])
        for h in range(RH):
            ctx.op("tensor", "matmul", pb[5][:, 128 * h:128 * (h + 1)],
                   ones_f[0:1, :], b["SRW"][0:1, 128 * h:128 * (h + 1)],
                   start=True, stop=True,
                   incs=((("pe", 1),) if h == RH - 1 else ()))
        r[("bcast_pe", c)] = ctx.val("pe")
        rstd = b["RSTD"][c % 2]
        ctx.wait_val("scalar", "pe", r[("bcast_pe", c)])
        ctx.wait_val("scalar", "pool", r.get(("og_pool", c - 2), 0))
        ctx.op("scalar", "activation", rstd[:, :], pb[5][:, 0:512], AF.Copy,
               incs=(("act", 1),))
        r[("rstd_act", c)] = ctx.val("act")
        sb16_ready = sb16_next
        og = b["OG4"][c % 2]
        ctx.wait_val("gpsimd", "act", r[("rstd_act", c)])
        ctx.wait_val("gpsimd", "st", r.get(("og_st", c - 2), 0))
        for hc in range(16):
            ctx.op("gpsimd", "tensor_mul", og[:, 128 * hc:128 * (hc + 1)],
                   oo[:, 128 * hc:128 * (hc + 1)],
                   rstd[:, 128 * (hc // 4):128 * (hc // 4 + 1)],
                   incs=(("pool", 1),))
        ctx.wait("gpsimd", "pool")
        ctx.op("gpsimd", "tensor_mul", og[:, :], og[:, :], GT[:, :],
               incs=(("pool", 1),))
        r[("og_pool", c)] = ctx.val("pool")
        ctx.wait_val("sync", "pool", r[("og_pool", c)])
        ctx.dma("sync", "st", ogv[:, :, RC * c:RC * (c + 1)],
                og[:, :].rearrange("p (hc t) -> p hc t", hc=16))
        r[("og_st", c)] = ctx.val("st")

    ctx.wait("sync", "st")
    ctx.wait("gpsimd", "pe")
    ctx.wait("gpsimd", "dve")
    ctx.wait("gpsimd", "act")
    ctx.wait("gpsimd", "st")
    wov = b["l0_woT"].rearrange("(c p) e -> p c e", p=128)
    for i in range(2):
        ctx.dma("gpsimd", "wb",
                b["WO"][:, :].rearrange("p (c e) -> p c e", c=DC)[:, 8 * i:8 * i + 8, :],
                wov[:, 8 * i:8 * i + 8, :])
    ctx.dma("sync", "xa", b["GNC"][:, :], b["gn_t"])
    ctx.wait("vector", "wb")
    ctx.wait("vector", "xa")
    for hc in range(DC):
        ctx.op("vector", "tensor_scalar", b["WO"][:, D * hc:D * (hc + 1)],
               b["WO"][:, D * hc:D * (hc + 1)], b["GNC"][:, hc % 4:hc % 4 + 1], None,
               op0=ALU.mult, incs=((("dve", 1),) if hc == DC - 1 else ()))
    pe_at_tb = {}
    dve_at_tb = {}
    xa_free = {}
    c0bank_h = Hist(4)
    for tb in range(NTB):
        if tb >= 2:
            ctx.wait_val("sync", "pe", pe_at_tb[tb - 2])
        if tb >= 1:
            ctx.wait_val("sync", "dve", xa_free[tb - 1])
        ctx.dma("sync", "xa", b["ZV"][:, :].rearrange("p (c t) -> p c t", c=DC),
                ogv[:, :, TB * tb:TB * (tb + 1)])
        ctx.dma("sync", "xb", b["XA"][:, :].rearrange("p (c t) -> p c t", c=DC),
                x0v[:, :, TB * tb:TB * (tb + 1)])
        ctx.wait("tensor", "wb")
        ctx.wait("tensor", "xa")
        if tb == 0:
            ctx.wait("tensor", "dve")
        for dc in range(DC):
            bank = pb[dc % 4]
            ctx.wait_val("tensor", "act", c0bank_h.guard())
            for kc in range(DC):
                ctx.op("tensor", "matmul", bank[:, :],
                       b["WO"][:, D * kc + 128 * dc: D * kc + 128 * (dc + 1)],
                       b["ZV"][:, TB * kc:TB * (kc + 1)],
                       start=(kc == 0), stop=(kc == DC - 1),
                       incs=((("pe", 1),) if kc == DC - 1 else ()))
            ctx.wait("scalar", "pe")
            ctx.wait("scalar", "dve")
            ctx.op("scalar", "activation", b["Y"][:, TB * dc:TB * (dc + 1)], bank[:, :],
                   AF.Copy, incs=(("act", 1),))
            c0bank_h.push(ctx.val("act"))
            emit_postln_prep(ctx, b, dc)
        pe_at_tb[tb] = ctx.val("pe")
        xa_free[tb] = ctx.val("dve")
        dve_at_tb[tb] = emit_postln(ctx, b, pb, ones_b, ones_c, ones_f,
                                    x1v[:, :, TB * tb:TB * (tb + 1)], banks=(5, 6))


def build_attn0_program():
    nc = bass.Bass("TRN2", target_bir_lowering=False)
    io = {}
    io["x0T"] = nc.dram_tensor("x0T", [D, T], F32, kind="ExternalInput").ap()
    io["l0_wqT"] = nc.dram_tensor("l0_wqT", [D, 1024], F32, kind="ExternalInput").ap()
    io["l0_wkT"] = nc.dram_tensor("l0_wkT", [D, 1024], F32, kind="ExternalInput").ap()
    io["l0_wvT"] = nc.dram_tensor("l0_wvT", [D, D], F32, kind="ExternalInput").ap()
    io["l0_wgT"] = nc.dram_tensor("l0_wgT", [D, D], F32, kind="ExternalInput").ap()
    io["l0_woT"] = nc.dram_tensor("l0_woT", [D, D], F32, kind="ExternalInput").ap()
    io["gn_t"] = nc.dram_tensor("gn_t", [128, 4], F32, kind="ExternalInput").ap()
    io["x1T"] = nc.dram_tensor("x1T", [D, T], F32, kind="ExternalOutput").ap()
    io["q_scr"] = nc.dram_tensor("q_scr", [1024, T], BF16, kind="Internal").ap()
    io["k_scr"] = nc.dram_tensor("k_scr", [1024, T], BF16, kind="Internal").ap()
    io["v_scr"] = nc.dram_tensor("v_scr", [T, D], BF16, kind="Internal").ap()
    io["g_scr"] = nc.dram_tensor("g_scr", [D, T], BF16, kind="Internal").ap()
    io["og_scr"] = nc.dram_tensor("og_scr", [D, T], BF16, kind="Internal").ap()

    sb = SbufAlloc(nc)
    ones_b = sb.tile("ones_b", [128, 1], BF16)
    ones_c = sb.tile("ones_c", [128, 1], F32)
    ones_f = sb.tile("ones_f", [1, 128], F32)
    bufs = plan_attn0(sb, io)
    pb = [nc.alloc_psum_tensor(f"pb{i}", [128, 512], F32) for i in range(7)]
    ptr = nc.alloc_psum_tensor("ptr", [128, 1024], BF16)
    sems = {k: nc.alloc_semaphore(name=f"s_{k}") for k in SEM_NAMES}

    def emit(ctx):
        ctx.op("vector", "memset", ones_b[:, :], 1.0)
        ctx.op("vector", "memset", ones_c[:, :], 1.0)
        ctx.op("vector", "memset", ones_f[:, :], 1.0, incs=(("dve", 1),))
        emit_attn0(ctx, bufs, pb, ptr, ones_b, ones_c, ones_f)
        ctx.wait("sync", "st")

    _run_block(nc, emit, sems)
    return nc


def host_prep_attn0(inputs):
    wqT = np.ascontiguousarray(inputs["L0_attn_Wq"].T)
    wkT = np.ascontiguousarray(inputs["L0_attn_Wk"].T)
    wvT = np.ascontiguousarray(inputs["L0_attn_Wv"].T)
    wgT = np.ascontiguousarray(inputs["L0_attn_Wg"].T)
    woT = np.ascontiguousarray(inputs["L0_attn_Wo"].T)
    gn_t = np.ascontiguousarray(inputs["L0_attn_gnorm_w"].reshape(4, 128).T)
    x = inputs["L0_attn_x"]
    maps = []
    for c in range(NCORES):
        maps.append({
            "x0T": np.ascontiguousarray(x[c].T),
            "l0_wqT": wqT, "l0_wkT": wkT, "l0_wvT": wvT,
            "l0_wgT": wgT, "l0_woT": woT, "gn_t": gn_t,
        })
    return maps



NI1 = 5632
NOC1 = 88
OSEG = 1024

def plan_ffn1(sb, io):
    b = dict(io)
    m = sb.mark()
    b["UI"] = sb.tile("p4_ui", [128, OSEG], I32)
    b["NL"] = sb.tile("p4_nl", [128, OSEG], I32)
    b["NH"] = sb.tile("p4_nh", [128, OSEG], I32)
    b["T2"] = sb.tile("p4_t2", [128, OSEG], I32)
    b["SG"] = sb.tile("p4_sg", [128, OSEG], I32)
    b["VP"] = sb.tile("p4_vp", [128, OSEG], F32)
    b["SCT"] = sb.tile("p4_sct", [128, OSEG], F32)
    b["OUTQ"] = [sb.tile(f"p4_outq{i}", [128, OSEG], BF16) for i in range(2)]
    b["XB"] = sb.tile("p4_xb", [128, DC * T], BF16)
    b["GW"] = [sb.tile(f"p4_gw{i}", [128, DC * 128], BF16) for i in range(2)]
    b["UW"] = [sb.tile(f"p4_uw{i}", [128, DC * 128], BF16) for i in range(2)]
    b["HG"] = [sb.tile(f"p4_hg{i}", [128, TB], BF16) for i in range(4)]
    b["HU"] = [sb.tile(f"p4_hu{i}", [128, TB], BF16) for i in range(4)]
    b["HH"] = [sb.tile(f"p4_hh{i}", [128, TB], BF16) for i in range(4)]
    sb.reset(m)
    b["HT"] = [sb.tile(f"p4_ht{i}", [128, 44 * TB], BF16) for i in range(2)]
    b["DW"] = [sb.tile(f"p4_dw{i}", [128, 44 * 128], BF16) for i in range(2)]
    b["XA"] = sb.tile("p4_xa", [128, DC * TB], F32)
    m_xn = sb.mark()
    b["XN"] = sb.tile("p4_xn", [128, DC * TB], F32)
    m_after = sb.mark()
    sb.reset(m_xn)
    b["YSQ"] = sb.tile("p4_ysq", [128, DC * TB], BF16)
    sb.reset(m_after)
    b["Y"] = sb.tile("p4_y", [128, DC * TB], BF16)
    b["ST"] = sb.tile("p4_st", [1, 5 * TB], F32)
    sb.reset(m)
    return b


def emit_dequant_seg(ctx, b, u8v, scv, ov, g, out_chunks,
                     ld_eng="gpsimd", ld_sem="ga", st_eng="sync", st_sem="st",
                     tiled=False):
    osl = slice(OSEG * g, OSEG * (g + 1))
    ctx.wait(ld_eng, "dve")
    ctx.dma(ld_eng, ld_sem, b["UI"][:, :], u8v[:, osl])
    ctx.dma(ld_eng, ld_sem, b["SCT"][:, :], scv[:, osl])
    ctx.wait("vector", ld_sem)
    ctx.wait("vector", st_sem)
    ctx.op("vector", "tensor_scalar", b["NL"][:, :], b["UI"][:, :], 15, None,
           op0=ALU.bitwise_and, incs=(("dve", 1),))
    ctx.op("vector", "tensor_scalar", b["NH"][:, :], b["UI"][:, :], 4, None,
           op0=ALU.logical_shift_right, incs=(("dve", 1),))
    for par, N in ((0, b["NL"]), (1, b["NH"])):
        ctx.wait("vector", "dve")
        ctx.op("vector", "tensor_scalar", b["T2"][:, :], N[:, :], 22, 0x01C00000,
               op0=ALU.logical_shift_left, op1=ALU.bitwise_and, incs=(("dve", 1),))
        ctx.op("vector", "tensor_scalar", b["SG"][:, :], N[:, :], 8, 28,
               op0=ALU.bitwise_and, op1=ALU.logical_shift_left, incs=(("dve", 1),))
        ctx.wait("vector", "dve")
        ctx.op("vector", "tensor_scalar", b["T2"][:, :], b["T2"][:, :],
               0x3F000000, None, op0=ALU.add, incs=(("dve", 1),))
        ctx.wait("vector", "dve")
        ctx.op("vector", "tensor_scalar", b["VP"][:, :], b["T2"][:, :].bitcast(F32),
               2.0, -1.0, op0=ALU.mult, op1=ALU.add, incs=(("dve", 1),))
        ctx.wait("vector", "dve")
        ctx.op("vector", "tensor_tensor", b["VP"][:, :], b["VP"][:, :],
               b["T2"][:, :].bitcast(F32), op=ALU.min, incs=(("dve", 1),))
        ctx.wait("vector", "dve")
        ctx.op("vector", "tensor_tensor", b["VP"][:, :].bitcast(I32),
               b["VP"][:, :].bitcast(I32), b["SG"][:, :], op=ALU.bitwise_or,
               incs=(("dve", 1),))
        ctx.wait("vector", "dve")
        ctx.op("vector", "tensor_mul", b["OUTQ"][par][:, :], b["VP"][:, :],
               b["SCT"][:, :], incs=(("dve", 1),))
        ctx.wait(st_eng, "dve")
        if tiled:
            ctx.dma(st_eng, st_sem, ov[:, out_chunks[par], 8 * g:8 * (g + 1), :],
                    b["OUTQ"][par][:, :].rearrange("p (hb j) -> p hb j", hb=8))
        else:
            ctx.dma(st_eng, st_sem, ov[:, out_chunks[par], osl],
                    b["OUTQ"][par][:, :])


def emit_d1_seg(ctx, bD, u8v_all, scv_all, ov, job, state, use_pool=False):
    OS = 512
    s, g = job
    k = state["i"]
    state["i"] += 1
    sl = k % 2
    osl = slice(OS * g, OS * (g + 1))
    u8v = u8v_all[:, s, :]
    scv = scv_all[:, s, :]
    out_chunks = (2 * s, 2 * s + 1)
    ctx.wait_val("gpsimd", "dve", state["dve"].get(k - 2, 0))
    ctx.dma("gpsimd", "dqa", bD["UI"][sl][:, :], u8v[:, osl])
    ctx.dma("gpsimd", "dqa", bD["SCT"][sl][:, :], scv[:, osl])
    UI, SCT = bD["UI"][sl], bD["SCT"][sl]
    if use_pool:
        eng, esem = "gpsimd", "pool"
        ctx.wait("gpsimd", "dqa")
    else:
        eng, esem = "vector", "dve"
        ctx.wait("vector", "dqa")
    ctx.op(eng, "tensor_scalar", bD["NL"][:, :], UI[:, :], 15, None,
           op0=ALU.bitwise_and, incs=((esem, 1),))
    ctx.op(eng, "tensor_scalar", bD["NH"][:, :], UI[:, :], 4, None,
           op0=ALU.logical_shift_right, incs=((esem, 1),))
    for par, N in ((0, bD["NL"]), (1, bD["NH"])):
        ctx.wait(eng, esem)
        ctx.op(eng, "tensor_scalar", bD["T2"][:, :], N[:, :], 22, 0x01C00000,
               op0=ALU.logical_shift_left, op1=ALU.bitwise_and, incs=((esem, 1),))
        ctx.op(eng, "tensor_scalar", bD["SG"][:, :], N[:, :], 8, 28,
               op0=ALU.bitwise_and, op1=ALU.logical_shift_left, incs=((esem, 1),))
        ctx.wait(eng, esem)
        ctx.op(eng, "tensor_scalar", bD["T2"][:, :], bD["T2"][:, :],
               0x3F000000, None, op0=ALU.add, incs=((esem, 1),))
        if use_pool:
            ctx.wait_val("gpsimd", "dve", ctx.val("dve"))
        ctx.wait(eng, esem)
        ctx.op(eng, "tensor_scalar", bD["VP"][:, :], bD["T2"][:, :].bitcast(F32),
               2.0, -1.0, op0=ALU.mult, op1=ALU.add, incs=((esem, 1),))
        ctx.wait(eng, esem)
        ctx.op(eng, "tensor_tensor", bD["VP"][:, :], bD["VP"][:, :],
               bD["T2"][:, :].bitcast(F32), op=ALU.min, incs=((esem, 1),))
        ctx.wait(eng, esem)
        ctx.op(eng, "tensor_tensor", bD["VP"][:, :].bitcast(I32),
               bD["VP"][:, :].bitcast(I32), bD["SG"][:, :], op=ALU.bitwise_or,
               incs=((esem, 1),))
        outq = bD["OUTQ"][par][sl]
        if use_pool:
            ctx.wait("vector", "pool")
            ctx.wait("vector", "dqa")
        else:
            ctx.wait("vector", "dve")
        ctx.wait_val("vector", "dqs", state["st"].get(k - 2, 0))
        ctx.op("vector", "tensor_mul", outq[:, :], bD["VP"][:, :], SCT[:, :],
               incs=(("dve", 1),))
        ctx.wait("sync", "dve")
        ctx.dma("sync", "dqs", ov[:, out_chunks[par], osl], outq[:, :])
    state["dve"][k] = ctx.val("dve")
    state["st"][k] = ctx.val("dqs")


def emit_dequant_strip(ctx, b, u8v, scv, ov, s, nseg, out_chunks, tiled=False):
    for g in range(nseg):
        emit_dequant_seg(ctx, b, u8v, scv, ov, g, out_chunks, tiled=tiled)


def emit_ffn1(ctx, b, pb, ones_b, ones_c, ones_f, x_in="x3T", x_out="yT",
              gate_dequant=True):
    x3T, yT = b[x_in], b[x_out]
    x3v = x3T.rearrange("(c p) t -> p c t", p=128)
    yv = yT.rearrange("(c p) t -> p c t", p=128)
    wgu, wdn, hscr = b["wgu_scr"], b["wdn_scr"], b["h_scr"]

    if gate_dequant:
        gupv = b["gupT"].rearrange("(s p) o -> p s o", p=128)
        gscv = b["gup_sr"].rearrange("(s p) o -> p s o", p=128)
        wguv = wgu.rearrange("(c p) o -> p c o", p=128)
        for s in range(8):
            emit_dequant_strip(ctx, b, gupv[:, s, :], gscv[:, s, :], wguv, s, 11,
                               (2 * s, 2 * s + 1))

    ctx.wait("sync", "st")
    ctx.wait("gpsimd", "pe")
    ctx.wait("gpsimd", "act")
    ctx.wait("gpsimd", "dve")
    ctx.wait("gpsimd", "st")
    xpv = x3T.rearrange("(s j w) t -> j s w t", j=128, w=2)
    XB = b["XB"]
    for i in range(4):
        for w in range(2):
            ctx.dma("gpsimd", "gb",
                    XB[:, :].rearrange("p (s w t) -> p s w t", s=8, w=2)[:, 2 * i:2 * i + 2, w, :],
                    xpv[:, 2 * i:2 * i + 2, w, :])
    gv = wgu.rearrange("(c p) o -> p c o", p=128)
    dnv = b["dnT"].rearrange("(s p) o -> p s o", p=128)
    dscv = b["dn_sr"].rearrange("(s p) o -> p s o", p=128)
    wdnv = b["wdn_t"].rearrange("p (hc c j) -> p c hc j", hc=16, c=44)
    pe_at_chunk = {}
    dn_strip = 0
    ebank_h = Hist(2)
    ehh_h = Hist(4)
    est_h = Hist(4)
    for ic in range(44):
        slot = ic % 2
        wsem = "wa" if slot == 0 else "wb"
        if ic >= 2:
            ctx.wait_val("gpsimd", "pe", pe_at_chunk[ic - 2])
        ctx.dma("gpsimd", wsem,
                b["GW"][slot][:, :].rearrange("p (c o) -> p c o", c=DC),
                gv[:, :, 128 * ic:128 * (ic + 1)])
        ctx.dma("gpsimd", wsem,
                b["UW"][slot][:, :].rearrange("p (c o) -> p c o", c=DC),
                gv[:, :, 128 * (ic + 44):128 * (ic + 45)])
        if ic % 2 == 0 and dn_strip < 22:
            emit_dequant_strip(ctx, b, dnv[:, dn_strip, :], dscv[:, dn_strip, :],
                               wdnv, dn_strip, 2, (2 * dn_strip, 2 * dn_strip + 1),
                               tiled=True)
            dn_strip += 1
        ctx.wait("tensor", wsem)
        if ic == 0:
            ctx.wait("tensor", "gb")
        for tb in range(NTB):
            bg = pb[2 * (tb % 2)]
            bu = pb[2 * (tb % 2) + 1]
            ctx.wait_val("tensor", "act", ebank_h.guard())
            for kc in range(DC):
                ctx.op("tensor", "matmul", bg[:, :],
                       b["GW"][slot][:, 128 * kc:128 * (kc + 1)],
                       XB[:, T * kc + TB * tb: T * kc + TB * (tb + 1)],
                       start=(kc == 0), stop=(kc == DC - 1))
            for kc in range(DC):
                ctx.op("tensor", "matmul", bu[:, :],
                       b["UW"][slot][:, 128 * kc:128 * (kc + 1)],
                       XB[:, T * kc + TB * tb: T * kc + TB * (tb + 1)],
                       start=(kc == 0), stop=(kc == DC - 1),
                       incs=((("pe", 1),) if kc == DC - 1 else ()))
            hg = b["HG"][tb % 4]
            hu = b["HU"][tb % 4]
            ctx.wait("scalar", "pe")
            ctx.wait_val("scalar", "dve", ehh_h.guard())
            ctx.op("scalar", "activation", hg[:, :], bg[:, :], AF.Silu,
                   incs=(("act", 1),))
            ctx.op("scalar", "activation", hu[:, :], bu[:, :], AF.Copy,
                   incs=(("act", 1),))
            ebank_h.push(ctx.val("act"))
            hh = b["HH"][tb % 4]
            ctx.wait("vector", "act")
            ctx.wait_val("vector", "st", est_h.guard())
            ctx.op("vector", "tensor_mul", hh[:, :], hg[:, :], hu[:, :],
                   incs=(("dve", 1),))
            ehh_h.push(ctx.val("dve"))
            ctx.wait("sync", "dve")
            ctx.dma("sync", "st",
                    hscr.rearrange("(c p) t -> p c t", p=128)[:, ic, TB * tb:TB * (tb + 1)],
                    hh[:, :])
            est_h.push(ctx.val("st"))
        pe_at_chunk[ic] = ctx.val("pe")

    dv = b["wdn_t"].rearrange("p (hc c j) -> p hc c j", hc=16, c=44)
    hpv = hscr.rearrange("(s j w) t -> j s w t", j=128, w=2)
    inv_d = 1.0 / D
    ctx.wait("sync", "st")
    pe_at_tb = {}
    pe_at_dw = {}
    dve_at_tb = {}
    xa_free = {}
    fbank_h = Hist(4)
    ctx.wait("gpsimd", "dve")
    ctx.wait("gpsimd", "st")
    ctx.wait("gpsimd", "act")
    for tb in range(NTB):
        hts = b["HT"][tb % 2]
        if tb >= 2:
            ctx.wait_val("sync", "pe", pe_at_tb[tb - 2])
        if tb >= 1:
            ctx.wait_val("sync", "dve", xa_free[tb - 1])
        for w in range(2):
            ctx.dma("sync", "xa",
                    hts[:, :].rearrange("p (s w t) -> p s w t", s=22, w=2)[:, :, w, :],
                    hpv[:, :, w, TB * tb:TB * (tb + 1)])
        ctx.dma("sync", "xb",
                b["XA"][:, :].rearrange("p (c t) -> p c t", c=DC),
                x3v[:, :, TB * tb:TB * (tb + 1)])
        for dc in range(DC):
            slot = dc % 2
            wsem = "wa" if slot == 0 else "wb"
            key = (tb, dc)
            prev = (tb, dc - 2) if dc >= 2 else ((tb - 1, DC - 2 + dc) if tb >= 1 else None)
            if prev is not None:
                ctx.wait_val("gpsimd", "pe", pe_at_dw[prev])
            else:
                ctx.wait_val("gpsimd", "pe", pe_at_chunk[43])
            ctx.dma("gpsimd", wsem,
                    b["DW"][slot][:, :].rearrange("p (c o) -> p c o", c=44),
                    dv[:, dc, :, :])
            bankd = pb[(4, 5, 0, 1)[dc % 4]]
            ctx.wait("tensor", wsem)
            if dc == 0:
                ctx.wait("tensor", "xa")
            ctx.wait_val("tensor", "act", fbank_h.guard())
            for kc in range(44):
                ctx.op("tensor", "matmul", bankd[:, :],
                       b["DW"][slot][:, 128 * kc:128 * (kc + 1)],
                       hts[:, TB * kc:TB * (kc + 1)],
                       start=(kc == 0), stop=(kc == 43),
                       incs=((("pe", 1),) if kc == 43 else ()))
            pe_at_dw[key] = ctx.val("pe")
            ctx.wait("scalar", "pe")
            ctx.wait("scalar", "dve")
            ctx.op("scalar", "activation", b["Y"][:, TB * dc:TB * (dc + 1)], bankd[:, :],
                   AF.Copy, incs=(("act", 1),))
            fbank_h.push(ctx.val("act"))
            ctx.wait("vector", "act")
            if dc == 0:
                ctx.wait("vector", "st")
            ctx.op("vector", "tensor_mul", b["YSQ"][:, TB * dc:TB * (dc + 1)],
                   b["Y"][:, TB * dc:TB * (dc + 1)], b["Y"][:, TB * dc:TB * (dc + 1)],
                   incs=(("dve", 1),))
        pe_at_tb[tb] = ctx.val("pe")
        srm = pb[6]
        ctx.wait("tensor", "act")
        ctx.wait("tensor", "dve")
        for dc in range(DC):
            ctx.op("tensor", "matmul", srm[0:1, 0:TB], ones_b[:, 0:1],
                   b["YSQ"][:, TB * dc:TB * (dc + 1)], start=(dc == 0), stop=(dc == DC - 1),
                   incs=((("pe", 1),) if dc == DC - 1 else ()))
        ST = b["ST"]
        erow = ST[0:1, 0:TB]
        rrow = ST[0:1, TB:2 * TB]
        ctx.wait("scalar", "pe")
        ctx.op("scalar", "activation", erow, srm[0:1, 0:TB], AF.Copy, scale=inv_d,
               incs=(("act", 1),))
        ctx.wait("vector", "act")
        ctx.op("vector", "tensor_scalar", rrow, erow, 1e-6, None, op0=ALU.add,
               incs=(("dve", 1),))
        ctx.wait("scalar", "dve")
        ctx.op("scalar", "activation", rrow, rrow, AF.Sqrt, incs=(("act", 1),))
        ctx.wait("vector", "act")
        ctx.op("vector", "reciprocal", rrow, rrow, incs=(("dve", 1),))
        ctx.wait("tensor", "dve")
        ctx.op("tensor", "matmul", srm[:, 0:TB], ones_f[0:1, :], rrow,
               start=True, stop=True, incs=(("pe", 1),))
        ctx.wait("vector", "pe")
        for dc in range(DC):
            ctx.op("vector", "tensor_mul", b["Y"][:, TB * dc:TB * (dc + 1)],
                   b["Y"][:, TB * dc:TB * (dc + 1)], srm[:, 0:TB],
                   incs=(("dve", 1),))
            emit_postln_prep(ctx, b, dc)
        xa_free[tb] = ctx.val("dve")
        dve_at_tb[tb] = emit_postln(ctx, b, pb, ones_b, ones_c, ones_f,
                                    yv[:, :, TB * tb:TB * (tb + 1)], banks=(5, 6))


def build_ffn1_program():
    nc = bass.Bass("TRN2", target_bir_lowering=False)
    io = {}
    io["x3T"] = nc.dram_tensor("x3T", [D, T], F32, kind="ExternalInput").ap()
    io["gupT"] = nc.dram_tensor("gupT", [1024, 11264], U8, kind="ExternalInput").ap()
    io["gup_sr"] = nc.dram_tensor("gup_sr", [1024, 11264], F16, kind="ExternalInput").ap()
    io["dnT"] = nc.dram_tensor("dnT", [2816, 2048], U8, kind="ExternalInput").ap()
    io["dn_sr"] = nc.dram_tensor("dn_sr", [2816, 2048], F16, kind="ExternalInput").ap()
    io["yT"] = nc.dram_tensor("yT", [D, T], F32, kind="ExternalOutput").ap()
    io["wgu_scr"] = nc.dram_tensor("wgu_scr", [D, 11264], BF16, kind="Internal").ap()
    io["wdn_scr"] = nc.dram_tensor("wdn_scr", [NI1, D], BF16, kind="Internal").ap()
    io["h_scr"] = nc.dram_tensor("h_scr", [NI1, T], BF16, kind="Internal").ap()

    sb = SbufAlloc(nc)
    ones_b = sb.tile("ones_b", [128, 1], BF16)
    ones_c = sb.tile("ones_c", [128, 1], F32)
    ones_f = sb.tile("ones_f", [1, 128], F32)
    bufs = plan_ffn1(sb, io)
    pb = [nc.alloc_psum_tensor(f"pb{i}", [128, 512], F32) for i in range(8)]
    sems = {k: nc.alloc_semaphore(name=f"s_{k}") for k in SEM_NAMES}

    def emit(ctx):
        ctx.op("vector", "memset", ones_b[:, :], 1.0)
        ctx.op("vector", "memset", ones_c[:, :], 1.0)
        ctx.op("vector", "memset", ones_f[:, :], 1.0, incs=(("dve", 1),))
        emit_ffn1(ctx, bufs, pb, ones_b, ones_c, ones_f)
        ctx.wait("sync", "st")

    _run_block(nc, emit, sems)
    return nc


def host_prep_ffn1(inputs, x3_full):
    gupT = np.ascontiguousarray(
        inputs["L1_ffn_gate_up_packed"].view(np.uint8).reshape(11264, 1024).T)
    gup_sr = np.ascontiguousarray(
        np.repeat(inputs["L1_ffn_gate_up_scales"], 64, axis=0))
    dnT = np.ascontiguousarray(
        inputs["L1_ffn_down_packed"].view(np.uint8).reshape(2048, 2816).T)
    dn_sr = np.ascontiguousarray(
        np.repeat(inputs["L1_ffn_down_scales"], 64, axis=0))
    maps = []
    for c in range(NCORES):
        maps.append({
            "x3T": np.ascontiguousarray(x3_full[c].T),
            "gupT": gupT, "gup_sr": gup_sr, "dnT": dnT, "dn_sr": dn_sr,
        })
    return maps


# version 34
# speedup vs baseline: 1.0611x; 1.0611x over previous
import math
import numpy as np
import ml_dtypes
import concourse.bass as bass
import concourse.mybir as mybir
from concourse.bass_utils import run_bass_kernel_spmd

F32 = mybir.dt.float32
BF16 = mybir.dt.bfloat16
I32 = mybir.dt.int32
U8 = mybir.dt.uint8
F16 = mybir.dt.float16
AF = mybir.ActivationFunctionType
ALU = mybir.AluOpType

NCORES = 8
T = 4096
D = 2048
TB = 512
NTB = T // TB
DC = D // 128

SEM_NAMES = (
    "wa", "wb",
    "xa", "xb",
    "ga", "gb",
    "st",
    "pe",
    "act",
    "dve",
    "pool",
    "dqa", "dqs",
)


class Ctx:

    def __init__(self, nc, which, eng, sems):
        self.nc = nc
        self.which = which
        self.eng = eng
        self.sems = sems
        self.cnt = {k: 0 for k in sems}
        self.last_wait = {k: 0 for k in sems}
        self.acked = {k: 0 for k in sems}

    def op(self, engine, meth, *a, incs=(), **kw):
        if self.which == engine:
            inst = getattr(self.eng, meth)(*a, **kw)
            for s, n in incs:
                inst.then_inc(self.sems[s], n)
        for s, n in incs:
            self.cnt[s] += n

    def wait(self, engine, *sem_names):
        for s in sem_names:
            if self.cnt[s] > 0:
                if self.which == engine:
                    self.eng.wait_ge(self.sems[s], self.cnt[s])
                self.last_wait[s] = max(self.last_wait[s], self.cnt[s])

    def val(self, s):
        return self.cnt[s]

    def wait_val(self, engine, s, v):
        if v > 0:
            if self.which == engine:
                self.eng.wait_ge(self.sems[s], v)
            self.last_wait[s] = max(self.last_wait[s], v)

    def dma(self, engine, sem, out, in_):
        if self.last_wait[sem] > self.acked[sem]:
            v = self.last_wait[sem]
            if self.which == engine:
                self.eng.wait_ge(self.sems[sem], v)
            self.acked[sem] = v
        self.op(engine, "dma_start", out, in_, incs=((sem, 16),))


class Hist:

    def __init__(self, depth):
        self.depth = depth
        self.vals = []

    def push(self, v):
        self.vals.append(v)

    def guard(self):
        return self.vals[-self.depth] if len(self.vals) >= self.depth else 0


class SbufAlloc:

    def __init__(self, nc):
        self.nc = nc
        self.cur = (nc.sbuf_base + 31) & ~31
        self.limit = nc.sbuf_top - 1024
        self.uid = 0
        self.hwm = 0

    def tile(self, name, shape, dtype):
        self.uid += 1
        h = self.nc.alloc_sbuf_tensor_at(f"{name}_{self.uid}", list(shape), dtype,
                                         offset=self.cur)
        per = int(np.prod(shape[1:])) * mybir.dt.size(dtype)
        self.cur = (self.cur + per + 31) & ~31
        self.hwm = max(self.hwm, self.cur)
        assert self.cur <= self.limit, f"SBUF overflow at {name}: {self.cur}"
        return h

    def mark(self):
        return self.cur

    def reset(self, mark):
        self.cur = mark



def emit_postln_prep(ctx, b, dc):
    sl = slice(TB * dc, TB * (dc + 1))
    ctx.wait("vector", "act")
    if dc == 0:
        ctx.wait("vector", "xb")
        ctx.wait("vector", "st")
    ctx.op("vector", "tensor_add", b["XN"][:, sl], b["XA"][:, sl], b["Y"][:, sl])
    ctx.op("vector", "tensor_mul", b["Y"][:, sl], b["XN"][:, sl], b["XN"][:, sl],
           incs=(("dve", 1),))


def emit_postln(ctx, b, pb, ones_b, ones_c, ones_f, x_out_view, banks=(6, 7)):
    inv_d = 1.0 / D
    s1, s2 = pb[banks[0]], pb[banks[1]]
    ctx.wait("tensor", "act")
    ctx.wait("tensor", "dve")
    for dc in range(DC):
        ctx.op("tensor", "matmul", s1[0:1, 0:TB], ones_c[:, 0:1],
               b["XN"][:, TB * dc:TB * (dc + 1)], start=(dc == 0), stop=(dc == DC - 1))
    for dc in range(DC):
        ctx.op("tensor", "matmul", s2[0:1, 0:TB], ones_b[:, 0:1],
               b["Y"][:, TB * dc:TB * (dc + 1)], start=(dc == 0), stop=(dc == DC - 1),
               incs=((("pe", 1),) if dc == DC - 1 else ()))
    ST = b["ST"]
    mrow = ST[0:1, 0:TB]
    qrow = ST[0:1, TB:2 * TB]
    msq = ST[0:1, 2 * TB:3 * TB]
    var = ST[0:1, 3 * TB:4 * TB]
    rstd = ST[0:1, 4 * TB:5 * TB]
    ctx.wait("scalar", "pe")
    ctx.op("scalar", "activation", mrow, s1[0:1, 0:TB], AF.Copy, scale=inv_d,
           incs=(("act", 1),))
    ctx.op("scalar", "activation", qrow, s2[0:1, 0:TB], AF.Copy, scale=inv_d,
           incs=(("act", 1),))
    ctx.wait("vector", "act")
    ctx.op("vector", "tensor_mul", msq, mrow, mrow, incs=(("dve", 1),))
    ctx.wait("vector", "dve")
    ctx.op("vector", "tensor_sub", var, qrow, msq, incs=(("dve", 1),))
    ctx.wait("vector", "dve")
    ctx.op("vector", "tensor_scalar", var, var, 1e-5, None, op0=ALU.add,
           incs=(("dve", 1),))
    ctx.wait("scalar", "dve")
    ctx.op("scalar", "activation", rstd, var, AF.Sqrt, incs=(("act", 1),))
    ctx.wait("vector", "act")
    ctx.op("vector", "reciprocal", rstd, rstd, incs=(("dve", 1),))
    bm, br = pb[banks[0]], pb[banks[1]]
    ctx.wait("tensor", "dve")
    ctx.op("tensor", "matmul", bm[:, 0:TB], ones_f[0:1, :], mrow,
           start=True, stop=True)
    ctx.op("tensor", "matmul", br[:, 0:TB], ones_f[0:1, :], rstd,
           start=True, stop=True, incs=(("pe", 1),))
    ctx.wait("vector", "pe")
    for dc in range(DC):
        ctx.op("vector", "scalar_tensor_tensor",
               b["XN"][:, TB * dc:TB * (dc + 1)],
               b["XN"][:, TB * dc:TB * (dc + 1)], 1.0, bm[:, 0:TB],
               op0=ALU.mult, op1=ALU.subtract,
               incs=((("dve", 1),) if dc == DC - 1 else ()))
    ctx.wait("vector", "dve")
    for dc in range(DC):
        ctx.op("vector", "tensor_mul",
               b["XN"][:, TB * dc:TB * (dc + 1)],
               b["XN"][:, TB * dc:TB * (dc + 1)], br[:, 0:TB],
               incs=((("dve", 1),) if dc == DC - 1 else ()))
    out_dve = ctx.val("dve")
    ctx.wait_val("scalar", "dve", out_dve)
    ctx.dma("scalar", "st", x_out_view,
           b["XN"][:, :].rearrange("p (c t) -> p c t", c=DC))
    return out_dve



def plan_ffn0(sb, io):
    b = dict(io)
    b["GSC"] = sb.tile("p2_gsc", [128, 44], F32)
    b["USC"] = sb.tile("p2_usc", [128, 44], F32)
    b["DSC"] = sb.tile("p2_dsc", [128, 16], F32)
    m = sb.mark()
    b["XB"] = sb.tile("p2_xb", [128, DC * T], BF16)
    b["GW"] = [sb.tile(f"p2_gw{i}", [128, DC * 128], BF16) for i in range(2)]
    b["UW"] = [sb.tile(f"p2_uw{i}", [128, DC * 128], BF16) for i in range(2)]
    b["HG"] = [sb.tile(f"p2_hg{i}", [128, TB], BF16) for i in range(2)]
    b["HU"] = [sb.tile(f"p2_hu{i}", [128, TB], BF16) for i in range(2)]
    b["HH"] = [sb.tile(f"p2_hh{i}", [128, TB], BF16) for i in range(2)]
    sb.reset(m)
    b["HT"] = [sb.tile(f"p2_ht{i}", [128, 44 * TB], BF16) for i in range(2)]
    b["DW"] = [sb.tile(f"p2_dw{i}", [128, 44 * 128], BF16) for i in range(2)]
    b["XA"] = sb.tile("p2_xa", [128, DC * TB], F32)
    b["XN"] = sb.tile("p2_xn", [128, DC * TB], F32)
    b["Y"] = sb.tile("p2_y", [128, DC * TB], BF16)
    b["ST"] = sb.tile("p2_st", [1, 5 * TB], F32)
    sb.reset(m)
    return b


def emit_ffn0(ctx, b, pb, ones_b, ones_c, ones_f,
              x_in="x1T", x_out="x2T", ln_banks=(6, 7)):
    x1T, x2T = b[x_in], b[x_out]
    gwT, uwT, dwT, hscr = b["gwT"], b["uwT"], b["dwT"], b["h_scr"]
    NI = 44

    ctx.dma("sync", "xa", b["GSC"][:, :], b["gs_t"])
    ctx.dma("sync", "xa", b["USC"][:, :], b["us_t"])
    ctx.dma("sync", "xa", b["DSC"][:, :], b["ds_t"])
    XB = b["XB"]
    xv = x1T.rearrange("(c p) t -> p c t", p=128)
    for i in range(4):
        ctx.dma("gpsimd", "ga",
               XB[:, :].rearrange("p (c t) -> p c t", c=DC)[:, 4 * i:4 * i + 4, :],
               xv[:, 4 * i:4 * i + 4, :])

    gv = gwT.rearrange("(c p) o -> p c o", p=128)
    uv = uwT.rearrange("(c p) o -> p c o", p=128)
    pe_at_chunk = {}
    ebank_h = Hist(2)
    ehh_h = Hist(2)
    est_h = Hist(2)
    for ic in range(NI):
        slot = ic % 2
        wsem = "wa" if slot == 0 else "wb"
        if ic >= 2:
            ctx.wait_val("gpsimd", "pe", pe_at_chunk[ic - 2])
        ctx.dma("gpsimd", wsem,
               b["GW"][slot][:, :].rearrange("p (c o) -> p c o", c=DC),
               gv[:, :, 128 * ic:128 * (ic + 1)])
        ctx.dma("gpsimd", wsem,
               b["UW"][slot][:, :].rearrange("p (c o) -> p c o", c=DC),
               uv[:, :, 128 * ic:128 * (ic + 1)])
        if ic < 16:
            dwtv = dwT.rearrange("(c p) (dc j) -> p dc c j", p=128, j=128)
            ctx.dma("gpsimd", "gb",
                    b["dw_bf"].rearrange("p (dc c j) -> p dc c j", dc=16, c=44)[:, ic, :, :],
                    dwtv[:, ic, :, :])
        ctx.wait("tensor", wsem)
        if ic == 0:
            ctx.wait("tensor", "ga")
        for tb in range(NTB):
            bg = pb[2 * (tb % 2)]
            bu = pb[2 * (tb % 2) + 1]
            ctx.wait_val("tensor", "act", ebank_h.guard())
            for kc in range(DC):
                ctx.op("tensor", "matmul", bg[:, :],
                       b["GW"][slot][:, 128 * kc:128 * (kc + 1)],
                       XB[:, T * kc + TB * tb: T * kc + TB * (tb + 1)],
                       start=(kc == 0), stop=(kc == DC - 1))
            for kc in range(DC):
                ctx.op("tensor", "matmul", bu[:, :],
                       b["UW"][slot][:, 128 * kc:128 * (kc + 1)],
                       XB[:, T * kc + TB * tb: T * kc + TB * (tb + 1)],
                       start=(kc == 0), stop=(kc == DC - 1),
                       incs=((("pe", 1),) if kc == DC - 1 else ()))
            hg = b["HG"][tb % 2]
            hu = b["HU"][tb % 2]
            ctx.wait("scalar", "pe")
            ctx.wait_val("scalar", "dve", ehh_h.guard())
            if ic == 0 and tb == 0:
                ctx.wait("scalar", "xa")
            ctx.op("scalar", "activation", hg[:, :], bg[:, :], AF.Silu,
                   scale=b["GSC"][:, ic:ic + 1], incs=(("act", 1),))
            ctx.op("scalar", "activation", hu[:, :], bu[:, :], AF.Copy,
                   scale=b["USC"][:, ic:ic + 1], incs=(("act", 1),))
            ebank_h.push(ctx.val("act"))
            hh = b["HH"][tb % 2]
            ctx.wait("vector", "act")
            ctx.wait_val("vector", "st", est_h.guard())
            ctx.op("vector", "tensor_mul", hh[:, :], hg[:, :], hu[:, :],
                   incs=(("dve", 1),))
            ehh_h.push(ctx.val("dve"))
            ctx.wait("sync", "dve")
            ctx.dma("sync", "st",
                   hscr.rearrange("(c p) t -> p c t", p=128)[:, ic, TB * tb:TB * (tb + 1)],
                   hh[:, :])
            est_h.push(ctx.val("st"))
        pe_at_chunk[ic] = ctx.val("pe")

    dv = b["dw_bf"].rearrange("p (dc c j) -> p dc c j", dc=16, c=44)
    hv = hscr.rearrange("(c p) t -> p c t", p=128)
    x1v = x1T.rearrange("(c p) t -> p c t", p=128)
    x2v = x2T.rearrange("(c p) t -> p c t", p=128)
    inv_d = 1.0 / D
    ctx.wait("sync", "st")
    ctx.wait("gpsimd", "gb")
    ctx.wait("scalar", "gb")
    pe_at_tb = {}
    pe_at_dw = {}
    dve_at_tb = {}
    xa_free = {}
    dbank_h = Hist(4)
    for tb in range(NTB):
        hts = b["HT"][tb % 2]
        if tb >= 2:
            ctx.wait_val("sync", "pe", pe_at_tb[tb - 2])
        if tb >= 1:
            ctx.wait_val("sync", "dve", xa_free[tb - 1])
        ctx.dma("sync", "xa",
               hts[:, :].rearrange("p (c t) -> p c t", c=44),
               hv[:, :, TB * tb:TB * (tb + 1)])
        ctx.dma("sync", "xb",
               b["XA"][:, :].rearrange("p (c t) -> p c t", c=DC),
               x1v[:, :, TB * tb:TB * (tb + 1)])
        for dc in range(DC):
            slot = dc % 2
            wsem = "wa" if slot == 0 else "wb"
            key = (tb, dc)
            prev = (tb, dc - 2) if dc >= 2 else ((tb - 1, DC - 2 + dc) if tb >= 1 else None)
            if prev is not None:
                ctx.wait_val("gpsimd", "pe", pe_at_dw[prev])
            else:
                ctx.wait_val("gpsimd", "pe", pe_at_chunk[NI - 1])
            ctx.dma("gpsimd", wsem,
                    b["DW"][slot][:, :].rearrange("p (c o) -> p c o", c=44),
                    dv[:, dc, :, :])
            bankd = pb[(4, 5, 0, 1)[dc % 4]]
            ctx.wait("tensor", wsem)
            if dc == 0:
                ctx.wait("tensor", "xa")
            ctx.wait_val("tensor", "act", dbank_h.guard())
            for kc in range(44):
                ctx.op("tensor", "matmul", bankd[:, :],
                       b["DW"][slot][:, 128 * kc:128 * (kc + 1)],
                       hts[:, TB * kc:TB * (kc + 1)],
                       start=(kc == 0), stop=(kc == 43),
                       incs=((("pe", 1),) if kc == 43 else ()))
            pe_at_dw[key] = ctx.val("pe")
            ctx.wait("scalar", "pe")
            ctx.wait("scalar", "dve")
            ctx.op("scalar", "activation", b["Y"][:, TB * dc:TB * (dc + 1)], bankd[:, :],
                   AF.Copy, scale=b["DSC"][:, dc:dc + 1], incs=(("act", 1),))
            dbank_h.push(ctx.val("act"))
            emit_postln_prep(ctx, b, dc)
        pe_at_tb[tb] = ctx.val("pe")
        xa_free[tb] = ctx.val("dve")
        dve_at_tb[tb] = emit_postln(ctx, b, pb, ones_b, ones_c, ones_f,
                                    x2v[:, :, TB * tb:TB * (tb + 1)],
                                    banks=ln_banks)



def _run_block(nc, emit, sems):
    with nc.Block() as block:
        @block.sync
        def _(eng):
            emit(Ctx(nc, "sync", eng, sems))

        @block.tensor
        def _(eng):
            emit(Ctx(nc, "tensor", eng, sems))

        @block.scalar
        def _(eng):
            emit(Ctx(nc, "scalar", eng, sems))

        @block.vector
        def _(eng):
            emit(Ctx(nc, "vector", eng, sems))

        @block.gpsimd
        def _(eng):
            emit(Ctx(nc, "gpsimd", eng, sems))


def build_ffn0_program():
    nc = bass.Bass("TRN2", target_bir_lowering=False)
    io = {}
    io["x1T"] = nc.dram_tensor("x1T", [D, T], F32, kind="ExternalInput").ap()
    io["gwT"] = nc.dram_tensor("gwT", [D, 5632], F32, kind="ExternalInput").ap()
    io["uwT"] = nc.dram_tensor("uwT", [D, 5632], F32, kind="ExternalInput").ap()
    io["dwT"] = nc.dram_tensor("dwT", [5632, D], F32, kind="ExternalInput").ap()
    io["gs_t"] = nc.dram_tensor("gs_t", [128, 44], F32, kind="ExternalInput").ap()
    io["us_t"] = nc.dram_tensor("us_t", [128, 44], F32, kind="ExternalInput").ap()
    io["ds_t"] = nc.dram_tensor("ds_t", [128, 16], F32, kind="ExternalInput").ap()
    io["x2T"] = nc.dram_tensor("x2T", [D, T], F32, kind="ExternalOutput").ap()
    io["h_scr"] = nc.dram_tensor("h_scr", [5632, T], BF16, kind="Internal").ap()

    sb = SbufAlloc(nc)
    ones_b = sb.tile("ones_b", [128, 1], BF16)
    ones_c = sb.tile("ones_c", [128, 1], F32)
    ones_f = sb.tile("ones_f", [1, 128], F32)
    bufs = plan_ffn0(sb, io)
    pb = [nc.alloc_psum_tensor(f"pb{i}", [128, 512], F32) for i in range(8)]
    sems = {k: nc.alloc_semaphore(name=f"s_{k}") for k in SEM_NAMES}

    def emit(ctx):
        ctx.op("vector", "memset", ones_b[:, :], 1.0)
        ctx.op("vector", "memset", ones_c[:, :], 1.0)
        ctx.op("vector", "memset", ones_f[:, :], 1.0, incs=(("dve", 1),))
        emit_ffn0(ctx, bufs, pb, ones_b, ones_c, ones_f)
        ctx.wait("sync", "st")

    _run_block(nc, emit, sems)
    return nc


def host_prep_ffn0(inputs, x1_full):
    gwT = np.ascontiguousarray(inputs["L0_ffn_gate_w"].T)
    uwT = np.ascontiguousarray(inputs["L0_ffn_up_w"].T)
    dwT = np.ascontiguousarray(inputs["L0_ffn_down_w"].T)
    gs_t = np.ascontiguousarray(inputs["L0_ffn_gate_s"].reshape(44, 128).T)
    us_t = np.ascontiguousarray(inputs["L0_ffn_up_s"].reshape(44, 128).T)
    ds_t = np.ascontiguousarray(inputs["L0_ffn_down_s"].reshape(16, 128).T)
    maps = []
    for c in range(NCORES):
        maps.append({
            "x1T": np.ascontiguousarray(x1_full[c].T),
            "gwT": gwT, "uwT": uwT, "dwT": dwT,
            "gs_t": gs_t, "us_t": us_t, "ds_t": ds_t,
        })
    return maps


ENGINES = ("sync", "tensor", "scalar", "vector", "gpsimd")


def phase_barrier(ctx):
    for e in ENGINES:
        ctx.wait(e, *SEM_NAMES)


def build_fused_program():
    nc = bass.Bass("TRN2", target_bir_lowering=False)
    io = {}
    io["x0T"] = nc.dram_tensor("x0T", [D, T], F32, kind="ExternalInput").ap()
    io["yT"] = nc.dram_tensor("yT", [D, T], F32, kind="ExternalOutput").ap()
    io["l0_wqT"] = nc.dram_tensor("l0_wqT", [D, 1024], F32, kind="ExternalInput").ap()
    io["l0_wkT"] = nc.dram_tensor("l0_wkT", [D, 1024], F32, kind="ExternalInput").ap()
    io["l0_wvT"] = nc.dram_tensor("l0_wvT", [D, D], F32, kind="ExternalInput").ap()
    io["l0_wgT"] = nc.dram_tensor("l0_wgT", [D, D], F32, kind="ExternalInput").ap()
    io["l0_woT"] = nc.dram_tensor("l0_woT", [D, D], F32, kind="ExternalInput").ap()
    io["gn_t"] = nc.dram_tensor("gn_t", [128, 4], F32, kind="ExternalInput").ap()
    io["gwT"] = nc.dram_tensor("gwT", [D, 5632], F32, kind="ExternalInput").ap()
    io["uwT"] = nc.dram_tensor("uwT", [D, 5632], F32, kind="ExternalInput").ap()
    io["dwT"] = nc.dram_tensor("dwT", [5632, D], F32, kind="ExternalInput").ap()
    io["gs_t"] = nc.dram_tensor("gs_t", [128, 44], F32, kind="ExternalInput").ap()
    io["us_t"] = nc.dram_tensor("us_t", [128, 44], F32, kind="ExternalInput").ap()
    io["ds_t"] = nc.dram_tensor("ds_t", [128, 16], F32, kind="ExternalInput").ap()
    io["wqT"] = nc.dram_tensor("wqT", [D, D], F32, kind="ExternalInput").ap()
    io["wkT"] = nc.dram_tensor("wkT", [D, D], F32, kind="ExternalInput").ap()
    io["wvT"] = nc.dram_tensor("wvT", [D, D], F32, kind="ExternalInput").ap()
    io["woT"] = nc.dram_tensor("woT", [D, D], F32, kind="ExternalInput").ap()
    io["gupT"] = nc.dram_tensor("gupT", [1024, 11264], U8, kind="ExternalInput").ap()
    io["gup_sr"] = nc.dram_tensor("gup_sr", [1024, 11264], F16, kind="ExternalInput").ap()
    io["dnT"] = nc.dram_tensor("dnT", [2816, 2048], U8, kind="ExternalInput").ap()
    io["dn_sr"] = nc.dram_tensor("dn_sr", [2816, 2048], F16, kind="ExternalInput").ap()
    io["x1T"] = nc.dram_tensor("x1T", [D, T], F32, kind="Internal").ap()
    io["x2T"] = nc.dram_tensor("x2T", [D, T], F32, kind="Internal").ap()
    io["x3T"] = nc.dram_tensor("x3T", [D, T], F32, kind="Internal").ap()
    io["a0_q"] = nc.dram_tensor("a0_q", [1024, T], BF16, kind="Internal").ap()
    io["a0_k"] = nc.dram_tensor("a0_k", [1024, T], BF16, kind="Internal").ap()
    io["a0_v"] = nc.dram_tensor("a0_v", [T, D], BF16, kind="Internal").ap()
    io["a0_g"] = nc.dram_tensor("a0_g", [D, T], BF16, kind="Internal").ap()
    io["a0_og"] = nc.dram_tensor("a0_og", [D, T], BF16, kind="Internal").ap()
    io["h0_scr"] = nc.dram_tensor("h0_scr", [5632, T], BF16, kind="Internal").ap()
    io["a1_q"] = nc.dram_tensor("a1_q", [D, T], BF16, kind="Internal").ap()
    io["a1_k"] = nc.dram_tensor("a1_k", [D, T], BF16, kind="Internal").ap()
    io["a1_v"] = nc.dram_tensor("a1_v", [T, D], BF16, kind="Internal").ap()
    io["a1_z"] = nc.dram_tensor("a1_z", [D, T], BF16, kind="Internal").ap()
    io["wgu_scr"] = nc.dram_tensor("wgu_scr", [D, 11264], BF16, kind="Internal").ap()
    io["wdn_scr"] = nc.dram_tensor("wdn_scr", [NI1, D], BF16, kind="Internal").ap()
    io["h1_scr"] = nc.dram_tensor("h1_scr", [NI1, T], BF16, kind="Internal").ap()
    io["dw_bf"] = nc.dram_tensor("dw_bf", [128, 16 * 44 * 128], BF16, kind="Internal").ap()
    io["wdn_t"] = nc.dram_tensor("wdn_t", [128, 16 * 44 * 128], BF16, kind="Internal").ap()

    sb = SbufAlloc(nc)
    ones_b = sb.tile("ones_b", [128, 1], BF16)
    ones_c = sb.tile("ones_c", [128, 1], F32)
    ones_f = sb.tile("ones_f", [1, 128], F32)
    m0 = sb.mark()
    io1 = {k: io[k] for k in ("x0T", "x1T", "l0_wqT", "l0_wkT", "l0_wvT",
                              "l0_wgT", "l0_woT", "gn_t")}
    io1.update(q_scr=io["a0_q"], k_scr=io["a0_k"], v_scr=io["a0_v"],
               g_scr=io["a0_g"], og_scr=io["a0_og"])
    b1 = plan_attn0(sb, io1)
    sb.reset(m0)
    io2 = {k: io[k] for k in ("x1T", "x2T", "gwT", "uwT", "dwT",
                              "gs_t", "us_t", "ds_t", "dw_bf")}
    io2.update(h_scr=io["h0_scr"])
    b2 = plan_ffn0(sb, io2)
    sb.reset(m0)
    io3 = {k: io[k] for k in ("x2T", "x3T", "wqT", "wkT", "wvT", "woT")}
    io3.update(q_scr=io["a1_q"], k_scr=io["a1_k"], v_scr=io["a1_v"],
               z_scr=io["a1_z"])
    hwm_save = sb.hwm
    sb.hwm = sb.cur
    b3 = plan_attn1(sb, io3)
    a1_hwm = sb.hwm
    sb.hwm = max(hwm_save, a1_hwm)
    sb.reset(a1_hwm)
    OS2 = 512
    bD = {}
    bD["UI"] = [sb.tile(f"d1_ui{i}", [128, OS2], I32) for i in range(2)]
    bD["SCT"] = [sb.tile(f"d1_sct{i}", [128, OS2], F32) for i in range(2)]
    bD["NL"] = sb.tile("d1_nl", [128, OS2], I32)
    bD["NH"] = sb.tile("d1_nh", [128, OS2], I32)
    bD["T2"] = sb.tile("d1_t2", [128, OS2], I32)
    bD["SG"] = sb.tile("d1_sg", [128, OS2], I32)
    bD["VP"] = sb.tile("d1_vp", [128, OS2], F32)
    bD["OUTQ"] = [[sb.tile(f"d1_outq{p}_{i}", [128, OS2], BF16) for i in range(2)]
                  for p in range(2)]
    sb.reset(m0)
    io4 = {k: io[k] for k in ("x3T", "yT", "gupT", "gup_sr", "dnT", "dn_sr",
                              "wgu_scr", "wdn_scr", "wdn_t")}
    io4.update(h_scr=io["h1_scr"])
    b4 = plan_ffn1(sb, io4)
    sb.reset(m0)

    pb = [nc.alloc_psum_tensor(f"pb{i}", [128, 512], F32) for i in range(7)]
    ptr = nc.alloc_psum_tensor("ptr", [128, 1024], BF16)
    sems = {k: nc.alloc_semaphore(name=f"s_{k}") for k in SEM_NAMES}

    gupv = io["gupT"].rearrange("(s p) o -> p s o", p=128)
    gscv = io["gup_sr"].rearrange("(s p) o -> p s o", p=128)
    wguv = io["wgu_scr"].rearrange("(c p) o -> p c o", p=128)
    d1_jobs = [(s, g) for s in range(8) for g in range(11264 // 512)]
    N_FILLER_SLOTS = 3 * NTB + NH * NQB

    total_cap = 3 * NTB * 16 + sum(4 * (qb + 1) for _h in range(NH)
                                   for qb in range(NQB))

    def emit(ctx):
        st8 = {"i": 0, "dve": {}, "st": {}, "cap": 0, "pool_ok": False}

        def filler(w=16):
            st8["cap"] += w
            target = min(len(d1_jobs), len(d1_jobs) * st8["cap"] // total_cap)
            while st8["i"] < target:
                emit_d1_seg(ctx, bD, gupv, gscv, wguv, d1_jobs[st8["i"]], st8,
                            use_pool=st8["pool_ok"])

        ctx.op("vector", "memset", ones_b[:, :], 1.0)
        ctx.op("vector", "memset", ones_c[:, :], 1.0)
        ctx.op("vector", "memset", ones_f[:, :], 1.0, incs=(("dve", 1),))
        emit_attn0(ctx, b1, pb, ptr, ones_b, ones_c, ones_f)
        phase_barrier(ctx)
        emit_ffn0(ctx, b2, pb, ones_b, ones_c, ones_f, ln_banks=(2, 3))
        phase_barrier(ctx)
        emit_attn1(ctx, b3, pb, ones_b, ones_c, ones_f, ln_banks=(2, 3),
                   filler=filler)
        while st8["i"] < len(d1_jobs):
            emit_d1_seg(ctx, bD, gupv, gscv, wguv, d1_jobs[st8["i"]], st8)
        phase_barrier(ctx)
        emit_ffn1(ctx, b4, pb, ones_b, ones_c, ones_f, gate_dequant=False)
        ctx.wait("sync", "st")

    _run_block(nc, emit, sems)
    return nc


def host_prep_fused(inputs):
    w = {
        "l0_wqT": np.ascontiguousarray(inputs["L0_attn_Wq"].T),
        "l0_wkT": np.ascontiguousarray(inputs["L0_attn_Wk"].T),
        "l0_wvT": np.ascontiguousarray(inputs["L0_attn_Wv"].T),
        "l0_wgT": np.ascontiguousarray(inputs["L0_attn_Wg"].T),
        "l0_woT": np.ascontiguousarray(inputs["L0_attn_Wo"].T),
        "gn_t": np.ascontiguousarray(inputs["L0_attn_gnorm_w"].reshape(4, 128).T),
        "gwT": np.ascontiguousarray(inputs["L0_ffn_gate_w"].T),
        "uwT": np.ascontiguousarray(inputs["L0_ffn_up_w"].T),
        "dwT": np.ascontiguousarray(inputs["L0_ffn_down_w"].T),
        "gs_t": np.ascontiguousarray(inputs["L0_ffn_gate_s"].reshape(44, 128).T),
        "us_t": np.ascontiguousarray(inputs["L0_ffn_up_s"].reshape(44, 128).T),
        "ds_t": np.ascontiguousarray(inputs["L0_ffn_down_s"].reshape(16, 128).T),
        "wqT": np.ascontiguousarray(inputs["L1_attn_W_Q"].transpose(2, 0, 1).reshape(D, D)),
        "wkT": np.ascontiguousarray(inputs["L1_attn_W_K"].transpose(2, 0, 1).reshape(D, D)),
        "wvT": np.ascontiguousarray(inputs["L1_attn_W_V"].transpose(2, 0, 1).reshape(D, D)),
        "woT": np.ascontiguousarray(inputs["L1_attn_W_O"].transpose(0, 2, 1).reshape(D, D)),
        "gupT": np.ascontiguousarray(
            inputs["L1_ffn_gate_up_packed"].view(np.uint8).reshape(11264, 1024).T),
        "gup_sr": np.ascontiguousarray(
            np.repeat(inputs["L1_ffn_gate_up_scales"], 64, axis=0)),
        "dnT": np.ascontiguousarray(
            inputs["L1_ffn_down_packed"].view(np.uint8).reshape(2048, 2816).T),
        "dn_sr": np.ascontiguousarray(
            np.repeat(inputs["L1_ffn_down_scales"], 64, axis=0)),
    }
    x = inputs["L0_attn_x"]
    return [dict(w, x0T=np.ascontiguousarray(x[c].T)) for c in range(NCORES)]


def kernel(**inputs):
    nc = build_fused_program()
    maps = host_prep_fused(inputs)
    res = run_bass_kernel_spmd(nc, maps, core_ids=list(range(NCORES)))
    out = np.stack([np.ascontiguousarray(r["yT"].T) for r in res.results])
    return out.astype(np.float32)


def host_prep_ffn0_t(inputs, x1_list):
    gwT = np.ascontiguousarray(inputs["L0_ffn_gate_w"].T)
    uwT = np.ascontiguousarray(inputs["L0_ffn_up_w"].T)
    dwT = np.ascontiguousarray(inputs["L0_ffn_down_w"].T)
    gs_t = np.ascontiguousarray(np.asarray(inputs["L0_ffn_gate_s"]).reshape(44, 128).T)
    us_t = np.ascontiguousarray(np.asarray(inputs["L0_ffn_up_s"]).reshape(44, 128).T)
    ds_t = np.ascontiguousarray(np.asarray(inputs["L0_ffn_down_s"]).reshape(16, 128).T)
    return [{"x1T": x1_list[c], "gwT": gwT, "uwT": uwT, "dwT": dwT,
             "gs_t": gs_t, "us_t": us_t, "ds_t": ds_t} for c in range(NCORES)]


def host_prep_attn1_t(inputs, x2_list):
    wqT = np.ascontiguousarray(inputs["L1_attn_W_Q"].transpose(2, 0, 1).reshape(D, D))
    wkT = np.ascontiguousarray(inputs["L1_attn_W_K"].transpose(2, 0, 1).reshape(D, D))
    wvT = np.ascontiguousarray(inputs["L1_attn_W_V"].transpose(2, 0, 1).reshape(D, D))
    woT = np.ascontiguousarray(inputs["L1_attn_W_O"].transpose(0, 2, 1).reshape(D, D))
    return [{"x2T": x2_list[c], "wqT": wqT, "wkT": wkT, "wvT": wvT, "woT": woT}
            for c in range(NCORES)]


def host_prep_ffn1_t(inputs, x3_list):
    gupT = np.ascontiguousarray(
        np.asarray(inputs["L1_ffn_gate_up_packed"]).view(np.uint8).reshape(11264, 1024).T)
    gup_sr = np.ascontiguousarray(
        np.repeat(np.asarray(inputs["L1_ffn_gate_up_scales"]), 64, axis=0))
    dnT = np.ascontiguousarray(
        np.asarray(inputs["L1_ffn_down_packed"]).view(np.uint8).reshape(2048, 2816).T)
    dn_sr = np.ascontiguousarray(
        np.repeat(np.asarray(inputs["L1_ffn_down_scales"]), 64, axis=0))
    return [{"x3T": x3_list[c], "gupT": gupT, "gup_sr": gup_sr,
             "dnT": dnT, "dn_sr": dn_sr} for c in range(NCORES)]



NH = 16
DH = 128
QB = 512
NQB = T // QB
KC = 128

def plan_attn1(sb, io):
    b = dict(io)
    b["BM"] = sb.tile("p3_bm", [128, 4 * QB], BF16)
    b["EB"] = sb.tile("p3_eb", [128, 1], F32)
    m = sb.mark()
    b["WT"] = sb.tile("p3_wt", [128, DC * D], BF16)
    b["XT"] = [sb.tile(f"p3_xt{i}", [128, DC * TB], BF16) for i in range(2)]
    b["EV"] = [sb.tile(f"p3_ev{i}", [128, TB], BF16) for i in range(2)]
    sb.reset(m)
    b["KH"] = [sb.tile(f"p3_kh{i}", [128, T], BF16) for i in range(2)]
    b["VH"] = [sb.tile(f"p3_vh{i}", [128, 32 * DH], BF16) for i in range(2)]
    b["QH"] = [sb.tile(f"p3_qh{i}", [128, T], BF16) for i in range(2)]
    b["PT"] = sb.tile("p3_pt", [128, 32 * QB], BF16)
    b["RLT"] = sb.tile("p3_rlt", [128, QB], F32)
    b["ZB"] = sb.tile("p3_zb", [128, QB], BF16)
    b["RL"] = sb.tile("p3_rl", [1, QB], F32)
    sb.reset(m)
    b["WO"] = sb.tile("p3_wo", [128, DC * D], BF16)
    b["ZV"] = sb.tile("p3_zv", [128, DC * TB], BF16)
    b["XA"] = sb.tile("p3_xa", [128, DC * TB], F32)
    b["XN"] = sb.tile("p3_xn", [128, DC * TB], F32)
    b["Y"] = sb.tile("p3_y", [128, DC * TB], BF16)
    b["ST"] = sb.tile("p3_st", [1, 5 * TB], F32)
    sb.reset(m)
    return b


def emit_attn1(ctx, b, pb, ones_b, ones_c, ones_f, x_in="x2T", x_out="x3T",
               ln_banks=(6, 7), filler=None, flash_hook=None):
    x2T, x3T = b[x_in], b[x_out]
    qs, ks, vs, zs = b["q_scr"], b["k_scr"], b["v_scr"], b["z_scr"]
    x2v = x2T.rearrange("(c p) t -> p c t", p=128)
    x3v = x3T.rearrange("(c p) t -> p c t", p=128)

    for a in range(4):
        ctx.op("gpsimd", "memset", b["BM"][:, QB * a:QB * (a + 1)], 1.0,
               incs=(("pool", 1),))
        ctx.wait("gpsimd", "pool")
        ctx.op("gpsimd", "affine_select",
               b["BM"][:, QB * a:QB * (a + 1)], b["BM"][:, QB * a:QB * (a + 1)],
               pattern=[[1, QB]], compare_op=ALU.is_ge, fill=0.0,
               channel_multiplier=-1, base=-128 * a, incs=(("pool", 1),))

    pe_at_w = {}
    pe_at_mm = {}
    bank_h = Hist(4)
    ev_h = Hist(2)
    for wi, (wT, outscr, tokmajor) in enumerate((
            (b["wqT"], qs, False), (b["wkT"], ks, False), (b["wvT"], vs, True))):
        wv = wT.rearrange("(c p) e -> p c e", p=128)
        if wi > 0:
            ctx.wait_val("gpsimd", "pe", pe_at_w[wi - 1])
        for i in range(2):
            ctx.dma("gpsimd", "wa",
                   b["WT"][:, :].rearrange("p (c e) -> p c e", c=DC)[:, 8 * i:8 * i + 8, :],
                   wv[:, 8 * i:8 * i + 8, :])
        for tb in range(NTB):
            if filler is not None:
                filler()
            xt = b["XT"][tb % 2]
            key = (wi, tb)
            prev = (wi, tb - 2) if tb >= 2 else ((wi - 1, NTB - 2 + tb) if wi >= 1 else None)
            if prev is not None:
                ctx.wait_val("gpsimd", "pe", pe_at_mm[prev])
            ctx.dma("gpsimd", "ga",
                   xt[:, :].rearrange("p (c t) -> p c t", c=DC),
                   x2v[:, :, TB * tb:TB * (tb + 1)])
            ctx.wait("tensor", "wa")
            ctx.wait("tensor", "ga")
            if not tokmajor:
                for oc in range(DC):
                    bank = pb[oc % 4]
                    ctx.wait_val("tensor", "act", bank_h.guard())
                    for kc in range(DC):
                        ctx.op("tensor", "matmul", bank[:, :],
                               b["WT"][:, D * kc + 128 * oc: D * kc + 128 * (oc + 1)],
                               xt[:, TB * kc:TB * (kc + 1)],
                               start=(kc == 0), stop=(kc == DC - 1),
                               incs=((("pe", 1),) if kc == DC - 1 else ()))
                    ev = b["EV"][oc % 2]
                    ctx.wait("scalar", "pe")
                    ctx.wait_val("scalar", "st", ev_h.guard())
                    ctx.op("scalar", "activation", ev[:, :], bank[:, :], AF.Copy,
                           incs=(("act", 1),))
                    bank_h.push(ctx.val("act"))
                    ctx.wait("sync", "act")
                    ctx.dma("sync", "st",
                           outscr.rearrange("(c p) t -> p c t", p=128)[:, oc, TB * tb:TB * (tb + 1)],
                           ev[:, :])
                    ev_h.push(ctx.val("st"))
            else:
                for ts in range(4):
                    for eb in range(4):
                        bank = pb[(4 * ts + eb) % 4]
                        ctx.wait_val("tensor", "act", bank_h.guard())
                        for kc in range(DC):
                            ctx.op("tensor", "matmul", bank[:, :],
                                   xt[:, TB * kc + 128 * ts: TB * kc + 128 * (ts + 1)],
                                   b["WT"][:, D * kc + TB * eb: D * kc + TB * (eb + 1)],
                                   start=(kc == 0), stop=(kc == DC - 1),
                                   incs=((("pe", 1),) if kc == DC - 1 else ()))
                        ev = b["EV"][eb % 2]
                        ctx.wait("scalar", "pe")
                        ctx.wait_val("scalar", "st", ev_h.guard())
                        ctx.op("scalar", "activation", ev[:, :], bank[:, :], AF.Copy,
                               incs=(("act", 1),))
                        bank_h.push(ctx.val("act"))
                        ctx.wait("sync", "act")
                        ctx.dma("sync", "st",
                               vs.rearrange("(tt p) e -> p tt e", p=128)[:, 4 * tb + ts, TB * eb:TB * (eb + 1)],
                               ev[:, :])
                        ev_h.push(ctx.val("st"))
            pe_at_mm[key] = ctx.val("pe")
        pe_at_w[wi] = ctx.val("pe")

    if flash_hook is not None:
        flash_hook()
    ctx.wait("sync", "st")
    inv_sqrt_dh = 1.0 / math.sqrt(DH)
    ctx.op("vector", "memset", b["EB"][:, :], -8.0, incs=(("dve", 1),))
    dve_eb = ctx.val("dve")
    pe_at_hq = {}
    xa_at_head = {}
    lo_guard = {}
    sc_h = Hist(2)
    order = [(h, qb) for h in range(NH) for qb in range(NQB)]

    def load_head(hh):
        slot = hh % 2
        ctx.dma("sync", "xa", b["KH"][slot][:, :], ks[128 * hh:128 * (hh + 1), :])
        ctx.dma("sync", "xa",
               b["VH"][slot][:, :].rearrange("p (kc e) -> p kc e", kc=32),
               vs.rearrange("(kc p) e -> p kc e", p=128)[:, :, DH * hh:DH * (hh + 1)])
        ctx.dma("sync", "xa", b["QH"][slot][:, :], qs[128 * hh:128 * (hh + 1), :])
        xa_at_head[hh] = ctx.val("xa")

    for idx, (h, qb) in enumerate(order):
        if idx == 0:
            load_head(0)
        kh, vh, qh = b["KH"][h % 2], b["VH"][h % 2], b["QH"][h % 2]
        if qb == NQB - 2 and h + 1 < NH:
            if h >= 1:
                ctx.wait_val("sync", "pe", pe_at_hq[(h - 1, NQB - 1)])
            load_head(h + 1)
        nkc = 4 * (qb + 1)
        ctx.wait_val("tensor", "xa", xa_at_head[h])
        for kc in range(nkc):
            bank = pb[kc % 2]
            ctx.wait_val("tensor", "act", sc_h.guard())
            ctx.op("tensor", "matmul", bank[:, 0:QB],
                   kh[:, 128 * kc:128 * (kc + 1)],
                   qh[:, QB * qb:QB * (qb + 1)],
                   start=True, stop=True, incs=(("pe", 1),))
            ctx.wait("scalar", "pe")
            diag = kc - 4 * qb
            ctx.wait_val("scalar", "dve", dve_eb)
            ctx.op("scalar", "activation", b["PT"][:, QB * kc:QB * (kc + 1)],
                   bank[:, 0:QB], AF.Exp, scale=inv_sqrt_dh, bias=b["EB"][:, 0:1],
                   incs=(("act", 1),))
            sc_h.push(ctx.val("act"))
            if diag >= 0:
                ctx.wait("vector", "act")
                ctx.op("vector", "tensor_mul",
                       b["PT"][:, QB * kc:QB * (kc + 1)],
                       b["PT"][:, QB * kc:QB * (kc + 1)],
                       b["BM"][:, QB * diag:QB * (diag + 1)],
                       incs=(("dve", 1),))
        mask_dve = ctx.val("dve")
        bl, bo = (pb[2], pb[3]) if idx % 2 == 0 else (pb[4], pb[5])
        ctx.wait("tensor", "act")
        ctx.wait_val("tensor", "dve", mask_dve)
        ctx.wait_val("tensor", "act", lo_guard.get(("rl", idx - 2), 0))
        ctx.wait_val("tensor", "dve", lo_guard.get(("zb", idx - 2), 0))
        ctx.wait("tensor", "pool")
        for kc in range(nkc):
            ctx.op("tensor", "matmul", bl[0:1, 0:QB], ones_b[:, 0:1],
                   b["PT"][:, QB * kc:QB * (kc + 1)],
                   start=(kc == 0), stop=(kc == nkc - 1))
        for kc in range(nkc):
            ctx.op("tensor", "matmul", bo[:, 0:QB],
                   vh[:, DH * kc:DH * (kc + 1)],
                   b["PT"][:, QB * kc:QB * (kc + 1)],
                   start=(kc == 0), stop=(kc == nkc - 1),
                   incs=((("pe", 1),) if kc == nkc - 1 else ()))
        pe_at_hq[(h, qb)] = ctx.val("pe")
        ctx.wait("scalar", "pe")
        ctx.op("scalar", "activation", b["RL"][0:1, :], bl[0:1, 0:QB], AF.Copy,
               incs=(("act", 1),))
        lo_guard[("rl", idx)] = ctx.val("act")
        bb = pb[6]
        ctx.wait("tensor", "act")
        ctx.wait("tensor", "dve")
        ctx.op("tensor", "matmul", bb[:, 0:QB], ones_f[0:1, :], b["RL"][0:1, :],
               start=True, stop=True, incs=(("pe", 1),))
        ctx.wait("vector", "pe")
        ctx.op("vector", "reciprocal", b["RLT"][:, :], bb[:, 0:QB],
               incs=(("dve", 1),))
        ctx.wait("vector", "dve")
        ctx.wait("vector", "st")
        ctx.op("vector", "tensor_mul", b["ZB"][:, :], bo[:, 0:QB], b["RLT"][:, :],
               incs=(("dve", 1),))
        lo_guard[("zb", idx)] = ctx.val("dve")
        ctx.wait("sync", "dve")
        ctx.dma("sync", "st",
               zs.rearrange("(c p) t -> p c t", p=128)[:, h, QB * qb:QB * (qb + 1)],
               b["ZB"][:, :])
        if filler is not None:
            filler(nkc)

    ctx.wait("sync", "st")
    ctx.wait("gpsimd", "pe")
    ctx.wait("gpsimd", "dve")
    ctx.wait("gpsimd", "act")
    ctx.wait("gpsimd", "st")
    wov = b["woT"].rearrange("(c p) e -> p c e", p=128)
    for i in range(2):
        ctx.dma("gpsimd", "wb",
               b["WO"][:, :].rearrange("p (c e) -> p c e", c=DC)[:, 8 * i:8 * i + 8, :],
               wov[:, 8 * i:8 * i + 8, :])
    zv = zs.rearrange("(c p) t -> p c t", p=128)
    pe_at_tb = {}
    dve_at_tb = {}
    xa_free = {}
    cbank_h = Hist(2)
    for tb in range(NTB):
        if tb >= 2:
            ctx.wait_val("sync", "pe", pe_at_tb[tb - 2])
        if tb >= 1:
            ctx.wait_val("sync", "dve", xa_free[tb - 1])
        ctx.dma("sync", "xa",
               b["ZV"][:, :].rearrange("p (c t) -> p c t", c=DC),
               zv[:, :, TB * tb:TB * (tb + 1)])
        ctx.dma("sync", "xb",
               b["XA"][:, :].rearrange("p (c t) -> p c t", c=DC),
               x2v[:, :, TB * tb:TB * (tb + 1)])
        ctx.wait("tensor", "wb")
        ctx.wait("tensor", "xa")
        for dc in range(DC):
            bank = pb[4 + dc % 2]
            ctx.wait_val("tensor", "act", cbank_h.guard())
            for kc in range(DC):
                ctx.op("tensor", "matmul", bank[:, :],
                       b["WO"][:, D * kc + 128 * dc: D * kc + 128 * (dc + 1)],
                       b["ZV"][:, TB * kc:TB * (kc + 1)],
                       start=(kc == 0), stop=(kc == DC - 1),
                       incs=((("pe", 1),) if kc == DC - 1 else ()))
            ctx.wait("scalar", "pe")
            ctx.wait("scalar", "dve")
            ctx.op("scalar", "activation", b["Y"][:, TB * dc:TB * (dc + 1)], bank[:, :],
                   AF.Copy, incs=(("act", 1),))
            cbank_h.push(ctx.val("act"))
            emit_postln_prep(ctx, b, dc)
        pe_at_tb[tb] = ctx.val("pe")
        xa_free[tb] = ctx.val("dve")
        dve_at_tb[tb] = emit_postln(ctx, b, pb, ones_b, ones_c, ones_f,
                                    x3v[:, :, TB * tb:TB * (tb + 1)],
                                    banks=ln_banks)


def build_attn1_program():
    nc = bass.Bass("TRN2", target_bir_lowering=False)
    io = {}
    io["x2T"] = nc.dram_tensor("x2T", [D, T], F32, kind="ExternalInput").ap()
    io["wqT"] = nc.dram_tensor("wqT", [D, D], F32, kind="ExternalInput").ap()
    io["wkT"] = nc.dram_tensor("wkT", [D, D], F32, kind="ExternalInput").ap()
    io["wvT"] = nc.dram_tensor("wvT", [D, D], F32, kind="ExternalInput").ap()
    io["woT"] = nc.dram_tensor("woT", [D, D], F32, kind="ExternalInput").ap()
    io["x3T"] = nc.dram_tensor("x3T", [D, T], F32, kind="ExternalOutput").ap()
    io["q_scr"] = nc.dram_tensor("q_scr", [D, T], BF16, kind="Internal").ap()
    io["k_scr"] = nc.dram_tensor("k_scr", [D, T], BF16, kind="Internal").ap()
    io["v_scr"] = nc.dram_tensor("v_scr", [T, D], BF16, kind="Internal").ap()
    io["z_scr"] = nc.dram_tensor("z_scr", [D, T], BF16, kind="Internal").ap()

    sb = SbufAlloc(nc)
    ones_b = sb.tile("ones_b", [128, 1], BF16)
    ones_c = sb.tile("ones_c", [128, 1], F32)
    ones_f = sb.tile("ones_f", [1, 128], F32)
    bufs = plan_attn1(sb, io)
    pb = [nc.alloc_psum_tensor(f"pb{i}", [128, 512], F32) for i in range(8)]
    sems = {k: nc.alloc_semaphore(name=f"s_{k}") for k in SEM_NAMES}

    def emit(ctx):
        ctx.op("vector", "memset", ones_b[:, :], 1.0)
        ctx.op("vector", "memset", ones_c[:, :], 1.0)
        ctx.op("vector", "memset", ones_f[:, :], 1.0, incs=(("dve", 1),))
        emit_attn1(ctx, bufs, pb, ones_b, ones_c, ones_f)
        ctx.wait("sync", "st")

    _run_block(nc, emit, sems)
    return nc


def host_prep_attn1(inputs, x2_full):
    wqT = np.ascontiguousarray(inputs["L1_attn_W_Q"].transpose(2, 0, 1).reshape(D, D))
    wkT = np.ascontiguousarray(inputs["L1_attn_W_K"].transpose(2, 0, 1).reshape(D, D))
    wvT = np.ascontiguousarray(inputs["L1_attn_W_V"].transpose(2, 0, 1).reshape(D, D))
    woT = np.ascontiguousarray(inputs["L1_attn_W_O"].transpose(0, 2, 1).reshape(D, D))
    maps = []
    for c in range(NCORES):
        maps.append({
            "x2T": np.ascontiguousarray(x2_full[c].T),
            "wqT": wqT, "wkT": wkT, "wvT": wvT, "woT": woT,
        })
    return maps



RH = 4
RDK = 256
RDV = 512
RC = 128
NRC = T // RC

def plan_attn0(sb, io):
    b = dict(io)
    b["EPS"] = sb.tile("p1_eps", [128, 1], F32)
    b["GBC"] = sb.tile("p1_gbc", [128, 8], F32)
    b["QSC"] = sb.tile("p1_qsc", [128, RH * TB], BF16)
    b["KSC"] = sb.tile("p1_ksc", [128, RH * TB], BF16)
    b["CM4"] = sb.tile("p1_cm4", [128, 4 * RC], BF16)
    b["ID"] = sb.tile("p1_id", [128, 128], BF16)
    b["SF"] = sb.tile("p1_sf", [128, RH * 2 * RDV], F32)
    b["SB16"] = sb.tile("p1_sb16", [128, RH * 2 * RDV], BF16)
    m = sb.mark()
    b["WT"] = sb.tile("p1_wt", [128, DC * D], BF16)
    b["XT"] = [sb.tile(f"p1_xt{i}", [128, DC * TB], BF16) for i in range(2)]
    b["COS"] = sb.tile("p1_cos", [128, T], F32)
    b["SIN"] = sb.tile("p1_sin", [128, T], F32)
    b["TM1"] = sb.tile("p1_tm1", [128, TB], F32)
    b["TM2"] = sb.tile("p1_tm2", [128, TB], F32)
    b["TM3"] = sb.tile("p1_tm3", [128, TB], F32)
    b["TM4"] = sb.tile("p1_tm4", [128, TB], F32)
    b["EV"] = [sb.tile(f"p1_ev{i}", [128, TB], BF16) for i in range(4)]
    b["TROW"] = sb.tile("p1_trow", [1, TB], F32)
    b["IVF"] = sb.tile("p1_ivf", [1, 128], F32)
    sb.reset(m)
    b["QT4"] = [sb.tile(f"p1_qt4{i}", [128, RH * 2 * 128], BF16) for i in range(4)]
    b["KT4"] = [sb.tile(f"p1_kt4{i}", [128, RH * 2 * 128], BF16) for i in range(4)]
    b["VC4"] = [sb.tile(f"p1_vc4{i}", [128, RH * RDV], BF16) for i in range(4)]
    b["GT4"] = [sb.tile(f"p1_gt4{i}", [128, RH * RDV], BF16) for i in range(4)]
    b["PA4"] = sb.tile("p1_pa4", [128, RH * 128], BF16)
    b["KTM4"] = sb.tile("p1_ktm4", [128, RH * 2 * 128], BF16)
    b["OO4"] = [sb.tile(f"p1_oo4{i}", [128, RH * RDV], F32) for i in range(2)]
    b["SQ4"] = [sb.tile(f"p1_sq4{i}", [128, RH * RDV], BF16) for i in range(2)]
    b["SRW"] = sb.tile("p1_srw", [1, RH * 128], F32)
    b["RSTD"] = [sb.tile(f"p1_rstd{i}", [128, RH * 128], F32) for i in range(2)]
    b["OG4"] = [sb.tile(f"p1_og4{i}", [128, RH * RDV], BF16) for i in range(2)]
    sb.reset(m)
    b["WO"] = sb.tile("p1_wo", [128, DC * D], BF16)
    b["GNC"] = sb.tile("p1_gnc", [128, 4], F32)
    b["ZV"] = sb.tile("p1_zv", [128, DC * TB], BF16)
    b["XA"] = sb.tile("p1_xa", [128, DC * TB], F32)
    b["XN"] = sb.tile("p1_xn", [128, DC * TB], F32)
    b["Y"] = sb.tile("p1_y", [128, DC * TB], BF16)
    b["ST"] = sb.tile("p1_st", [1, 5 * TB], F32)
    sb.reset(m)
    return b


def emit_attn0(ctx, b, pb, ptr, ones_b, ones_c, ones_f, x_in="x0T", x_out="x1T"):
    x0T, x1T = b[x_in], b[x_out]
    x0v = x0T.rearrange("(c p) t -> p c t", p=128)
    x1v = x1T.rearrange("(c p) t -> p c t", p=128)
    qs, ks, vs, gs, ogs = b["q_scr"], b["k_scr"], b["v_scr"], b["g_scr"], b["og_scr"]
    gammas = [1.0 - 2.0 ** (-5.0 - h) for h in range(RH)]
    LN2_10K = math.log(10000.0)
    TWO_PI = 2.0 * math.pi

    ctx.op("vector", "memset", b["EPS"][:, :], 1e-5, incs=(("dve", 1),))
    for h in range(RH):
        lg = math.log(gammas[h])
        ctx.op("vector", "memset", b["GBC"][:, h:h + 1], lg - math.log(16.0))
        ctx.op("vector", "memset", b["GBC"][:, 4 + h:5 + h], -lg,
               incs=((("dve", 1),) if h == RH - 1 else ()))
    ctx.op("gpsimd", "memset", b["ID"][:, :], 1.0, incs=(("pool", 1),))
    ctx.wait("gpsimd", "pool")
    ctx.op("gpsimd", "affine_select", b["ID"][:, :], b["ID"][:, :],
           pattern=[[-1, 128]], compare_op=ALU.is_equal, fill=0.0,
           channel_multiplier=1, incs=(("pool", 1),))
    ctx.op("gpsimd", "iota", b["IVF"][:, :], pattern=[[1, 128]], base=0,
           channel_multiplier=0, allow_small_or_imprecise_dtypes=True,
           incs=(("pool", 1),))
    for h in range(RH):
        for a in range(TB // RC):
            ctx.op("gpsimd", "iota", b["QSC"][:, TB * h + RC * a:TB * h + RC * (a + 1)],
                   pattern=[[1, RC]], base=0, channel_multiplier=0,
                   allow_small_or_imprecise_dtypes=True, incs=(("pool", 1),))
            ctx.op("gpsimd", "iota", b["KSC"][:, TB * h + RC * a:TB * h + RC * (a + 1)],
                   pattern=[[1, RC]], base=0, channel_multiplier=0,
                   allow_small_or_imprecise_dtypes=True, incs=(("pool", 1),))
    ctx.op("gpsimd", "memset", b["CM4"][:, :], 1.0, incs=(("pool", 1),))
    ctx.wait("gpsimd", "pool")
    for a in range(4):
        ctx.op("gpsimd", "affine_select",
               b["CM4"][:, RC * a:RC * (a + 1)], b["CM4"][:, RC * a:RC * (a + 1)],
               pattern=[[1, RC]], compare_op=ALU.is_ge, fill=0.0,
               channel_multiplier=-1, incs=(("pool", 1),))
    ctx.wait("scalar", "pool")
    ctx.wait("scalar", "dve")
    ctx.op("scalar", "activation", b["IVF"][:, :], b["IVF"][:, :], AF.Exp,
           scale=-LN2_10K / 128.0, incs=(("act", 1),))
    for h in range(RH):
        lg = math.log(gammas[h])
        ctx.op("scalar", "activation", b["QSC"][:, TB * h:TB * (h + 1)],
               b["QSC"][:, TB * h:TB * (h + 1)], AF.Exp, scale=lg,
               bias=b["GBC"][:, h:h + 1], incs=(("act", 1),))
        ctx.op("scalar", "activation", b["KSC"][:, TB * h:TB * (h + 1)],
               b["KSC"][:, TB * h:TB * (h + 1)], AF.Exp, scale=-lg,
               bias=b["GBC"][:, 4 + h:5 + h], incs=(("act", 1),))
    for tb in range(NTB):
        ctx.wait("gpsimd", "act")
        ctx.wait_val("gpsimd", "pe", ctx.val("pe"))
        ctx.op("gpsimd", "iota", b["TROW"][:, :], pattern=[[1, TB]], base=TB * tb,
               channel_multiplier=0, allow_small_or_imprecise_dtypes=True,
               incs=(("pool", 1),))
        bank = pb[2]
        ctx.wait("tensor", "pool")
        ctx.wait("tensor", "act")
        ctx.wait("tensor", "dve")
        ctx.op("tensor", "matmul", bank[:, 0:TB], b["IVF"][:, :], b["TROW"][:, :],
               start=True, stop=True, incs=(("pe", 1),))
        INV2PI = 1.0 / TWO_PI
        ctx.wait("vector", "pe")
        ctx.op("vector", "tensor_scalar", b["TM3"][:, :], bank[:, 0:TB],
               INV2PI, None, op0=ALU.mult, incs=(("dve", 1),))
        ctx.wait("vector", "dve")
        ctx.op("vector", "tensor_copy", b["TM4"][:, :].bitcast(I32), b["TM3"][:, :],
               incs=(("dve", 1),))
        ctx.wait("vector", "dve")
        ctx.op("vector", "tensor_copy", b["TM2"][:, :], b["TM4"][:, :].bitcast(I32),
               incs=(("dve", 1),))
        ctx.wait("vector", "dve")
        ctx.op("vector", "scalar_tensor_tensor", b["TM1"][:, :], b["TM2"][:, :],
               -TWO_PI, bank[:, 0:TB], op0=ALU.mult, op1=ALU.add, incs=(("dve", 1),))
        ctx.wait("vector", "dve")
        ctx.op("vector", "tensor_scalar", b["TM2"][:, :], b["TM1"][:, :],
               math.pi, None, op0=ALU.is_gt, incs=(("dve", 1),))
        ctx.wait("vector", "dve")
        ctx.op("vector", "scalar_tensor_tensor", b["TM1"][:, :], b["TM2"][:, :],
               -TWO_PI, b["TM1"][:, :], op0=ALU.mult, op1=ALU.add, incs=(("dve", 1),))
        ctx.op("vector", "tensor_scalar", b["TM3"][:, :], bank[:, 0:TB],
               1.0, 0.5 * math.pi, op0=ALU.mult, op1=ALU.add, incs=(("dve", 1),))
        ctx.wait("vector", "dve")
        ctx.op("vector", "tensor_scalar", b["TM2"][:, :], b["TM3"][:, :],
               INV2PI, None, op0=ALU.mult, incs=(("dve", 1),))
        ctx.wait("vector", "dve")
        ctx.op("vector", "tensor_copy", b["TM4"][:, :].bitcast(I32), b["TM2"][:, :],
               incs=(("dve", 1),))
        ctx.wait("vector", "dve")
        ctx.op("vector", "tensor_copy", b["TM2"][:, :], b["TM4"][:, :].bitcast(I32),
               incs=(("dve", 1),))
        ctx.wait("vector", "dve")
        ctx.op("vector", "scalar_tensor_tensor", b["TM4"][:, :], b["TM2"][:, :],
               -TWO_PI, b["TM3"][:, :], op0=ALU.mult, op1=ALU.add, incs=(("dve", 1),))
        ctx.wait("vector", "dve")
        ctx.op("vector", "tensor_scalar", b["TM2"][:, :], b["TM4"][:, :],
               math.pi, None, op0=ALU.is_gt, incs=(("dve", 1),))
        ctx.wait("vector", "dve")
        ctx.op("vector", "scalar_tensor_tensor", b["TM4"][:, :], b["TM2"][:, :],
               -TWO_PI, b["TM4"][:, :], op0=ALU.mult, op1=ALU.add, incs=(("dve", 1),))
        ctx.wait("scalar", "dve")
        ctx.op("scalar", "activation", b["SIN"][:, TB * tb:TB * (tb + 1)], b["TM1"][:, :],
               AF.Sin, incs=(("act", 1),))
        ctx.op("scalar", "activation", b["COS"][:, TB * tb:TB * (tb + 1)], b["TM4"][:, :],
               AF.Sin, incs=(("act", 1),))

    pe_at_wt = {}
    pe_at_xt = {}
    rope_h = Hist(2)
    for wi, (wT, outscr) in enumerate(((b["l0_wqT"], qs), (b["l0_wkT"], ks))):
        wv = wT.rearrange("(c p) e -> p c e", p=128)
        if wi >= 1:
            ctx.wait_val("gpsimd", "pe", pe_at_wt[wi - 1])
        ctx.dma("gpsimd", "wa",
                b["WT"][:, 0:DC * 1024].rearrange("p (c e) -> p c e", c=DC),
                wv[:, :, :])
        ovw = outscr.rearrange("(h c2 p) t -> p h c2 t", h=RH, c2=2)
        for tb in range(NTB):
            xt = b["XT"][tb % 2]
            key = (wi, tb)
            prev = (wi, tb - 2) if tb >= 2 else ((wi - 1, NTB - 2 + tb) if wi >= 1 else None)
            if prev is not None:
                ctx.wait_val("gpsimd", "pe", pe_at_xt[prev])
            ctx.dma("gpsimd", "ga",
                    xt[:, :].rearrange("p (c t) -> p c t", c=DC),
                    x0v[:, :, TB * tb:TB * (tb + 1)])
            ctx.wait("tensor", "wa")
            ctx.wait("tensor", "ga")
            for h in range(RH):
                b0 = pb[2 * (h % 2)]
                b1 = pb[2 * (h % 2) + 1]
                ctx.wait_val("tensor", "dve", rope_h.guard())
                for c2 in range(2):
                    bank = b0 if c2 == 0 else b1
                    oc = 2 * h + c2
                    for kc in range(DC):
                        ctx.op("tensor", "matmul", bank[:, :],
                               b["WT"][:, 1024 * kc + 128 * oc: 1024 * kc + 128 * (oc + 1)],
                               xt[:, TB * kc:TB * (kc + 1)],
                               start=(kc == 0), stop=(kc == DC - 1),
                               incs=((("pe", 1),) if kc == DC - 1 else ()))
                cosv = b["COS"][:, TB * tb:TB * (tb + 1)]
                sinv = b["SIN"][:, TB * tb:TB * (tb + 1)]
                ctx.wait("vector", "pe")
                ctx.wait("vector", "st")
                ctx.wait("vector", "act")
                ctx.op("vector", "tensor_mul", b["TM1"][:, :], b0[:, :], cosv)
                ctx.op("vector", "tensor_mul", b["TM2"][:, :], b1[:, :], sinv,
                       incs=(("dve", 1),))
                scv = (b["QSC"] if wi == 0 else b["KSC"])[:, TB * h:TB * (h + 1)]
                ctx.wait("vector", "dve")
                ctx.op("vector", "tensor_sub", b["TM1"][:, :], b["TM1"][:, :], b["TM2"][:, :])
                ctx.op("vector", "tensor_mul", b["TM3"][:, :], b1[:, :], cosv)
                ctx.op("vector", "tensor_mul", b["TM4"][:, :], b0[:, :], sinv,
                       incs=(("dve", 1),))
                rope_h.push(ctx.val("dve"))
                ctx.wait("vector", "dve")
                ctx.op("vector", "tensor_mul", b["EV"][0][:, :], b["TM1"][:, :], scv)
                ctx.op("vector", "tensor_add", b["TM3"][:, :], b["TM3"][:, :], b["TM4"][:, :],
                       incs=(("dve", 1),))
                ctx.wait("vector", "dve")
                ctx.op("vector", "tensor_mul", b["EV"][1][:, :], b["TM3"][:, :], scv,
                       incs=(("dve", 1),))
                ctx.wait("sync", "dve")
                ctx.dma("sync", "st", ovw[:, h, 0, TB * tb:TB * (tb + 1)], b["EV"][0][:, :])
                ctx.dma("sync", "st", ovw[:, h, 1, TB * tb:TB * (tb + 1)], b["EV"][1][:, :])
            pe_at_xt[key] = ctx.val("pe")
        pe_at_wt[wi] = ctx.val("pe")

    ctx.wait("tensor", "dve")
    a0bank_h = Hist(4)
    a0ev_h = Hist(2)
    a0ev_h.push(ctx.val("st"))
    a0ev_h.push(ctx.val("st"))
    for wi, (wT, outscr, tokmajor, width, silu) in enumerate((
            (b["l0_wvT"], vs, True, D, False), (b["l0_wgT"], gs, False, D, True))):
        wv = wT.rearrange("(c p) e -> p c e", p=128)
        ctx.wait_val("gpsimd", "pe", pe_at_wt[1] if wi == 0 else pe_at_wt2[wi - 1])
        for i in range(2):
            ctx.dma("gpsimd", "wa",
                    b["WT"][:, :].rearrange("p (c e) -> p c e", c=DC)[:, 8 * i:8 * i + 8, :],
                    wv[:, 8 * i:8 * i + 8, :])
        for tb in range(NTB):
            xt = b["XT"][tb % 2]
            key = (2 + wi, tb)
            prev = (2 + wi, tb - 2) if tb >= 2 else ((1 + wi, NTB - 2 + tb))
            ctx.wait_val("gpsimd", "pe", pe_at_xt[prev])
            ctx.dma("gpsimd", "ga",
                    xt[:, :].rearrange("p (c t) -> p c t", c=DC),
                    x0v[:, :, TB * tb:TB * (tb + 1)])
            ctx.wait("tensor", "wa")
            ctx.wait("tensor", "ga")
            if tokmajor:
                for ts in range(4):
                    for eb in range(4):
                        bank = pb[(4 * ts + eb) % 4]
                        ctx.wait_val("tensor", "act", a0bank_h.guard())
                        for kc in range(DC):
                            ctx.op("tensor", "matmul", bank[:, :],
                                   xt[:, TB * kc + 128 * ts: TB * kc + 128 * (ts + 1)],
                                   b["WT"][:, D * kc + TB * eb: D * kc + TB * (eb + 1)],
                                   start=(kc == 0), stop=(kc == DC - 1),
                                   incs=((("pe", 1),) if kc == DC - 1 else ()))
                        ev = b["EV"][eb % 2]
                        ctx.wait("scalar", "pe")
                        ctx.wait_val("scalar", "st", a0ev_h.guard())
                        ctx.op("scalar", "activation", ev[:, :], bank[:, :], AF.Copy,
                               incs=(("act", 1),))
                        a0bank_h.push(ctx.val("act"))
                        ctx.wait("sync", "act")
                        ctx.dma("sync", "st",
                                outscr.rearrange("(tt p) e -> p tt e", p=128)[:, 4 * tb + ts, TB * eb:TB * (eb + 1)],
                                ev[:, :])
                        a0ev_h.push(ctx.val("st"))
            else:
                for oc in range(DC):
                    bank = pb[oc % 4]
                    ctx.wait_val("tensor", "act", a0bank_h.guard())
                    for kc in range(DC):
                        ctx.op("tensor", "matmul", bank[:, :],
                               b["WT"][:, D * kc + 128 * oc: D * kc + 128 * (oc + 1)],
                               xt[:, TB * kc:TB * (kc + 1)],
                               start=(kc == 0), stop=(kc == DC - 1),
                               incs=((("pe", 1),) if kc == DC - 1 else ()))
                    ev = b["EV"][oc % 2]
                    ctx.wait("scalar", "pe")
                    ctx.wait_val("scalar", "st", a0ev_h.guard())
                    ctx.op("scalar", "activation", ev[:, :], bank[:, :],
                           AF.Silu if silu else AF.Copy, incs=(("act", 1),))
                    a0bank_h.push(ctx.val("act"))
                    ctx.wait("sync", "act")
                    ctx.dma("sync", "st",
                            outscr.rearrange("(c p) t -> p c t", p=128)[:, oc, TB * tb:TB * (tb + 1)],
                            ev[:, :])
                    a0ev_h.push(ctx.val("st"))
            pe_at_xt[key] = ctx.val("pe")
        pe_at_wt2 = {wi: ctx.val("pe")}

    ctx.wait("sync", "st")
    ctx.wait("gpsimd", "st")
    ctx.wait("vector", "pe")
    ctx.wait("vector", "act")
    ctx.wait("vector", "st")
    ctx.wait("vector", "pool")
    ctx.op("vector", "memset", b["SF"][:, :], 0.0)
    ctx.op("vector", "memset", b["SB16"][:, :], 0.0, incs=(("dve", 1),))
    sb16_ready = [("dve", ctx.val("dve"))] * RH
    qv = qs.rearrange("(h c2 p) t -> p h c2 t", h=RH, c2=2)
    kv = ks.rearrange("(h c2 p) t -> p h c2 t", h=RH, c2=2)
    vv = vs.rearrange("(c p) e -> p c e", p=128)
    gv = gs.rearrange("(hc p) t -> p hc t", p=128)
    ogv = ogs.rearrange("(hc p) t -> p hc t", p=128)
    gC = [gammas[h] ** RC for h in range(RH)]
    r = {}

    def issue_loads(cc):
        if cc >= NRC:
            return
        sl = cc % 4
        if cc >= 4:
            ctx.wait_val("sync", "pe", r[("pe_end", cc - 4)])
            ctx.wait_val("gpsimd", "pe", r[("pe_end", cc - 4)])
            ctx.wait_val("gpsimd", "pool", r[("og_pool", cc - 4)])
        ctx.dma("sync", "xa",
                b["QT4"][sl][:, :].rearrange("p (h c2 t) -> p h c2 t", h=RH, c2=2),
                qv[:, :, :, RC * cc:RC * (cc + 1)])
        ctx.dma("sync", "xa",
                b["KT4"][sl][:, :].rearrange("p (h c2 t) -> p h c2 t", h=RH, c2=2),
                kv[:, :, :, RC * cc:RC * (cc + 1)])
        ctx.dma("gpsimd", "gb", b["VC4"][sl][:, :], vv[:, cc, :])
        ctx.dma("gpsimd", "gb",
                b["GT4"][sl][:, :].rearrange("p (hc t) -> p hc t", hc=16),
                gv[:, :, RC * cc:RC * (cc + 1)])
        r[("ld_xa", cc)] = ctx.val("xa")
        r[("ld_gb", cc)] = ctx.val("gb")

    issue_loads(0)
    issue_loads(1)
    for c in range(NRC):
        slot = c % 4
        QT, KT, VC, GT = (b["QT4"][slot], b["KT4"][slot],
                          b["VC4"][slot], b["GT4"][slot])
        issue_loads(c + 2)
        ctx.wait_val("tensor", "xa", r[("ld_xa", c)])
        ctx.wait_val("tensor", "gb", r[("ld_gb", c)])
        ctx.wait_val("tensor", "dve", r.get(("supd", c - 1, RH - 1), 0))
        for h in range(RH):
            for c2 in range(2):
                ctx.op("tensor", "matmul", pb[0][:, 128 * h:128 * (h + 1)],
                       KT[:, 128 * (2 * h + c2):128 * (2 * h + c2 + 1)],
                       QT[:, 128 * (2 * h + c2):128 * (2 * h + c2 + 1)],
                       start=(c2 == 0), stop=(c2 == 1),
                       incs=((("pe", 1),) if h == RH - 1 and c2 == 1 else ()))
        r[("pe_attn", c)] = ctx.val("pe")
        ctx.wait_val("tensor", "dve", r.get(("ktm_dve", c - 1), 0))
        for h in range(RH):
            for c2 in range(2):
                j = 2 * h + c2
                ctx.op("tensor", "transpose", ptr[:, 128 * j:128 * (j + 1)],
                       KT[:, 128 * j:128 * (j + 1)], b["ID"][:, :],
                       incs=((("pe", 1),) if j == 7 else ()))
        r[("pe_ktr", c)] = ctx.val("pe")
        ctx.wait_val("vector", "pe", r[("pe_attn", c)])
        ctx.op("vector", "tensor_mul", b["PA4"][:, :], pb[0][:, :], b["CM4"][:, :],
               incs=(("dve", 1),))
        r[("pa_dve", c)] = ctx.val("dve")
        ctx.wait_val("vector", "pe", r[("pe_ktr", c)])
        for h in range(RH):
            ctx.op("vector", "tensor_scalar", b["KTM4"][:, 256 * h:256 * (h + 1)],
                   ptr[:, 256 * h:256 * (h + 1)], gC[h], None, op0=ALU.mult,
                   incs=((("dve", 1),) if h == RH - 1 else ()))
        r[("ktm_dve", c)] = ctx.val("dve")
        ctx.wait_val("tensor", "act", r.get(("oo_act", c - 1), 0))
        ctx.wait_val("tensor", "dve", r[("pa_dve", c)])
        for h in range(RH):
            ctx.wait_val("tensor", *sb16_ready[h])
            for dvc in range(4):
                dsl = slice(RDV * h + 128 * dvc, RDV * h + 128 * (dvc + 1))
                for c2 in range(2):
                    ctx.op("tensor", "matmul",
                           pb[1 + h][:, 128 * dvc:128 * (dvc + 1)],
                           b["SB16"][:, RDV * (2 * h + c2) + 128 * dvc:
                                     RDV * (2 * h + c2) + 128 * (dvc + 1)],
                           QT[:, 128 * (2 * h + c2):128 * (2 * h + c2 + 1)],
                           start=(c2 == 0), stop=False)
                ctx.op("tensor", "matmul",
                       pb[1 + h][:, 128 * dvc:128 * (dvc + 1)],
                       VC[:, dsl], b["PA4"][:, 128 * h:128 * (h + 1)],
                       start=False, stop=True,
                       incs=((("pe", 1),) if h == RH - 1 and dvc == 3 else ()))
        r[("pe_o", c)] = ctx.val("pe")
        ctx.wait_val("tensor", "dve", r[("ktm_dve", c)])
        for h in range(RH):
            gd = r.get(("supd", c, h - 1), r.get(("supd", c - 1, RH - 1), 0))
            ctx.wait_val("tensor", "dve", gd)
            for c2 in range(2):
                ctx.op("tensor", "matmul", pb[6 if c2 == 0 else 0][:, :],
                       b["KTM4"][:, 128 * (2 * h + c2):128 * (2 * h + c2 + 1)],
                       VC[:, RDV * h:RDV * (h + 1)],
                       start=True, stop=True,
                       incs=((("pe", 1),) if c2 == 1 else ()))
            r[("pe_delta", c, h)] = ctx.val("pe")
            ctx.wait_val("vector", "pe", r[("pe_delta", c, h)])
            for c2 in range(2):
                ctx.op("vector", "scalar_tensor_tensor",
                       b["SF"][:, RDV * (2 * h + c2):RDV * (2 * h + c2 + 1)],
                       b["SF"][:, RDV * (2 * h + c2):RDV * (2 * h + c2 + 1)],
                       gC[h], pb[6 if c2 == 0 else 0][:, :], op0=ALU.mult, op1=ALU.add,
                       incs=((("dve", 1),) if c2 == 1 else ()))
            r[("supd", c, h)] = ctx.val("dve")
        r[("pe_end", c)] = ctx.val("pe")
        sb16_next = []
        ctx.wait_val("scalar", "pe", r[("pe_o", c)])
        for h in range(RH):
            ctx.wait_val("scalar", "dve", r[("supd", c, h)])
            ctx.op("scalar", "activation",
                   b["SB16"][:, RDV * 2 * h:RDV * 2 * (h + 1)],
                   b["SF"][:, RDV * 2 * h:RDV * 2 * (h + 1)], AF.Copy,
                   incs=(("act", 1),))
            sb16_next.append(("act", ctx.val("act")))
        oo = b["OO4"][c % 2]
        sq = b["SQ4"][c % 2]
        ctx.wait_val("scalar", "dve", r.get(("sq_dve", c - 2), 0))
        ctx.wait_val("scalar", "pool", r.get(("og_pool", c - 2), 0))
        for h in range(RH):
            ctx.op("scalar", "activation", oo[:, RDV * h:RDV * (h + 1)],
                   pb[1 + h][:, :], AF.Copy, incs=(("act", 1),))
        r[("oo_act", c)] = ctx.val("act")
        ctx.wait_val("vector", "act", r[("oo_act", c)])
        ctx.wait_val("vector", "pe", r.get(("stats_pe", c - 2), 0))
        ctx.op("vector", "tensor_mul", sq[:, :], oo[:, :], oo[:, :],
               incs=(("dve", 1),))
        r[("sq_dve", c)] = ctx.val("dve")
        ctx.wait_val("tensor", "dve", r[("sq_dve", c)])
        ctx.wait_val("tensor", "act", r.get(("rstd_act", c - 1), 0))
        for h in range(RH):
            for dvc in range(4):
                ctx.op("tensor", "matmul", pb[5][0:1, 128 * h:128 * (h + 1)],
                       ones_b[:, 0:1],
                       sq[:, 128 * (4 * h + dvc):128 * (4 * h + dvc + 1)],
                       start=(dvc == 0), stop=(dvc == 3),
                       incs=((("pe", 1),) if h == RH - 1 and dvc == 3 else ()))
        r[("stats_pe", c)] = ctx.val("pe")
        ctx.wait_val("scalar", "pe", r[("stats_pe", c)])
        ctx.op("scalar", "activation", b["SRW"][0:1, :], pb[5][0:1, 0:512], AF.Ln,
               scale=1.0 / RDV, bias=b["EPS"][0:1, 0:1], incs=(("act", 1),))
        ctx.wait("scalar", "act")
        ctx.op("scalar", "activation", b["SRW"][0:1, :], b["SRW"][0:1, :], AF.Exp,
               scale=-0.5, incs=(("act", 1),))
        r[("exp_act", c)] = ctx.val("act")
        ctx.wait_val("tensor", "act", r[("exp_act", c)])
        for h in range(RH):
            ctx.op("tensor", "matmul", pb[5][:, 128 * h:128 * (h + 1)],
                   ones_f[0:1, :], b["SRW"][0:1, 128 * h:128 * (h + 1)],
                   start=True, stop=True,
                   incs=((("pe", 1),) if h == RH - 1 else ()))
        r[("bcast_pe", c)] = ctx.val("pe")
        rstd = b["RSTD"][c % 2]
        ctx.wait_val("scalar", "pe", r[("bcast_pe", c)])
        ctx.wait_val("scalar", "pool", r.get(("og_pool", c - 2), 0))
        ctx.op("scalar", "activation", rstd[:, :], pb[5][:, 0:512], AF.Copy,
               incs=(("act", 1),))
        r[("rstd_act", c)] = ctx.val("act")
        sb16_ready = sb16_next
        og = b["OG4"][c % 2]
        ctx.wait_val("gpsimd", "act", r[("rstd_act", c)])
        ctx.wait_val("gpsimd", "st", r.get(("og_st", c - 2), 0))
        for hc in range(16):
            ctx.op("gpsimd", "tensor_mul", og[:, 128 * hc:128 * (hc + 1)],
                   oo[:, 128 * hc:128 * (hc + 1)],
                   rstd[:, 128 * (hc // 4):128 * (hc // 4 + 1)],
                   incs=(("pool", 1),))
        ctx.wait("gpsimd", "pool")
        ctx.op("gpsimd", "tensor_mul", og[:, :], og[:, :], GT[:, :],
               incs=(("pool", 1),))
        r[("og_pool", c)] = ctx.val("pool")
        ctx.wait_val("sync", "pool", r[("og_pool", c)])
        ctx.dma("sync", "st", ogv[:, :, RC * c:RC * (c + 1)],
                og[:, :].rearrange("p (hc t) -> p hc t", hc=16))
        r[("og_st", c)] = ctx.val("st")

    ctx.wait("sync", "st")
    ctx.wait("gpsimd", "pe")
    ctx.wait("gpsimd", "dve")
    ctx.wait("gpsimd", "act")
    ctx.wait("gpsimd", "st")
    wov = b["l0_woT"].rearrange("(c p) e -> p c e", p=128)
    for i in range(2):
        ctx.dma("gpsimd", "wb",
                b["WO"][:, :].rearrange("p (c e) -> p c e", c=DC)[:, 8 * i:8 * i + 8, :],
                wov[:, 8 * i:8 * i + 8, :])
    ctx.dma("sync", "xa", b["GNC"][:, :], b["gn_t"])
    ctx.wait("vector", "wb")
    ctx.wait("vector", "xa")
    for hc in range(DC):
        ctx.op("vector", "tensor_scalar", b["WO"][:, D * hc:D * (hc + 1)],
               b["WO"][:, D * hc:D * (hc + 1)], b["GNC"][:, hc % 4:hc % 4 + 1], None,
               op0=ALU.mult, incs=((("dve", 1),) if hc == DC - 1 else ()))
    pe_at_tb = {}
    dve_at_tb = {}
    xa_free = {}
    c0bank_h = Hist(4)
    for tb in range(NTB):
        if tb >= 2:
            ctx.wait_val("sync", "pe", pe_at_tb[tb - 2])
        if tb >= 1:
            ctx.wait_val("sync", "dve", xa_free[tb - 1])
        ctx.dma("sync", "xa", b["ZV"][:, :].rearrange("p (c t) -> p c t", c=DC),
                ogv[:, :, TB * tb:TB * (tb + 1)])
        ctx.dma("sync", "xb", b["XA"][:, :].rearrange("p (c t) -> p c t", c=DC),
                x0v[:, :, TB * tb:TB * (tb + 1)])
        ctx.wait("tensor", "wb")
        ctx.wait("tensor", "xa")
        if tb == 0:
            ctx.wait("tensor", "dve")
        for dc in range(DC):
            bank = pb[dc % 4]
            ctx.wait_val("tensor", "act", c0bank_h.guard())
            for kc in range(DC):
                ctx.op("tensor", "matmul", bank[:, :],
                       b["WO"][:, D * kc + 128 * dc: D * kc + 128 * (dc + 1)],
                       b["ZV"][:, TB * kc:TB * (kc + 1)],
                       start=(kc == 0), stop=(kc == DC - 1),
                       incs=((("pe", 1),) if kc == DC - 1 else ()))
            ctx.wait("scalar", "pe")
            ctx.wait("scalar", "dve")
            ctx.op("scalar", "activation", b["Y"][:, TB * dc:TB * (dc + 1)], bank[:, :],
                   AF.Copy, incs=(("act", 1),))
            c0bank_h.push(ctx.val("act"))
            emit_postln_prep(ctx, b, dc)
        pe_at_tb[tb] = ctx.val("pe")
        xa_free[tb] = ctx.val("dve")
        dve_at_tb[tb] = emit_postln(ctx, b, pb, ones_b, ones_c, ones_f,
                                    x1v[:, :, TB * tb:TB * (tb + 1)], banks=(5, 6))


def build_attn0_program():
    nc = bass.Bass("TRN2", target_bir_lowering=False)
    io = {}
    io["x0T"] = nc.dram_tensor("x0T", [D, T], F32, kind="ExternalInput").ap()
    io["l0_wqT"] = nc.dram_tensor("l0_wqT", [D, 1024], F32, kind="ExternalInput").ap()
    io["l0_wkT"] = nc.dram_tensor("l0_wkT", [D, 1024], F32, kind="ExternalInput").ap()
    io["l0_wvT"] = nc.dram_tensor("l0_wvT", [D, D], F32, kind="ExternalInput").ap()
    io["l0_wgT"] = nc.dram_tensor("l0_wgT", [D, D], F32, kind="ExternalInput").ap()
    io["l0_woT"] = nc.dram_tensor("l0_woT", [D, D], F32, kind="ExternalInput").ap()
    io["gn_t"] = nc.dram_tensor("gn_t", [128, 4], F32, kind="ExternalInput").ap()
    io["x1T"] = nc.dram_tensor("x1T", [D, T], F32, kind="ExternalOutput").ap()
    io["q_scr"] = nc.dram_tensor("q_scr", [1024, T], BF16, kind="Internal").ap()
    io["k_scr"] = nc.dram_tensor("k_scr", [1024, T], BF16, kind="Internal").ap()
    io["v_scr"] = nc.dram_tensor("v_scr", [T, D], BF16, kind="Internal").ap()
    io["g_scr"] = nc.dram_tensor("g_scr", [D, T], BF16, kind="Internal").ap()
    io["og_scr"] = nc.dram_tensor("og_scr", [D, T], BF16, kind="Internal").ap()

    sb = SbufAlloc(nc)
    ones_b = sb.tile("ones_b", [128, 1], BF16)
    ones_c = sb.tile("ones_c", [128, 1], F32)
    ones_f = sb.tile("ones_f", [1, 128], F32)
    bufs = plan_attn0(sb, io)
    pb = [nc.alloc_psum_tensor(f"pb{i}", [128, 512], F32) for i in range(7)]
    ptr = nc.alloc_psum_tensor("ptr", [128, 1024], BF16)
    sems = {k: nc.alloc_semaphore(name=f"s_{k}") for k in SEM_NAMES}

    def emit(ctx):
        ctx.op("vector", "memset", ones_b[:, :], 1.0)
        ctx.op("vector", "memset", ones_c[:, :], 1.0)
        ctx.op("vector", "memset", ones_f[:, :], 1.0, incs=(("dve", 1),))
        emit_attn0(ctx, bufs, pb, ptr, ones_b, ones_c, ones_f)
        ctx.wait("sync", "st")

    _run_block(nc, emit, sems)
    return nc


def host_prep_attn0(inputs):
    wqT = np.ascontiguousarray(inputs["L0_attn_Wq"].T)
    wkT = np.ascontiguousarray(inputs["L0_attn_Wk"].T)
    wvT = np.ascontiguousarray(inputs["L0_attn_Wv"].T)
    wgT = np.ascontiguousarray(inputs["L0_attn_Wg"].T)
    woT = np.ascontiguousarray(inputs["L0_attn_Wo"].T)
    gn_t = np.ascontiguousarray(inputs["L0_attn_gnorm_w"].reshape(4, 128).T)
    x = inputs["L0_attn_x"]
    maps = []
    for c in range(NCORES):
        maps.append({
            "x0T": np.ascontiguousarray(x[c].T),
            "l0_wqT": wqT, "l0_wkT": wkT, "l0_wvT": wvT,
            "l0_wgT": wgT, "l0_woT": woT, "gn_t": gn_t,
        })
    return maps



NI1 = 5632
NOC1 = 88
OSEG = 1024

def plan_ffn1(sb, io):
    b = dict(io)
    m = sb.mark()
    b["UI"] = sb.tile("p4_ui", [128, OSEG], I32)
    b["NL"] = sb.tile("p4_nl", [128, OSEG], I32)
    b["NH"] = sb.tile("p4_nh", [128, OSEG], I32)
    b["T2"] = sb.tile("p4_t2", [128, OSEG], I32)
    b["SG"] = sb.tile("p4_sg", [128, OSEG], I32)
    b["VP"] = sb.tile("p4_vp", [128, OSEG], F32)
    b["SCT"] = sb.tile("p4_sct", [128, OSEG], F32)
    b["OUTQ"] = [sb.tile(f"p4_outq{i}", [128, OSEG], BF16) for i in range(2)]
    b["XB"] = sb.tile("p4_xb", [128, DC * T], BF16)
    b["GW"] = [sb.tile(f"p4_gw{i}", [128, DC * 128], BF16) for i in range(2)]
    b["UW"] = [sb.tile(f"p4_uw{i}", [128, DC * 128], BF16) for i in range(2)]
    b["HG"] = [sb.tile(f"p4_hg{i}", [128, TB], BF16) for i in range(4)]
    b["HU"] = [sb.tile(f"p4_hu{i}", [128, TB], BF16) for i in range(4)]
    b["HH"] = [sb.tile(f"p4_hh{i}", [128, TB], BF16) for i in range(4)]
    sb.reset(m)
    b["HT"] = [sb.tile(f"p4_ht{i}", [128, 44 * TB], BF16) for i in range(2)]
    b["DW"] = [sb.tile(f"p4_dw{i}", [128, 44 * 128], BF16) for i in range(2)]
    b["XA"] = sb.tile("p4_xa", [128, DC * TB], F32)
    m_xn = sb.mark()
    b["XN"] = sb.tile("p4_xn", [128, DC * TB], F32)
    m_after = sb.mark()
    sb.reset(m_xn)
    b["YSQ"] = sb.tile("p4_ysq", [128, DC * TB], BF16)
    sb.reset(m_after)
    b["Y"] = sb.tile("p4_y", [128, DC * TB], BF16)
    b["ST"] = sb.tile("p4_st", [1, 5 * TB], F32)
    sb.reset(m)
    return b


def emit_dequant_seg(ctx, b, u8v, scv, ov, g, out_chunks,
                     ld_eng="gpsimd", ld_sem="ga", st_eng="sync", st_sem="st",
                     tiled=False):
    osl = slice(OSEG * g, OSEG * (g + 1))
    ctx.wait(ld_eng, "dve")
    ctx.dma(ld_eng, ld_sem, b["UI"][:, :], u8v[:, osl])
    ctx.dma(ld_eng, ld_sem, b["SCT"][:, :], scv[:, osl])
    ctx.wait("vector", ld_sem)
    ctx.wait("vector", st_sem)
    ctx.op("vector", "tensor_scalar", b["NL"][:, :], b["UI"][:, :], 15, None,
           op0=ALU.bitwise_and, incs=(("dve", 1),))
    ctx.op("vector", "tensor_scalar", b["NH"][:, :], b["UI"][:, :], 4, None,
           op0=ALU.logical_shift_right, incs=(("dve", 1),))
    for par, N in ((0, b["NL"]), (1, b["NH"])):
        ctx.wait("vector", "dve")
        ctx.op("vector", "tensor_scalar", b["T2"][:, :], N[:, :], 22, 0x01C00000,
               op0=ALU.logical_shift_left, op1=ALU.bitwise_and, incs=(("dve", 1),))
        ctx.op("vector", "tensor_scalar", b["SG"][:, :], N[:, :], 8, 28,
               op0=ALU.bitwise_and, op1=ALU.logical_shift_left, incs=(("dve", 1),))
        ctx.wait("vector", "dve")
        ctx.op("vector", "tensor_scalar", b["T2"][:, :], b["T2"][:, :],
               0x3F000000, None, op0=ALU.add, incs=(("dve", 1),))
        ctx.wait("vector", "dve")
        ctx.op("vector", "tensor_scalar", b["VP"][:, :], b["T2"][:, :].bitcast(F32),
               2.0, -1.0, op0=ALU.mult, op1=ALU.add, incs=(("dve", 1),))
        ctx.wait("vector", "dve")
        ctx.op("vector", "tensor_tensor", b["VP"][:, :], b["VP"][:, :],
               b["T2"][:, :].bitcast(F32), op=ALU.min, incs=(("dve", 1),))
        ctx.wait("vector", "dve")
        ctx.op("vector", "tensor_tensor", b["VP"][:, :].bitcast(I32),
               b["VP"][:, :].bitcast(I32), b["SG"][:, :], op=ALU.bitwise_or,
               incs=(("dve", 1),))
        ctx.wait("vector", "dve")
        ctx.op("vector", "tensor_mul", b["OUTQ"][par][:, :], b["VP"][:, :],
               b["SCT"][:, :], incs=(("dve", 1),))
        ctx.wait(st_eng, "dve")
        if tiled:
            ctx.dma(st_eng, st_sem, ov[:, out_chunks[par], 8 * g:8 * (g + 1), :],
                    b["OUTQ"][par][:, :].rearrange("p (hb j) -> p hb j", hb=8))
        else:
            ctx.dma(st_eng, st_sem, ov[:, out_chunks[par], osl],
                    b["OUTQ"][par][:, :])


def emit_d1_seg(ctx, bD, u8v_all, scv_all, ov, job, state, use_pool=False):
    OS = 512
    s, g = job
    k = state["i"]
    state["i"] += 1
    sl = k % 2
    osl = slice(OS * g, OS * (g + 1))
    u8v = u8v_all[:, s, :]
    scv = scv_all[:, s, :]
    out_chunks = (2 * s, 2 * s + 1)
    ctx.wait_val("gpsimd", "dve", state["dve"].get(k - 2, 0))
    ctx.dma("gpsimd", "dqa", bD["UI"][sl][:, :], u8v[:, osl])
    ctx.dma("gpsimd", "dqa", bD["SCT"][sl][:, :], scv[:, osl])
    UI, SCT = bD["UI"][sl], bD["SCT"][sl]
    if use_pool:
        eng, esem = "gpsimd", "pool"
        ctx.wait("gpsimd", "dqa")
    else:
        eng, esem = "vector", "dve"
        ctx.wait("vector", "dqa")
    ctx.op(eng, "tensor_scalar", bD["NL"][:, :], UI[:, :], 15, None,
           op0=ALU.bitwise_and, incs=((esem, 1),))
    ctx.op(eng, "tensor_scalar", bD["NH"][:, :], UI[:, :], 4, None,
           op0=ALU.logical_shift_right, incs=((esem, 1),))
    for par, N in ((0, bD["NL"]), (1, bD["NH"])):
        ctx.wait(eng, esem)
        ctx.op(eng, "tensor_scalar", bD["T2"][:, :], N[:, :], 22, 0x01C00000,
               op0=ALU.logical_shift_left, op1=ALU.bitwise_and, incs=((esem, 1),))
        ctx.op(eng, "tensor_scalar", bD["SG"][:, :], N[:, :], 8, 28,
               op0=ALU.bitwise_and, op1=ALU.logical_shift_left, incs=((esem, 1),))
        ctx.wait(eng, esem)
        ctx.op(eng, "tensor_scalar", bD["T2"][:, :], bD["T2"][:, :],
               0x3F000000, None, op0=ALU.add, incs=((esem, 1),))
        if use_pool:
            ctx.wait_val("gpsimd", "dve", ctx.val("dve"))
        ctx.wait(eng, esem)
        ctx.op(eng, "tensor_scalar", bD["VP"][:, :], bD["T2"][:, :].bitcast(F32),
               2.0, -1.0, op0=ALU.mult, op1=ALU.add, incs=((esem, 1),))
        ctx.wait(eng, esem)
        ctx.op(eng, "tensor_tensor", bD["VP"][:, :], bD["VP"][:, :],
               bD["T2"][:, :].bitcast(F32), op=ALU.min, incs=((esem, 1),))
        ctx.wait(eng, esem)
        ctx.op(eng, "tensor_tensor", bD["VP"][:, :].bitcast(I32),
               bD["VP"][:, :].bitcast(I32), bD["SG"][:, :], op=ALU.bitwise_or,
               incs=((esem, 1),))
        outq = bD["OUTQ"][par][sl]
        if use_pool:
            ctx.wait("vector", "pool")
            ctx.wait("vector", "dqa")
        else:
            ctx.wait("vector", "dve")
        ctx.wait_val("vector", "dqs", state["st"].get(k - 2, 0))
        ctx.op("vector", "tensor_mul", outq[:, :], bD["VP"][:, :], SCT[:, :],
               incs=(("dve", 1),))
        ctx.wait("sync", "dve")
        ctx.dma("sync", "dqs", ov[:, out_chunks[par], osl], outq[:, :])
    state["dve"][k] = ctx.val("dve")
    state["st"][k] = ctx.val("dqs")


def emit_dequant_strip(ctx, b, u8v, scv, ov, s, nseg, out_chunks, tiled=False):
    for g in range(nseg):
        emit_dequant_seg(ctx, b, u8v, scv, ov, g, out_chunks, tiled=tiled)


def emit_ffn1(ctx, b, pb, ones_b, ones_c, ones_f, x_in="x3T", x_out="yT",
              gate_dequant=True):
    x3T, yT = b[x_in], b[x_out]
    x3v = x3T.rearrange("(c p) t -> p c t", p=128)
    yv = yT.rearrange("(c p) t -> p c t", p=128)
    wgu, wdn, hscr = b["wgu_scr"], b["wdn_scr"], b["h_scr"]

    if gate_dequant:
        gupv = b["gupT"].rearrange("(s p) o -> p s o", p=128)
        gscv = b["gup_sr"].rearrange("(s p) o -> p s o", p=128)
        wguv = wgu.rearrange("(c p) o -> p c o", p=128)
        for s in range(8):
            emit_dequant_strip(ctx, b, gupv[:, s, :], gscv[:, s, :], wguv, s, 11,
                               (2 * s, 2 * s + 1))

    ctx.wait("sync", "st")
    ctx.wait("gpsimd", "pe")
    ctx.wait("gpsimd", "act")
    ctx.wait("gpsimd", "dve")
    ctx.wait("gpsimd", "st")
    xpv = x3T.rearrange("(s j w) t -> j s w t", j=128, w=2)
    XB = b["XB"]
    for i in range(4):
        for w in range(2):
            ctx.dma("gpsimd", "gb",
                    XB[:, :].rearrange("p (s w t) -> p s w t", s=8, w=2)[:, 2 * i:2 * i + 2, w, :],
                    xpv[:, 2 * i:2 * i + 2, w, :])
    gv = wgu.rearrange("(c p) o -> p c o", p=128)
    dnv = b["dnT"].rearrange("(s p) o -> p s o", p=128)
    dscv = b["dn_sr"].rearrange("(s p) o -> p s o", p=128)
    wdnv = b["wdn_t"].rearrange("p (hc c j) -> p c hc j", hc=16, c=44)
    pe_at_chunk = {}
    dn_strip = 0
    ebank_h = Hist(3)
    ehh_h = Hist(4)
    est_h = Hist(4)
    for ic in range(44):
        slot = ic % 2
        wsem = "wa" if slot == 0 else "wb"
        if ic >= 2:
            ctx.wait_val("gpsimd", "pe", pe_at_chunk[ic - 2])
        ctx.dma("gpsimd", wsem,
                b["GW"][slot][:, :].rearrange("p (c o) -> p c o", c=DC),
                gv[:, :, 128 * ic:128 * (ic + 1)])
        ctx.dma("gpsimd", wsem,
                b["UW"][slot][:, :].rearrange("p (c o) -> p c o", c=DC),
                gv[:, :, 128 * (ic + 44):128 * (ic + 45)])
        if ic % 2 == 0 and dn_strip < 22:
            emit_dequant_strip(ctx, b, dnv[:, dn_strip, :], dscv[:, dn_strip, :],
                               wdnv, dn_strip, 2, (2 * dn_strip, 2 * dn_strip + 1),
                               tiled=True)
            dn_strip += 1
        ctx.wait("tensor", wsem)
        if ic == 0:
            ctx.wait("tensor", "gb")
        for tb in range(NTB):
            bg = pb[(0, 2, 4)[tb % 3]]
            bu = pb[(1, 3, 5)[tb % 3]]
            ctx.wait_val("tensor", "act", ebank_h.guard())
            for kc in range(DC):
                ctx.op("tensor", "matmul", bg[:, :],
                       b["GW"][slot][:, 128 * kc:128 * (kc + 1)],
                       XB[:, T * kc + TB * tb: T * kc + TB * (tb + 1)],
                       start=(kc == 0), stop=(kc == DC - 1))
            for kc in range(DC):
                ctx.op("tensor", "matmul", bu[:, :],
                       b["UW"][slot][:, 128 * kc:128 * (kc + 1)],
                       XB[:, T * kc + TB * tb: T * kc + TB * (tb + 1)],
                       start=(kc == 0), stop=(kc == DC - 1),
                       incs=((("pe", 1),) if kc == DC - 1 else ()))
            hg = b["HG"][tb % 4]
            hu = b["HU"][tb % 4]
            ctx.wait("scalar", "pe")
            ctx.wait_val("scalar", "dve", ehh_h.guard())
            ctx.op("scalar", "activation", hg[:, :], bg[:, :], AF.Silu,
                   incs=(("act", 1),))
            ctx.op("scalar", "activation", hu[:, :], bu[:, :], AF.Copy,
                   incs=(("act", 1),))
            ebank_h.push(ctx.val("act"))
            hh = b["HH"][tb % 4]
            ctx.wait("vector", "act")
            ctx.wait_val("vector", "st", est_h.guard())
            ctx.op("vector", "tensor_mul", hh[:, :], hg[:, :], hu[:, :],
                   incs=(("dve", 1),))
            ehh_h.push(ctx.val("dve"))
            ctx.wait("sync", "dve")
            ctx.dma("sync", "st",
                    hscr.rearrange("(c p) t -> p c t", p=128)[:, ic, TB * tb:TB * (tb + 1)],
                    hh[:, :])
            est_h.push(ctx.val("st"))
        pe_at_chunk[ic] = ctx.val("pe")

    dv = b["wdn_t"].rearrange("p (hc c j) -> p hc c j", hc=16, c=44)
    hpv = hscr.rearrange("(s j w) t -> j s w t", j=128, w=2)
    inv_d = 1.0 / D
    ctx.wait("sync", "st")
    pe_at_tb = {}
    pe_at_dw = {}
    dve_at_tb = {}
    xa_free = {}
    fbank_h = Hist(4)
    ctx.wait("gpsimd", "dve")
    ctx.wait("gpsimd", "st")
    ctx.wait("gpsimd", "act")
    for tb in range(NTB):
        hts = b["HT"][tb % 2]
        if tb >= 2:
            ctx.wait_val("sync", "pe", pe_at_tb[tb - 2])
        if tb >= 1:
            ctx.wait_val("sync", "dve", xa_free[tb - 1])
        for w in range(2):
            ctx.dma("sync", "xa",
                    hts[:, :].rearrange("p (s w t) -> p s w t", s=22, w=2)[:, :, w, :],
                    hpv[:, :, w, TB * tb:TB * (tb + 1)])
        ctx.dma("sync", "xb",
                b["XA"][:, :].rearrange("p (c t) -> p c t", c=DC),
                x3v[:, :, TB * tb:TB * (tb + 1)])
        for dc in range(DC):
            slot = dc % 2
            wsem = "wa" if slot == 0 else "wb"
            key = (tb, dc)
            prev = (tb, dc - 2) if dc >= 2 else ((tb - 1, DC - 2 + dc) if tb >= 1 else None)
            if prev is not None:
                ctx.wait_val("gpsimd", "pe", pe_at_dw[prev])
            else:
                ctx.wait_val("gpsimd", "pe", pe_at_chunk[43])
            ctx.dma("gpsimd", wsem,
                    b["DW"][slot][:, :].rearrange("p (c o) -> p c o", c=44),
                    dv[:, dc, :, :])
            bankd = pb[(4, 5, 0, 1)[dc % 4]]
            ctx.wait("tensor", wsem)
            if dc == 0:
                ctx.wait("tensor", "xa")
            ctx.wait_val("tensor", "act", fbank_h.guard())
            for kc in range(44):
                ctx.op("tensor", "matmul", bankd[:, :],
                       b["DW"][slot][:, 128 * kc:128 * (kc + 1)],
                       hts[:, TB * kc:TB * (kc + 1)],
                       start=(kc == 0), stop=(kc == 43),
                       incs=((("pe", 1),) if kc == 43 else ()))
            pe_at_dw[key] = ctx.val("pe")
            ctx.wait("scalar", "pe")
            ctx.wait("scalar", "dve")
            ctx.op("scalar", "activation", b["Y"][:, TB * dc:TB * (dc + 1)], bankd[:, :],
                   AF.Copy, incs=(("act", 1),))
            fbank_h.push(ctx.val("act"))
            ctx.wait("vector", "act")
            if dc == 0:
                ctx.wait("vector", "st")
            ctx.op("vector", "tensor_mul", b["YSQ"][:, TB * dc:TB * (dc + 1)],
                   b["Y"][:, TB * dc:TB * (dc + 1)], b["Y"][:, TB * dc:TB * (dc + 1)],
                   incs=(("dve", 1),))
        pe_at_tb[tb] = ctx.val("pe")
        srm = pb[6]
        ctx.wait("tensor", "act")
        ctx.wait("tensor", "dve")
        for dc in range(DC):
            ctx.op("tensor", "matmul", srm[0:1, 0:TB], ones_b[:, 0:1],
                   b["YSQ"][:, TB * dc:TB * (dc + 1)], start=(dc == 0), stop=(dc == DC - 1),
                   incs=((("pe", 1),) if dc == DC - 1 else ()))
        ST = b["ST"]
        erow = ST[0:1, 0:TB]
        rrow = ST[0:1, TB:2 * TB]
        ctx.wait("scalar", "pe")
        ctx.op("scalar", "activation", erow, srm[0:1, 0:TB], AF.Copy, scale=inv_d,
               incs=(("act", 1),))
        ctx.wait("vector", "act")
        ctx.op("vector", "tensor_scalar", rrow, erow, 1e-6, None, op0=ALU.add,
               incs=(("dve", 1),))
        ctx.wait("scalar", "dve")
        ctx.op("scalar", "activation", rrow, rrow, AF.Sqrt, incs=(("act", 1),))
        ctx.wait("vector", "act")
        ctx.op("vector", "reciprocal", rrow, rrow, incs=(("dve", 1),))
        ctx.wait("tensor", "dve")
        ctx.op("tensor", "matmul", srm[:, 0:TB], ones_f[0:1, :], rrow,
               start=True, stop=True, incs=(("pe", 1),))
        ctx.wait("vector", "pe")
        for dc in range(DC):
            ctx.op("vector", "tensor_mul", b["Y"][:, TB * dc:TB * (dc + 1)],
                   b["Y"][:, TB * dc:TB * (dc + 1)], srm[:, 0:TB],
                   incs=(("dve", 1),))
            emit_postln_prep(ctx, b, dc)
        xa_free[tb] = ctx.val("dve")
        dve_at_tb[tb] = emit_postln(ctx, b, pb, ones_b, ones_c, ones_f,
                                    yv[:, :, TB * tb:TB * (tb + 1)], banks=(5, 6))


def build_ffn1_program():
    nc = bass.Bass("TRN2", target_bir_lowering=False)
    io = {}
    io["x3T"] = nc.dram_tensor("x3T", [D, T], F32, kind="ExternalInput").ap()
    io["gupT"] = nc.dram_tensor("gupT", [1024, 11264], U8, kind="ExternalInput").ap()
    io["gup_sr"] = nc.dram_tensor("gup_sr", [1024, 11264], F16, kind="ExternalInput").ap()
    io["dnT"] = nc.dram_tensor("dnT", [2816, 2048], U8, kind="ExternalInput").ap()
    io["dn_sr"] = nc.dram_tensor("dn_sr", [2816, 2048], F16, kind="ExternalInput").ap()
    io["yT"] = nc.dram_tensor("yT", [D, T], F32, kind="ExternalOutput").ap()
    io["wgu_scr"] = nc.dram_tensor("wgu_scr", [D, 11264], BF16, kind="Internal").ap()
    io["wdn_scr"] = nc.dram_tensor("wdn_scr", [NI1, D], BF16, kind="Internal").ap()
    io["h_scr"] = nc.dram_tensor("h_scr", [NI1, T], BF16, kind="Internal").ap()

    sb = SbufAlloc(nc)
    ones_b = sb.tile("ones_b", [128, 1], BF16)
    ones_c = sb.tile("ones_c", [128, 1], F32)
    ones_f = sb.tile("ones_f", [1, 128], F32)
    bufs = plan_ffn1(sb, io)
    pb = [nc.alloc_psum_tensor(f"pb{i}", [128, 512], F32) for i in range(8)]
    sems = {k: nc.alloc_semaphore(name=f"s_{k}") for k in SEM_NAMES}

    def emit(ctx):
        ctx.op("vector", "memset", ones_b[:, :], 1.0)
        ctx.op("vector", "memset", ones_c[:, :], 1.0)
        ctx.op("vector", "memset", ones_f[:, :], 1.0, incs=(("dve", 1),))
        emit_ffn1(ctx, bufs, pb, ones_b, ones_c, ones_f)
        ctx.wait("sync", "st")

    _run_block(nc, emit, sems)
    return nc


def host_prep_ffn1(inputs, x3_full):
    gupT = np.ascontiguousarray(
        inputs["L1_ffn_gate_up_packed"].view(np.uint8).reshape(11264, 1024).T)
    gup_sr = np.ascontiguousarray(
        np.repeat(inputs["L1_ffn_gate_up_scales"], 64, axis=0))
    dnT = np.ascontiguousarray(
        inputs["L1_ffn_down_packed"].view(np.uint8).reshape(2048, 2816).T)
    dn_sr = np.ascontiguousarray(
        np.repeat(inputs["L1_ffn_down_scales"], 64, axis=0))
    maps = []
    for c in range(NCORES):
        maps.append({
            "x3T": np.ascontiguousarray(x3_full[c].T),
            "gupT": gupT, "gup_sr": gup_sr, "dnT": dnT, "dn_sr": dn_sr,
        })
    return maps
